# version 01
# baseline (speedup 1.0000x reference)
import math
import numpy as np
import ml_dtypes
import concourse.bass as bass
import concourse.mybir as mybir
from concourse.bass_utils import run_bass_kernel_spmd
from contextlib import ExitStack

F32 = mybir.dt.float32
BF16 = mybir.dt.bfloat16
AF = mybir.ActivationFunctionType
ALU = mybir.AluOpType
AX = mybir.AxisListType

D = 2048
DC = 16
NH = 16
TC = 1024
NTC = 8
SEQ = 4096
WIN = 2048
TAUI = 1.0 / math.sqrt(128.0)
NEG = -1e30

E0, F0, FC0 = 8, 1408, 11
E1, F1, FC1 = 16, 1024, 8
FSH = 2048

KSPAN = 3072

ENGNAMES = ("pe", "dve", "act", "pool", "sp")
FULL_SERIAL = True


class Prog:
    def __init__(self):
        self.nc = bass.Bass("TRN2", target_bir_lowering=False, debug=False)
        self.steps = []
        self.ctx = ExitStack()

    def s(self, eng, fn, dma=False):
        key = eng + ("_dma" if dma else "")
        inc = 16 if dma else 1
        self.steps.append((eng, fn, inc, key))

    def sb(self, name, shape, dtype):
        return self.ctx.enter_context(self.nc.sbuf_tensor(name, shape, dtype))

    def ps(self, name, shape, dtype):
        return self.ctx.enter_context(self.nc.psum_tensor(name, shape, dtype))

    def finish(self):
        nc = self.nc
        with ExitStack() as es:
            sems = {}
            for e in ENGNAMES:
                sems[e] = es.enter_context(nc.semaphore("sem_" + e))
                sems[e + "_dma"] = es.enter_context(nc.semaphore("semd_" + e))
            block = es.enter_context(nc.Block())

            steps = self.steps
            cums = []
            run = {}
            for (eng, fn, inc, key) in steps:
                cums.append(dict(run))
                run[key] = run.get(key, 0) + inc
            total = run

            def emit(engine_obj, name):
                for i, (eng, fn, inc, key) in enumerate(steps):
                    if eng != name:
                        continue
                    if i > 0:
                        _, _, _, pkey = steps[i - 1]
                        same_compute = (pkey == key and not key.endswith("_dma")
                                        and not FULL_SERIAL)
                        if not same_compute:
                            engine_obj.wait_ge(sems[pkey], cums[i].get(pkey, 0))
                    fn(engine_obj).then_inc(sems[key], inc)
                if name == "sp":
                    for k, v in total.items():
                        engine_obj.wait_ge(sems[k], v)

            @block.gpsimd
            def _(g):
                emit(g, "pool")

            @block.vector
            def _(v):
                emit(v, "dve")

            @block.scalar
            def _(a):
                emit(a, "act")

            @block.tensor
            def _(t):
                emit(t, "pe")

            @block.sync
            def _(s):
                emit(s, "sp")

        self.ctx.close()
        return nc



def setup_consts(P, need_f32_ident=True, padflag_src=None):
    t = {}
    t["ident_bf"] = P.sb("ident_bf", [128, 128], BF16)
    t["ones_col_bf"] = P.sb("ones_col_bf", [128, 1], BF16)
    t["ones_row_f"] = P.sb("ones_row_f", [1, 128], F32)
    t["ones_row_bf"] = P.sb("ones_row_bf", [1, 128], BF16)
    if need_f32_ident:
        t["ident_f"] = P.sb("ident_f", [128, 128], F32)

    P.s("pool", lambda g: g.memset(t["ident_bf"][:, :], 0.0))
    P.s("pool", lambda g: g.affine_select(
        out=t["ident_bf"][:, :], in_=t["ident_bf"][:, :],
        compare_op=ALU.not_equal, fill=1.0, base=0,
        pattern=[[-1, 128]], channel_multiplier=1))
    if need_f32_ident:
        P.s("pool", lambda g: g.memset(t["ident_f"][:, :], 0.0))
        P.s("pool", lambda g: g.affine_select(
            out=t["ident_f"][:, :], in_=t["ident_f"][:, :],
            compare_op=ALU.not_equal, fill=1.0, base=0,
            pattern=[[-1, 128]], channel_multiplier=1))
    P.s("pool", lambda g: g.memset(t["ones_col_bf"][:, :], 1.0))
    P.s("pool", lambda g: g.memset(t["ones_row_f"][:, :], 1.0))
    P.s("pool", lambda g: g.memset(t["ones_row_bf"][:, :], 1.0))

    if padflag_src is not None:
        t["padflag"] = P.sb("padflag", [1, KSPAN], BF16)
        with ExitStack() as pes:
            wlo = pes.enter_context(P.nc.sbuf_tensor("wlo", [1, 1], F32))
            pf32 = pes.enter_context(
                P.nc.sbuf_tensor("pf32", [1, KSPAN], F32))
            P.s("sp", lambda s: s.dma_start(out=wlo[:, :], in_=padflag_src),
                dma=True)
            P.s("dve", lambda v: v.tensor_scalar(wlo[:, :], wlo[:, :], 1024.0,
                                                 -2048.0, op0=ALU.mult,
                                                 op1=ALU.add))
            P.s("pool", lambda g: g.iota(pf32[:, :], [[1, KSPAN]], base=0,
                                         channel_multiplier=0,
                                         allow_small_or_imprecise_dtypes=True))
            P.s("dve", lambda v: v.tensor_scalar(pf32[:, :], pf32[:, :],
                                                 wlo[:, :], None, op0=ALU.add))
            P.s("dve", lambda v: v.tensor_scalar(pf32[:, :], pf32[:, :], 0.0,
                                                 None, op0=ALU.is_lt))
            P.s("dve", lambda v: v.tensor_scalar(t["padflag"][:, :],
                                                 pf32[:, :], float(NEG), None,
                                                 op0=ALU.mult))
    return t


def setup_sels(P):
    iob = P.sb("sels_iob", [64, DC * 128], F32)
    pidx = P.sb("sels_pidx", [64, 1], F32)
    sels = P.sb("sels", [64, DC * 128], BF16)
    P.s("pool", lambda g: g.iota(iob[:, :], [[1, 64], [0, 32]], base=0,
                                 channel_multiplier=0,
                                 allow_small_or_imprecise_dtypes=True))
    P.s("pool", lambda g: g.iota(pidx[:, :], [[0, 1]], base=0,
                                 channel_multiplier=1,
                                 allow_small_or_imprecise_dtypes=True))
    P.s("dve", lambda v: v.tensor_scalar(sels[:, :], iob[:, :], pidx[:, :],
                                         None, op0=ALU.is_equal))
    return sels


def emit_load_stream_fm(P, C, xin_ap, xfm, dtype):
    nc = P.nc
    with ExitStack() as es:
        xtm = es.enter_context(nc.sbuf_tensor("xtm_tmp", [128, NTC, D], F32))
        P.s("sp", lambda s: s.dma_start(
            out=xtm[:, :, :], in_=xin_ap.rearrange("(n p) d -> p n d", p=128)),
            dma=True)
        if dtype == BF16:
            xtmb = es.enter_context(
                nc.sbuf_tensor("xtmb_tmp", [128, NTC, D], BF16))
            P.s("dve", lambda v: v.tensor_copy(xtmb[:, :, :], xtm[:, :, :]))
            pt = C["PT"]
            for c in range(DC):
                for n in range(NTC):
                    P.s("pe", lambda t, c=c, n=n: t.transpose(
                        pt[:, n * 128:(n + 1) * 128],
                        xtmb[:, n, c * 128:(c + 1) * 128], C["ident_bf"][:, :]))
                P.s("dve", lambda v, c=c: v.tensor_copy(
                    xfm[:, c, :], pt[:, 0:1024]))
        else:
            psS = C["PS_S"]
            for c in range(DC):
                for n in range(NTC):
                    P.s("pe", lambda t, c=c, n=n: t.transpose(
                        psS[:, (n % 4) * 128:(n % 4) * 128 + 128],
                        xtm[:, n, c * 128:(c + 1) * 128], C["ident_f"][:, :]))
                    if n % 4 == 3:
                        P.s("dve", lambda v, c=c, n=n: v.tensor_copy(
                            xfm[:, c, (n - 3) * 128:(n + 1) * 128],
                            psS[:, 0:512]))


def emit_dequant(P, C, sels, codes_bf, scl_bf, dk_idx_pairs):
    PO = C["P_O"]
    for base, dk in dk_idx_pairs:
        for n4 in range(4):
            P.s("pe", lambda t, dk=dk, n4=n4: t.matmul(
                PO[:, 0:512], sels[:, dk * 128:(dk + 1) * 128],
                scl_bf[:, n4 * 512:(n4 + 1) * 512], start=True, stop=True))
            P.s("dve", lambda v, base=base, n4=n4: v.tensor_tensor(
                codes_bf[:, base + n4 * 512:base + (n4 + 1) * 512],
                codes_bf[:, base + n4 * 512:base + (n4 + 1) * 512],
                PO[:, 0:512], ALU.mult))


def emit_ln(P, C, zb_slices, out_writes, tw):
    s = P.s
    zc = C["ln_zc"]
    mu = C["ln_mu"]
    rstd = C["ln_rstd"]
    var = C["ln_var"]
    PS_S, P_O, P_AUX = C["PS_S"], C["P_O"], C["P_AUX"]
    for c in range(DC):
        s("act", lambda a, c=c: a.activation(zc[:, 0:tw], zb_slices[c],
                                             AF.Identity, bias=0.0, scale=1.0))
        s("act", lambda a, c=c: a.activation(zc[:, tw:2 * tw], zb_slices[c],
                                             AF.Square, bias=0.0, scale=1.0))
        s("pe", lambda t, c=c: t.matmul(PS_S[0:1, 0:tw],
                                        C["ones_col_bf"][:, :],
                                        zc[:, 0:tw], start=(c == 0),
                                        stop=(c == DC - 1)))
        s("pe", lambda t, c=c: t.matmul(PS_S[0:1, 512:512 + tw],
                                        C["ones_col_bf"][:, :],
                                        zc[:, tw:2 * tw], start=(c == 0),
                                        stop=(c == DC - 1)))
    s("dve", lambda v: v.tensor_scalar(mu[:, 0:tw], PS_S[0:1, 0:tw],
                                       1.0 / D, None, op0=ALU.mult))
    s("dve", lambda v: v.tensor_scalar(var[:, 0:tw], PS_S[0:1, 512:512 + tw],
                                       1.0 / D, None, op0=ALU.mult))
    s("act", lambda a: a.activation(rstd[:, 0:tw], mu[:, 0:tw], AF.Square,
                                    bias=0.0, scale=1.0))
    s("dve", lambda v: v.tensor_tensor(var[:, 0:tw], var[:, 0:tw],
                                       rstd[:, 0:tw], ALU.subtract))
    s("dve", lambda v: v.tensor_scalar(var[:, 0:tw], var[:, 0:tw], 1e-5, None,
                                       op0=ALU.add))
    s("act", lambda a: a.activation(rstd[:, 0:tw], var[:, 0:tw], AF.Sqrt,
                                    bias=0.0, scale=1.0))
    s("dve", lambda v: v.reciprocal(rstd[:, 0:tw], rstd[:, 0:tw]))
    s("pe", lambda t: t.matmul(P_AUX[:, 0:tw], C["ones_row_f"][:, :],
                               mu[:, 0:tw], start=True, stop=True))
    s("pe", lambda t: t.matmul(P_O[:, 0:tw], C["ones_row_f"][:, :],
                               rstd[:, 0:tw], start=True, stop=True))
    tmp = C["ln_tmp"]
    for c in range(DC):
        s("dve", lambda v, c=c: v.tensor_tensor(tmp[:, 0:tw], zb_slices[c],
                                                P_AUX[:, 0:tw], ALU.subtract))
        out_writes(c, tmp[:, 0:tw], P_O[:, 0:tw])


def emit_store_tm(P, C, fm_slices, out_ap, row0, tw, tag):
    nc = P.nc
    ntile = tw // 128
    with ExitStack() as es:
        otm = es.enter_context(nc.sbuf_tensor("otm_" + tag, [128, D], F32))
        for n in range(ntile):
            for c in range(DC):
                P.s("pe", lambda t, c=c, n=n: t.transpose(
                    C["PS_S"][:, (c % 4) * 128:(c % 4) * 128 + 128],
                    fm_slices[c][:, n * 128:(n + 1) * 128], C["ident_f"][:, :]))
                P.s("dve", lambda v, c=c, n=n: v.tensor_copy(
                    otm[:, c * 128:(c + 1) * 128],
                    C["PS_S"][:, (c % 4) * 128:(c % 4) * 128 + 128]))
            P.s("sp", lambda sy, n=n: sy.dma_start(
                out=out_ap[row0 + n * 128:row0 + (n + 1) * 128, :],
                in_=otm[:, :]), dma=True)



def build_A(layer):
    P = Prog()
    nc = P.nc
    xin = nc.dram_tensor("xin", [TC, D], F32, kind="ExternalInput").ap()
    if layer == 0:
        wlist = [nc.dram_tensor(n, [D, D], F32, kind="ExternalInput").ap()
                 for n in ("Wq", "Wk", "Wv")]
        slist = blist = [None] * 3
    else:
        wlist, slist, blist = [], [], []
        for nm in ("q", "k", "v"):
            wlist.append(nc.dram_tensor("c" + nm, [D, D], F32,
                                        kind="ExternalInput").ap())
            slist.append(nc.dram_tensor("s" + nm, [64, D], F32,
                                        kind="ExternalInput").ap())
            blist.append(nc.dram_tensor("b" + nm, [128, DC], F32,
                                        kind="ExternalInput").ap())
    qo = nc.dram_tensor("qo", [D, TC], BF16, kind="ExternalOutput").ap()
    ko = nc.dram_tensor("ko", [D, TC], BF16, kind="ExternalOutput").ap()
    vo = nc.dram_tensor("vo", [TC, D], BF16, kind="ExternalOutput").ap()
    outs = [qo, ko, vo]

    C = {}
    C["PS_S"] = P.ps("PS_S", [128, 2560], F32)
    C["P_O"] = P.ps("P_O", [128, 512], F32)
    C["P_AUX"] = P.ps("P_AUX", [128, 512], F32)
    C["PT"] = P.ps("PT", [128, 1024], BF16)
    C.update(setup_consts(P, need_f32_ident=False))
    if layer == 1:
        sels = setup_sels(P)

    xfm = P.sb("xfm", [128, DC, TC], BF16)
    emit_load_stream_fm(P, C, xin, xfm, BF16)

    wb = P.sb("wb", [128, DC * D], BF16)
    proj = P.sb("proj", [128, DC, TC], BF16)
    vtmn = P.sb("vtmn", [128, D], BF16)
    if layer == 1:
        sclb = P.sb("sclb", [64, D], BF16)
        biasb = P.sb("biasb", [128, DC], F32)

    for pi in range(3):
        wap = wlist[pi]
        P.s("pool", lambda g, wap=wap: g.dma_start(
            out=wb[:, :].rearrange("p (c f) -> p c f", f=D),
            in_=wap.rearrange("(c p) f -> p c f", p=128)), dma=True)
        if layer == 1:
            sap, bap = slist[pi], blist[pi]
            P.s("pool", lambda g, sap=sap: g.dma_start(
                out=sclb[:, :], in_=sap), dma=True)
            P.s("sp", lambda s, bap=bap: s.dma_start(
                out=biasb[:, :], in_=bap), dma=True)
            emit_dequant(P, C, sels, wb, sclb,
                         [(dk * D, dk) for dk in range(DC)])
        for m in range(DC):
            for st in range(2):
                for dk in range(DC):
                    P.s("pe", lambda t, m=m, st=st, dk=dk: t.matmul(
                        C["PS_S"][:, 0:512],
                        wb[:, dk * D + m * 128:dk * D + (m + 1) * 128],
                        xfm[:, dk, st * 512:(st + 1) * 512],
                        start=(dk == 0), stop=(dk == DC - 1)))
                if layer == 1:
                    P.s("act", lambda a, m=m, st=st: a.activation(
                        proj[:, m, st * 512:(st + 1) * 512],
                        C["PS_S"][:, 0:512], AF.Identity,
                        bias=biasb[:, m:m + 1], scale=1.0))
                else:
                    P.s("act", lambda a, m=m, st=st: a.activation(
                        proj[:, m, st * 512:(st + 1) * 512],
                        C["PS_S"][:, 0:512], AF.Identity, bias=0.0, scale=1.0))
        if pi < 2:
            P.s("sp", lambda s, pi=pi: s.dma_start(
                out=outs[pi].rearrange("(m p) t -> p m t", p=128),
                in_=proj[:, :, :]), dma=True)
        else:
            for n in range(NTC):
                for c in range(DC):
                    P.s("pe", lambda t, n=n, c=c: t.transpose(
                        C["PT"][:, (c % 8) * 128:(c % 8) * 128 + 128],
                        proj[:, c, n * 128:(n + 1) * 128], C["ident_bf"][:, :]))
                    if c % 8 == 7:
                        P.s("dve", lambda v, n=n, c=c: v.tensor_copy(
                            vtmn[:, (c - 7) * 128:(c + 1) * 128], C["PT"][:, :]))
                P.s("sp", lambda s, n=n: s.dma_start(
                    out=vo[n * 128:(n + 1) * 128, :], in_=vtmn[:, :]), dma=True)

    return P.finish()



def build_B(layer, debug_stop=None):
    P = Prog()
    nc = P.nc
    s = P.s
    xin = nc.dram_tensor("xin", [TC, D], F32, kind="ExternalInput").ap()
    qin = nc.dram_tensor("qin", [D, TC], BF16, kind="ExternalInput").ap()
    kspan = KSPAN if layer == 0 else SEQ
    kin = nc.dram_tensor("kin", [D, kspan], BF16, kind="ExternalInput").ap()
    vin = nc.dram_tensor("vin", [kspan, D], BF16, kind="ExternalInput").ap()
    if layer == 0:
        coreq4 = nc.dram_tensor("coreq4", [1, 1], F32,
                                kind="ExternalInput").ap()
        wo_ap = nc.dram_tensor("Wo", [D, D], F32, kind="ExternalInput").ap()
        gate_w = nc.dram_tensor("gate_w", [D, E0], F32,
                                kind="ExternalInput").ap()
        w_gate = nc.dram_tensor("w_gate", [E0, D, F0], F32,
                                kind="ExternalInput").ap()
        w_up = nc.dram_tensor("w_up", [E0, D, F0], F32,
                              kind="ExternalInput").ap()
        w_down = nc.dram_tensor("w_down", [E0, F0, D], F32,
                                kind="ExternalInput").ap()
    else:
        wo_ap = nc.dram_tensor("oc", [D, D], F32, kind="ExternalInput").ap()
        osc = nc.dram_tensor("osc", [64, D], F32, kind="ExternalInput").ap()
        bo = nc.dram_tensor("bo", [128, DC], F32, kind="ExternalInput").ap()
        gwr = nc.dram_tensor("gwr", [D, E1], F32, kind="ExternalInput").ap()
        bcorr = nc.dram_tensor("bcorr", [1, E1], F32,
                               kind="ExternalInput").ap()
        gwT = nc.dram_tensor("gwT", [E1, D, F1], F32,
                             kind="ExternalInput").ap()
        uwT = nc.dram_tensor("uwT", [E1, D, F1], F32,
                             kind="ExternalInput").ap()
        dwT = nc.dram_tensor("dwT", [E1, F1, D], F32,
                             kind="ExternalInput").ap()
        shgT = nc.dram_tensor("shgT", [D, FSH], F32, kind="ExternalInput").ap()
        shuT = nc.dram_tensor("shuT", [D, FSH], F32, kind="ExternalInput").ap()
        shdT = nc.dram_tensor("shdT", [FSH, D], F32, kind="ExternalInput").ap()
    xout = nc.dram_tensor("xout", [TC, D], F32, kind="ExternalOutput").ap()
    dbg = None
    if debug_stop == "ctx":
        dbg = nc.dram_tensor("dbg", [D, TC], BF16, kind="ExternalOutput").ap()
    elif debug_stop in ("z", "h", "y0"):
        dbg = nc.dram_tensor("dbg", [TC, D], F32, kind="ExternalOutput").ap()
    elif debug_stop == "cw":
        nexp0 = E0 if layer == 0 else E1
        ncpt0 = 2 if layer == 0 else 4
        dbg = nc.dram_tensor("dbg", [128, ncpt0, nexp0], F32,
                             kind="ExternalOutput").ap()

    C = {}
    C["PS_S"] = P.ps("PS_S", [128, 2560], F32)
    C["P_O"] = P.ps("P_O", [128, 512], F32)
    C["P_AUX"] = P.ps("P_AUX", [128, 512], F32)
    C["PT"] = P.ps("PT", [128, 1024], BF16)
    C.update(setup_consts(P, need_f32_ident=True,
                          padflag_src=(coreq4 if layer == 0 else None)))
    if layer == 1:
        sels = setup_sels(P)

    C["ln_zc"] = P.sb("ln_zc", [128, 1024], BF16)
    C["ln_mu"] = P.sb("ln_mu", [1, 512], F32)
    C["ln_var"] = P.sb("ln_var", [1, 512], F32)
    C["ln_rstd"] = P.sb("ln_rstd", [1, 512], F32)
    C["ln_tmp"] = P.sb("ln_tmp", [128, 512], F32)

    lacc = P.sb("lacc", [128, 2], F32)
    msml = P.sb("msml", [128, 8], F32)
    mb = P.sb("mb", [128, 1], F32)
    rrow = P.sb("rrow", [1, 256], F32)
    rb_sb = P.sb("rb_sb", [128, 256], F32)
    t128a = P.sb("t128a", [128, 128], F32)
    t128b = P.sb("t128b", [128, 128], F32)

    if layer == 0:
        maskd = P.sb("maskd", [128, 4 * 512], BF16)
        maske = P.sb("maske", [128, 4 * 512], BF16)
        for ph in range(4):
            phi = 128 * ph
            psi = 128 * ph + 1
            s("pool", lambda g, ph=ph: g.memset(
                maskd[:, ph * 512:(ph + 1) * 512], 0.0))
            s("pool", lambda g, ph=ph, phi=phi: g.affine_select(
                out=maskd[:, ph * 512:(ph + 1) * 512],
                in_=maskd[:, ph * 512:(ph + 1) * 512],
                compare_op=ALU.is_ge, fill=NEG, base=phi,
                pattern=[[-1, 512]], channel_multiplier=1))
            s("pool", lambda g, ph=ph: g.memset(
                maske[:, ph * 512:(ph + 1) * 512], 0.0))
            s("pool", lambda g, ph=ph, psi=psi: g.affine_select(
                out=maske[:, ph * 512:(ph + 1) * 512],
                in_=maske[:, ph * 512:(ph + 1) * 512],
                compare_op=ALU.is_ge, fill=NEG, base=-psi,
                pattern=[[1, 512]], channel_multiplier=-1))

    A = P.sb("Abuf", [128, DC, TC], F32)
    emit_load_stream_fm(P, C, xin, A, F32)

    ctx_es = ExitStack()
    ctx_sb = ctx_es.enter_context(nc.sbuf_tensor("ctx_sb", [128, NH, TC],
                                                 BF16))

    with ExitStack() as es:
        q_sb = es.enter_context(nc.sbuf_tensor("q_sb", [128, NH, TC], BF16))
        k_h = es.enter_context(nc.sbuf_tensor("k_h", [128, kspan], BF16))
        nkc_all = kspan // 128
        v_h = es.enter_context(nc.sbuf_tensor("v_h", [128, nkc_all, 128],
                                              BF16))
        rw = 2560 if layer == 0 else 2048
        p_sb = es.enter_context(nc.sbuf_tensor("p_sb", [128, rw], BF16))
        pt_sb = es.enter_context(nc.sbuf_tensor("pt_sb", [128, rw], BF16))

        s("sp", lambda sy: sy.dma_start(
            out=q_sb[:, :, :], in_=qin.rearrange("(h p) t -> p h t", p=128)),
            dma=True)

        for h in range(NH):
            s("sp", lambda sy, h=h: sy.dma_start(
                out=k_h[:, :], in_=kin[h * 128:(h + 1) * 128, :]), dma=True)
            s("sp", lambda sy, h=h: sy.dma_start(
                out=v_h[:, :, :],
                in_=vin[:, h * 128:(h + 1) * 128].rearrange(
                    "(n p) d -> p n d", p=128)), dma=True)
            for i in range(NTC):
                if layer == 0:
                    rounds = [list(range(i // 4, i // 4 + 5))]
                else:
                    rounds = [[0, 1, 2, 3], [4, 5, 6, 7]]
                nr = len(rounds)
                for r, chunks in enumerate(rounds):
                    nch = len(chunks)
                    for j, ch in enumerate(chunks):
                        s("pe", lambda t, h=h, i=i, j=j, ch=ch: t.matmul(
                            C["PS_S"][:, j * 512:(j + 1) * 512],
                            q_sb[:, h, i * 128:(i + 1) * 128],
                            k_h[:, ch * 512:(ch + 1) * 512],
                            start=True, stop=(layer != 0)))
                        if layer == 0:
                            s("pe", lambda t, j=j, ch=ch: t.matmul(
                                C["PS_S"][:, j * 512:(j + 1) * 512],
                                C["ones_row_bf"][:, :],
                                C["padflag"][:, ch * 512:(ch + 1) * 512],
                                start=False, stop=True))
                    if layer == 0:
                        ph = i % 4
                        s("dve", lambda v, ph=ph: v.tensor_tensor(
                            C["PS_S"][:, 0:512], C["PS_S"][:, 0:512],
                            maske[:, ph * 512:(ph + 1) * 512], ALU.add))
                        s("dve", lambda v, ph=ph: v.tensor_tensor(
                            C["PS_S"][:, 4 * 512:5 * 512],
                            C["PS_S"][:, 4 * 512:5 * 512],
                            maskd[:, ph * 512:(ph + 1) * 512], ALU.add))
                    s("dve", lambda v, nch=nch, r=r: v.tensor_reduce(
                        msml[:, r:r + 1], C["PS_S"][:, 0:nch * 512], AX.X,
                        ALU.max))
                    s("dve", lambda v, r=r: v.tensor_scalar(
                        mb[:, :], msml[:, r:r + 1], -TAUI, None, op0=ALU.mult))
                    s("act", lambda a, nch=nch: a.activation(
                        p_sb[:, 0:nch * 512], C["PS_S"][:, 0:nch * 512],
                        AF.Exp, bias=mb[:, :], scale=TAUI,
                        accum_out=lacc[:, 0:1]))
                    s("dve", lambda v, r=r: v.tensor_copy(
                        msml[:, 2 + r:3 + r], lacc[:, 0:1]))
                    nk128 = nch * 4
                    for kc in range(nk128):
                        s("pe", lambda t, kc=kc: t.transpose(
                            C["PT"][:, (kc % 8) * 128:(kc % 8) * 128 + 128],
                            p_sb[:, kc * 128:(kc + 1) * 128],
                            C["ident_bf"][:, :]))
                        if kc % 8 == 7 or kc == nk128 - 1:
                            lo = (kc // 8) * 8
                            s("dve", lambda v, kc=kc, lo=lo: v.tensor_copy(
                                pt_sb[:, lo * 128:(kc + 1) * 128],
                                C["PT"][:, 0:(kc - lo + 1) * 128]))
                    ch0 = chunks[0] * 4
                    for kc in range(nk128):
                        s("pe", lambda t, kc=kc, ch0=ch0, r=r, nk128=nk128:
                          t.matmul(
                            C["P_O"][:, r * 128:(r + 1) * 128],
                            v_h[:, ch0 + kc, :],
                            pt_sb[:, kc * 128:(kc + 1) * 128],
                            start=(kc == 0), stop=(kc == nk128 - 1)))
                if nr == 1:
                    s("dve", lambda v: v.reciprocal(mb[:, :], msml[:, 2:3]))
                    s("pe", lambda t: t.transpose(
                        C["PS_S"][0:1, 0:128], mb[:, :], C["ident_f"][:, :]))
                    s("dve", lambda v: v.tensor_copy(rrow[0:1, 0:128],
                                                     C["PS_S"][0:1, 0:128]))
                    s("pe", lambda t: t.matmul(
                        C["P_AUX"][:, 0:128], C["ones_row_f"][:, :],
                        rrow[0:1, 0:128], start=True, stop=True))
                    s("dve", lambda v: v.tensor_copy(rb_sb[:, 0:128],
                                                     C["P_AUX"][:, 0:128]))
                    s("dve", lambda v, h=h, i=i: v.tensor_tensor(
                        ctx_sb[:, h, i * 128:(i + 1) * 128],
                        C["P_O"][:, 0:128], rb_sb[:, 0:128], ALU.mult))
                else:
                    s("dve", lambda v: v.tensor_tensor(
                        mb[:, :], msml[:, 0:1], msml[:, 1:2], ALU.max))
                    s("dve", lambda v: v.tensor_scalar(
                        mb[:, :], mb[:, :], -TAUI, None, op0=ALU.mult))
                    for r in range(2):
                        s("act", lambda a, r=r: a.activation(
                            msml[:, 4 + r:5 + r], msml[:, r:r + 1], AF.Exp,
                            bias=mb[:, :], scale=TAUI))
                    s("dve", lambda v: v.tensor_tensor(
                        msml[:, 6:7], msml[:, 4:5], msml[:, 2:3], ALU.mult))
                    s("dve", lambda v: v.tensor_tensor(
                        msml[:, 7:8], msml[:, 5:6], msml[:, 3:4], ALU.mult))
                    s("dve", lambda v: v.tensor_tensor(
                        msml[:, 6:7], msml[:, 6:7], msml[:, 7:8], ALU.add))
                    s("dve", lambda v: v.reciprocal(msml[:, 6:7], msml[:, 6:7]))
                    for r in range(2):
                        s("dve", lambda v, r=r: v.tensor_tensor(
                            msml[:, 4 + r:5 + r], msml[:, 4 + r:5 + r],
                            msml[:, 6:7], ALU.mult))
                    for r in range(2):
                        s("pe", lambda t, r=r: t.transpose(
                            C["PS_S"][0:1, r * 128:(r + 1) * 128],
                            msml[:, 4 + r:5 + r], C["ident_f"][:, :]))
                    s("dve", lambda v: v.tensor_copy(rrow[0:1, 0:256],
                                                     C["PS_S"][0:1, 0:256]))
                    for r in range(2):
                        s("pe", lambda t, r=r: t.matmul(
                            C["P_AUX"][:, r * 128:(r + 1) * 128],
                            C["ones_row_f"][:, :],
                            rrow[0:1, r * 128:(r + 1) * 128],
                            start=True, stop=True))
                    s("dve", lambda v: v.tensor_copy(rb_sb[:, 0:256],
                                                     C["P_AUX"][:, 0:256]))
                    s("dve", lambda v: v.tensor_tensor(
                        t128a[:, :], C["P_O"][:, 0:128], rb_sb[:, 0:128],
                        ALU.mult))
                    s("dve", lambda v: v.tensor_tensor(
                        t128b[:, :], C["P_O"][:, 128:256], rb_sb[:, 128:256],
                        ALU.mult))
                    s("dve", lambda v, h=h, i=i: v.tensor_tensor(
                        ctx_sb[:, h, i * 128:(i + 1) * 128],
                        t128a[:, :], t128b[:, :], ALU.add))

    if debug_stop == "ctx":
        s("sp", lambda sy: sy.dma_start(
            out=dbg.rearrange("(h p) t -> p h t", p=128), in_=ctx_sb[:, :, :]),
            dma=True)
        ctx_es.close()
        return P.finish()

    with ExitStack() as es:
        wbh = es.enter_context(nc.sbuf_tensor("wbh", [128, 8 * D], BF16))
        zst = es.enter_context(nc.sbuf_tensor("zst", [128, DC, 512], F32))
        if layer == 1:
            sclb = es.enter_context(nc.sbuf_tensor("sclb", [64, D], BF16))
            biasb = es.enter_context(nc.sbuf_tensor("biasb", [128, DC], F32))
            s("pool", lambda g: g.dma_start(out=sclb[:, :], in_=osc), dma=True)
            s("sp", lambda sy: sy.dma_start(out=biasb[:, :], in_=bo), dma=True)
        for st in range(2):
            for half in range(2):
                s("pool", lambda g, half=half: g.dma_start(
                    out=wbh[:, :].rearrange("p (c f) -> p c f", f=D),
                    in_=wo_ap.rearrange("(c p) f -> p c f", p=128)[
                        :, half * 8:(half + 1) * 8, :]), dma=True)
                if layer == 1:
                    emit_dequant(P, C, sels, wbh, sclb,
                                 [(dk8 * D, half * 8 + dk8)
                                  for dk8 in range(8)])
                for m in range(DC):
                    for dk in range(8):
                        s("pe", lambda t, st=st, half=half, m=m, dk=dk:
                          t.matmul(
                            C["PS_S"][:, 0:512],
                            wbh[:, dk * D + m * 128:dk * D + (m + 1) * 128],
                            ctx_sb[:, half * 8 + dk, st * 512:(st + 1) * 512],
                            start=(dk == 0), stop=(dk == 7)))
                    if half == 0:
                        s("dve", lambda v, st=st, m=m: v.tensor_tensor(
                            zst[:, m, :], C["PS_S"][:, 0:512],
                            A[:, m, st * 512:(st + 1) * 512], ALU.add))
                    else:
                        s("dve", lambda v, m=m: v.tensor_tensor(
                            zst[:, m, :], C["PS_S"][:, 0:512],
                            zst[:, m, :], ALU.add))
            if layer == 1:
                for m in range(DC):
                    s("act", lambda a, m=m: a.activation(
                        zst[:, m, :], zst[:, m, :], AF.Identity,
                        bias=biasb[:, m:m + 1], scale=1.0))
            if debug_stop == "z":
                emit_store_tm(P, C, [zst[:, c, :] for c in range(DC)],
                              dbg, st * 512, 512, f"z{st}")
                continue

            def wr(c, tmp, rstd_ps, st=st):
                s("dve", lambda v, c=c, tmp=tmp, rstd_ps=rstd_ps:
                  v.tensor_tensor(A[:, c, st * 512:(st + 1) * 512], tmp,
                                  rstd_ps, ALU.mult))
            emit_ln(P, C, [zst[:, c, :] for c in range(DC)], wr, 512)

    ctx_es.close()
    if debug_stop == "z":
        return P.finish()
    if debug_stop == "h":
        for st in range(2):
            emit_store_tm(P, C, [A[:, c, st * 512:(st + 1) * 512]
                                 for c in range(DC)], dbg, st * 512, 512,
                          f"h{st}")
        return P.finish()

    if layer == 0:
        emit_ffn(P, C, A, xout, layer=0, router=(gate_w,),
                 weights=(w_gate, w_up, w_down), debug_stop=debug_stop,
                 dbg=dbg)
    else:
        emit_ffn(P, C, A, xout, layer=1, router=(gwr, bcorr),
                 weights=(gwT, uwT, dwT, shgT, shuT, shdT),
                 debug_stop=debug_stop, dbg=dbg)
    return P.finish()



def emit_ffn(P, C, A, xout, layer, router, weights, debug_stop=None,
             dbg=None):
    nc, s = P.nc, P.s
    TW = 256 if layer == 0 else 512
    NST = TC // TW
    NE = E0 if layer == 0 else E1
    FC = FC0 if layer == 0 else FC1
    FF = F0 if layer == 0 else F1
    nexp = NE if layer == 0 else NE + 2
    fcgroups = [list(range(g, min(g + 4, FC))) for g in range(0, FC, 4)]
    NWE = max(DC * FF, FC * D)

    ncpt = TW // 128
    with ExitStack() as es:
        gwb = es.enter_context(nc.sbuf_tensor("gwb", [128, DC, NE], BF16))
        cw = es.enter_context(nc.sbuf_tensor("cw", [128, ncpt, NE], F32))
        h_bf = es.enter_context(nc.sbuf_tensor("h_bf_st", [128, DC, TW], BF16))
        wflat = es.enter_context(nc.sbuf_tensor("wflat", [128, NWE], BF16))
        a_f = es.enter_context(nc.sbuf_tensor("a_f", [128, FC, TW], F32))
        a_bf = es.enter_context(nc.sbuf_tensor("a_bf", [128, FC, TW], BF16))
        y_sb = es.enter_context(nc.sbuf_tensor("y_sb", [128, DC, TW], F32))
        sm = es.enter_context(nc.sbuf_tensor("sm", [128, NE], F32))
        m1 = es.enter_context(nc.sbuf_tensor("m1", [128, 1], F32))
        m2 = es.enter_context(nc.sbuf_tensor("m2", [128, 1], F32))
        eq = es.enter_context(nc.sbuf_tensor("eq", [128, NE], F32))
        crow = es.enter_context(nc.sbuf_tensor("crow", [1, TW], F32))
        if layer == 1:
            sg = es.enter_context(nc.sbuf_tensor("sg", [128, NE], F32))
            gsb = es.enter_context(nc.sbuf_tensor("gsb", [128, 4], F32))
            bcb = es.enter_context(nc.sbuf_tensor("bcb", [1, NE], F32))
            bcbc = es.enter_context(nc.sbuf_tensor("bcbc", [128, NE], F32))

        gr_ap = router[0]
        s("pool", lambda g: g.dma_start(
            out=gwb[:, :, :], in_=gr_ap.rearrange("(c p) e -> p c e", p=128)),
            dma=True)
        if layer == 1:
            s("sp", lambda sy: sy.dma_start(out=bcb[:, :], in_=router[1]),
              dma=True)
            s("pe", lambda t: t.matmul(C["P_AUX"][:, 0:NE],
                                       C["ones_row_f"][:, :], bcb[:, :],
                                       start=True, stop=True))
            s("dve", lambda v: v.tensor_copy(bcbc[:, :], C["P_AUX"][:, 0:NE]))

        for st in range(NST):
            for c in range(DC):
                s("act", lambda a, c=c, st=st: a.activation(
                    h_bf[:, c, :], A[:, c, st * TW:(st + 1) * TW],
                    AF.Identity, bias=0.0, scale=1.0))
            for nn in range(ncpt):
                for dk in range(DC):
                    s("pe", lambda t, nn=nn, dk=dk: t.matmul(
                        C["P_O"][:, 0:NE], h_bf[:, dk, nn * 128:(nn + 1) * 128],
                        gwb[:, dk, :], start=(dk == 0), stop=(dk == DC - 1)))
                if layer == 0:
                    emit_router_l0(P, C, cw, nn, sm, m1, m2, eq)
                else:
                    emit_router_l1(P, C, cw, nn, sm, sg, gsb, bcbc, m1, m2, eq)
            if debug_stop == "cw":
                s("sp", lambda sy: sy.dma_start(
                    out=dbg, in_=cw[:, :, :]), dma=True)
                return

            s("pool", lambda g: g.memset(y_sb[:, :, :], 0.0))
            for e in range(nexp):
                shared = (layer == 1 and e >= NE)
                half_sh = e - NE
                if layer == 0:
                    gap, uap, dap = weights[0][e], weights[1][e], weights[2][e]
                elif not shared:
                    gap, uap, dap = weights[0][e], weights[1][e], weights[2][e]
                else:
                    shgT, shuT, shdT = weights[3], weights[4], weights[5]
                    gap = shgT[:, half_sh * F1:(half_sh + 1) * F1]
                    uap = shuT[:, half_sh * F1:(half_sh + 1) * F1]
                    dap = shdT[half_sh * F1:(half_sh + 1) * F1, :]
                for which, mat_ap in (("g", gap), ("u", uap)):
                    s("pool", lambda g, mat_ap=mat_ap: g.dma_start(
                        out=wflat[:, 0:DC * FF].rearrange(
                            "p (c f) -> p c f", f=FF),
                        in_=mat_ap.rearrange("(c p) f -> p c f", p=128)),
                        dma=True)
                    for grp in fcgroups:
                        for u, fc in enumerate(grp):
                            for dk in range(DC):
                                s("pe", lambda t, st=st, u=u, fc=fc, dk=dk:
                                  t.matmul(
                                    C["PS_S"][:, u * TW:(u + 1) * TW],
                                    wflat[:, dk * FF + fc * 128:
                                          dk * FF + (fc + 1) * 128],
                                    h_bf[:, dk, :],
                                    start=(dk == 0), stop=(dk == DC - 1)))
                        for u, fc in enumerate(grp):
                            if which == "g":
                                s("act", lambda a, u=u, fc=fc: a.activation(
                                    a_f[:, fc, :],
                                    C["PS_S"][:, u * TW:(u + 1) * TW],
                                    AF.Silu, bias=0.0, scale=1.0))
                            else:
                                s("dve", lambda v, u=u, fc=fc: v.tensor_tensor(
                                    a_f[:, fc, :], a_f[:, fc, :],
                                    C["PS_S"][:, u * TW:(u + 1) * TW],
                                    ALU.mult))
                if not shared:
                    for nn in range(ncpt):
                        s("pe", lambda t, nn=nn, e=e: t.transpose(
                            C["PS_S"][0:1, nn * 128:(nn + 1) * 128],
                            cw[:, nn, e:e + 1], C["ident_f"][:, :]))
                    s("dve", lambda v: v.tensor_copy(crow[0:1, 0:TW],
                                                     C["PS_S"][0:1, 0:TW]))
                    s("pe", lambda t: t.matmul(
                        C["P_O"][:, 0:TW], C["ones_row_f"][:, :],
                        crow[0:1, 0:TW], start=True, stop=True))
                    for fc in range(FC):
                        s("dve", lambda v, fc=fc: v.tensor_tensor(
                            a_bf[:, fc, :], a_f[:, fc, :], C["P_O"][:, 0:TW],
                            ALU.mult))
                else:
                    for fc in range(FC):
                        s("dve", lambda v, fc=fc: v.tensor_copy(
                            a_bf[:, fc, :], a_f[:, fc, :]))
                s("pool", lambda g, dap=dap: g.dma_start(
                    out=wflat[:, 0:FC * D].rearrange("p (c f) -> p c f", f=D),
                    in_=dap.rearrange("(c p) f -> p c f", p=128)), dma=True)
                for m in range(DC):
                    for fc in range(FC):
                        s("pe", lambda t, m=m, fc=fc: t.matmul(
                            C["P_AUX"][:, 0:TW],
                            wflat[:, fc * D + m * 128:fc * D + (m + 1) * 128],
                            a_bf[:, fc, :], start=(fc == 0),
                            stop=(fc == FC - 1)))
                    s("dve", lambda v, m=m: v.tensor_tensor(
                        y_sb[:, m, :], y_sb[:, m, :], C["P_AUX"][:, 0:TW],
                        ALU.add))
            if debug_stop == "y0":
                emit_store_tm(P, C, [y_sb[:, c, :] for c in range(DC)],
                              dbg, 0, TW, "y0")
                return
            for m in range(DC):
                s("dve", lambda v, m=m, st=st: v.tensor_tensor(
                    y_sb[:, m, :], y_sb[:, m, :],
                    A[:, m, st * TW:(st + 1) * TW], ALU.add))

            def wr2(c, tmp, rstd_ps):
                s("dve", lambda v, c=c, tmp=tmp, rstd_ps=rstd_ps:
                  v.tensor_tensor(y_sb[:, c, :], tmp, rstd_ps, ALU.mult))
            emit_ln(P, C, [y_sb[:, c, :] for c in range(DC)], wr2, TW)
            emit_store_tm(P, C, [y_sb[:, c, :] for c in range(DC)], xout,
                          st * TW, TW, f"o{st}")


def emit_router_l0(P, C, cw, nn, sm, m1, m2, eq):
    s = P.s
    s("dve", lambda v: v.tensor_reduce(m1[:, :], C["P_O"][:, 0:E0], AX.X,
                                       ALU.max))
    s("dve", lambda v: v.tensor_scalar(m2[:, :], m1[:, :], -1.0, None,
                                       op0=ALU.mult))
    s("act", lambda a: a.activation(sm[:, :], C["P_O"][:, 0:E0], AF.Exp,
                                    bias=m2[:, :], scale=1.0))
    s("dve", lambda v: v.tensor_reduce(m1[:, :], sm[:, :], AX.X, ALU.max))
    s("dve", lambda v: v.tensor_scalar(eq[:, :], sm[:, :], m1[:, :],
                                       float(NEG), op0=ALU.is_ge,
                                       op1=ALU.mult))
    s("dve", lambda v: v.tensor_tensor(eq[:, :], sm[:, :], eq[:, :], ALU.add))
    s("dve", lambda v: v.tensor_reduce(m2[:, :], eq[:, :], AX.X, ALU.max))
    s("dve", lambda v: v.tensor_scalar(eq[:, :], sm[:, :], m2[:, :], None,
                                       op0=ALU.is_ge))
    s("dve", lambda v: v.tensor_tensor(sm[:, :], sm[:, :], eq[:, :], ALU.mult))
    s("dve", lambda v: v.tensor_tensor(m1[:, :], m1[:, :], m2[:, :], ALU.add))
    s("dve", lambda v: v.reciprocal(m1[:, :], m1[:, :]))
    s("dve", lambda v, nn=nn: v.tensor_scalar(cw[:, nn, :], sm[:, :],
                                              m1[:, :], None, op0=ALU.mult))


def emit_router_l1(P, C, cw, nn, s4c, sg, gsb, bcbc, m1, m2, eq):
    s = P.s
    NGRP, EPG, TOPK, RSCALE = 4, 4, 4, 2.5
    s("act", lambda a: a.activation(sg[:, :], C["P_O"][:, 0:E1], AF.Sigmoid,
                                    bias=0.0, scale=1.0))
    s("dve", lambda v: v.tensor_tensor(s4c[:, :], sg[:, :], bcbc[:, :],
                                       ALU.add))
    for g4 in range(NGRP):
        seg = s4c[:, g4 * EPG:(g4 + 1) * EPG]
        s("dve", lambda v, seg=seg: v.tensor_reduce(m1[:, :], seg, AX.X,
                                                    ALU.max))
        s("dve", lambda v, seg=seg: v.tensor_scalar(
            eq[:, 0:EPG], seg, m1[:, :], float(NEG), op0=ALU.is_ge,
            op1=ALU.mult))
        s("dve", lambda v, seg=seg: v.tensor_tensor(eq[:, 0:EPG], seg,
                                                    eq[:, 0:EPG], ALU.add))
        s("dve", lambda v: v.tensor_reduce(m2[:, :], eq[:, 0:EPG], AX.X,
                                           ALU.max))
        s("dve", lambda v, g4=g4: v.tensor_tensor(gsb[:, g4:g4 + 1], m1[:, :],
                                                  m2[:, :], ALU.add))
    s("dve", lambda v: v.tensor_reduce(m1[:, :], gsb[:, :], AX.X, ALU.max))
    s("dve", lambda v: v.tensor_scalar(eq[:, 0:NGRP], gsb[:, :], m1[:, :],
                                       float(NEG), op0=ALU.is_ge,
                                       op1=ALU.mult))
    s("dve", lambda v: v.tensor_tensor(eq[:, 0:NGRP], gsb[:, :], eq[:, 0:NGRP],
                                       ALU.add))
    s("dve", lambda v: v.tensor_reduce(m2[:, :], eq[:, 0:NGRP], AX.X, ALU.max))
    s("dve", lambda v: v.tensor_scalar(gsb[:, :], gsb[:, :], m2[:, :], None,
                                       op0=ALU.is_ge))
    for rep in range(2):
        if rep == 1:
            s("dve", lambda v: v.tensor_tensor(s4c[:, :], sg[:, :], bcbc[:, :],
                                               ALU.add))
        for g4 in range(NGRP):
            s("dve", lambda v, g4=g4: v.tensor_scalar(
                m1[:, :], gsb[:, g4:g4 + 1], float(-NEG), float(NEG),
                op0=ALU.mult, op1=ALU.add))
            s("dve", lambda v, g4=g4: v.tensor_scalar(
                s4c[:, g4 * EPG:(g4 + 1) * EPG],
                s4c[:, g4 * EPG:(g4 + 1) * EPG], m1[:, :], None, op0=ALU.add))
        if rep == 0:
            for it in range(TOPK):
                s("dve", lambda v: v.tensor_reduce(m2[:, :], s4c[:, :], AX.X,
                                                   ALU.max))
                if it < TOPK - 1:
                    s("dve", lambda v: v.tensor_scalar(
                        eq[:, :], s4c[:, :], m2[:, :], float(NEG),
                        op0=ALU.is_ge, op1=ALU.mult))
                    s("dve", lambda v: v.tensor_tensor(s4c[:, :], s4c[:, :],
                                                       eq[:, :], ALU.add))
    s("dve", lambda v: v.tensor_scalar(eq[:, :], s4c[:, :], m2[:, :], None,
                                       op0=ALU.is_ge))
    s("dve", lambda v: v.tensor_tensor(eq[:, :], sg[:, :], eq[:, :], ALU.mult))
    s("dve", lambda v: v.tensor_reduce(m1[:, :], eq[:, :], AX.X, ALU.add))
    s("dve", lambda v: v.tensor_scalar(m1[:, :], m1[:, :], 1e-20, None,
                                       op0=ALU.add))
    s("dve", lambda v: v.reciprocal(m1[:, :], m1[:, :]))
    s("dve", lambda v: v.tensor_scalar(m1[:, :], m1[:, :], RSCALE, None,
                                       op0=ALU.mult))
    s("dve", lambda v, nn=nn: v.tensor_scalar(cw[:, nn, :], eq[:, :],
                                              m1[:, :], None, op0=ALU.mult))



def _to_pc(b):
    return np.ascontiguousarray(np.asarray(b, np.float32).reshape(DC, 128).T)


_prog_cache = {}


def _get_prog(kind, layer):
    key = (kind, layer)
    if key not in _prog_cache:
        _prog_cache[key] = build_A(layer) if kind == "A" else build_B(layer)
    return _prog_cache[key]


def _run(nc, in_maps):
    res = run_bass_kernel_spmd(nc, in_maps, core_ids=list(range(8)))
    return res.results


def _window_feeds(ko, vo):
    kfeed, vfeed = [], []
    for b in range(2):
        kb = np.concatenate(ko[4 * b:4 * b + 4], axis=1)
        vb = np.concatenate(vo[4 * b:4 * b + 4], axis=0)
        for q4 in range(4):
            wlo = q4 * TC - WIN
            kf = np.zeros((D, KSPAN), dtype=kb.dtype)
            vf = np.zeros((KSPAN, D), dtype=vb.dtype)
            lo = max(0, wlo)
            kf[:, lo - wlo:KSPAN] = kb[:, lo:wlo + KSPAN]
            vf[lo - wlo:KSPAN, :] = vb[lo:wlo + KSPAN, :]
            kfeed.append(np.ascontiguousarray(kf))
            vfeed.append(np.ascontiguousarray(vf))
    return kfeed, vfeed


def kernel(**inputs):
    f32 = np.float32
    x = np.asarray(inputs["L0_attn_x"], f32)
    streams = [np.ascontiguousarray(x[c // 4, (c % 4) * TC:(c % 4 + 1) * TC, :])
               for c in range(8)]

    ncA0 = _get_prog("A", 0)
    wq = np.asarray(inputs["L0_attn_Wq"], f32)
    wk = np.asarray(inputs["L0_attn_Wk"], f32)
    wv = np.asarray(inputs["L0_attn_Wv"], f32)
    rA0 = _run(ncA0, [{"xin": streams[c], "Wq": wq, "Wk": wk, "Wv": wv}
                      for c in range(8)])
    qo = [np.asarray(r["qo"]) for r in rA0]
    ko = [np.asarray(r["ko"]) for r in rA0]
    vo = [np.asarray(r["vo"]) for r in rA0]
    kfeed, vfeed = _window_feeds(ko, vo)

    ncB0 = _get_prog("B", 0)
    rB0 = _run(ncB0, [{
        "xin": streams[c], "qin": qo[c], "kin": kfeed[c], "vin": vfeed[c],
        "coreq4": np.full((1, 1), float(c % 4), f32),
        "Wo": np.asarray(inputs["L0_attn_Wo"], f32),
        "gate_w": np.asarray(inputs["L0_ffn_gate_w"], f32),
        "w_gate": np.asarray(inputs["L0_ffn_w_gate"], f32),
        "w_up": np.asarray(inputs["L0_ffn_w_up"], f32),
        "w_down": np.asarray(inputs["L0_ffn_w_down"], f32),
    } for c in range(8)])
    streams = [np.asarray(r["xout"]) for r in rB0]

    ncA1 = _get_prog("A", 1)
    rA1 = _run(ncA1, [{
        "xin": streams[c],
        "cq": np.asarray(inputs["L1_attn_wq_codes"], f32),
        "sq": np.asarray(inputs["L1_attn_wq_scales"], f32),
        "bq": _to_pc(inputs["L1_attn_bq"]),
        "ck": np.asarray(inputs["L1_attn_wk_codes"], f32),
        "sk": np.asarray(inputs["L1_attn_wk_scales"], f32),
        "bk": _to_pc(inputs["L1_attn_bk"]),
        "cv": np.asarray(inputs["L1_attn_wv_codes"], f32),
        "sv": np.asarray(inputs["L1_attn_wv_scales"], f32),
        "bv": _to_pc(inputs["L1_attn_bv"]),
    } for c in range(8)])
    qo = [np.asarray(r["qo"]) for r in rA1]
    ko = [np.asarray(r["ko"]) for r in rA1]
    vo = [np.asarray(r["vo"]) for r in rA1]
    kfull = [np.ascontiguousarray(np.concatenate(ko[4 * b:4 * b + 4], axis=1))
             for b in range(2)]
    vfull = [np.ascontiguousarray(np.concatenate(vo[4 * b:4 * b + 4], axis=0))
             for b in range(2)]

    ncB1 = _get_prog("B", 1)
    gwT = np.ascontiguousarray(
        np.asarray(inputs["L1_ffn_gate_w"], f32).transpose(0, 2, 1))
    uwT = np.ascontiguousarray(
        np.asarray(inputs["L1_ffn_up_w"], f32).transpose(0, 2, 1))
    dwT = np.ascontiguousarray(
        np.asarray(inputs["L1_ffn_down_w"], f32).transpose(0, 2, 1))
    shgT = np.ascontiguousarray(np.asarray(inputs["L1_ffn_sh_gate_w"], f32).T)
    shuT = np.ascontiguousarray(np.asarray(inputs["L1_ffn_sh_up_w"], f32).T)
    shdT = np.ascontiguousarray(np.asarray(inputs["L1_ffn_sh_down_w"], f32).T)
    gwr = np.ascontiguousarray(np.asarray(inputs["L1_ffn_gate_weight"], f32).T)
    bcorr = np.ascontiguousarray(
        np.asarray(inputs["L1_ffn_e_score_correction_bias"], f32)[None, :])
    rB1 = _run(ncB1, [{
        "xin": streams[c], "qin": qo[c], "kin": kfull[c // 4],
        "vin": vfull[c // 4],
        "oc": np.asarray(inputs["L1_attn_wo_codes"], f32),
        "osc": np.asarray(inputs["L1_attn_wo_scales"], f32),
        "bo": _to_pc(inputs["L1_attn_bo"]),
        "gwr": gwr, "bcorr": bcorr, "gwT": gwT, "uwT": uwT, "dwT": dwT,
        "shgT": shgT, "shuT": shuT, "shdT": shdT,
    } for c in range(8)])
    outs = [np.asarray(r["xout"]) for r in rB1]
    out = np.stack([np.concatenate(outs[0:4], 0), np.concatenate(outs[4:8], 0)])
    return out.astype(np.float32)


# version 03
# speedup vs baseline: 1.1989x; 1.1989x over previous
import math
import numpy as np
import ml_dtypes
import concourse.bass as bass
import concourse.mybir as mybir
from concourse.bass_utils import run_bass_kernel_spmd
from contextlib import ExitStack

F32 = mybir.dt.float32
BF16 = mybir.dt.bfloat16
AF = mybir.ActivationFunctionType
ALU = mybir.AluOpType
AX = mybir.AxisListType

D = 2048
DC = 16
NH = 16
TC = 1024
NTC = 8
SEQ = 4096
WIN = 2048
TAUI = 1.0 / math.sqrt(128.0)
NEG = -1e30

E0, F0, FC0 = 8, 1408, 11
E1, F1, FC1 = 16, 1024, 8
FSH = 2048

KSPAN = 3072

ENGNAMES = ("pe", "dve", "act", "pool", "sp")
FULL_SERIAL = True


class Prog:
    def __init__(self):
        self.nc = bass.Bass("TRN2", target_bir_lowering=False, debug=False)
        self.steps = []
        self.ctx = ExitStack()

    def s(self, eng, fn, dma=False):
        key = eng + ("_dma" if dma else "")
        inc = 16 if dma else 1
        self.steps.append((eng, fn, inc, key))

    def sb(self, name, shape, dtype):
        return self.ctx.enter_context(self.nc.sbuf_tensor(name, shape, dtype))

    def ps(self, name, shape, dtype):
        return self.ctx.enter_context(self.nc.psum_tensor(name, shape, dtype))

    def finish(self):
        nc = self.nc
        with ExitStack() as es:
            sems = {}
            for e in ENGNAMES:
                sems[e] = es.enter_context(nc.semaphore("sem_" + e))
                sems[e + "_dma"] = es.enter_context(nc.semaphore("semd_" + e))
            block = es.enter_context(nc.Block())

            steps = self.steps
            cums = []
            run = {}
            for (eng, fn, inc, key) in steps:
                cums.append(dict(run))
                run[key] = run.get(key, 0) + inc
            total = run

            def emit(engine_obj, name):
                for i, (eng, fn, inc, key) in enumerate(steps):
                    if eng != name:
                        continue
                    if i > 0:
                        _, _, _, pkey = steps[i - 1]
                        same_compute = (pkey == key and not key.endswith("_dma")
                                        and not FULL_SERIAL)
                        if not same_compute:
                            engine_obj.wait_ge(sems[pkey], cums[i].get(pkey, 0))
                    fn(engine_obj).then_inc(sems[key], inc)
                if name == "sp":
                    for k, v in total.items():
                        engine_obj.wait_ge(sems[k], v)

            @block.gpsimd
            def _(g):
                emit(g, "pool")

            @block.vector
            def _(v):
                emit(v, "dve")

            @block.scalar
            def _(a):
                emit(a, "act")

            @block.tensor
            def _(t):
                emit(t, "pe")

            @block.sync
            def _(s):
                emit(s, "sp")

        self.ctx.close()
        return nc



def setup_consts(P, need_f32_ident=True, padflag_src=None):
    t = {}
    t["ident_bf"] = P.sb("ident_bf", [128, 128], BF16)
    t["ones_col_bf"] = P.sb("ones_col_bf", [128, 1], BF16)
    t["ones_row_f"] = P.sb("ones_row_f", [1, 128], F32)
    t["ones_row_bf"] = P.sb("ones_row_bf", [1, 128], BF16)
    if need_f32_ident:
        t["ident_f"] = P.sb("ident_f", [128, 128], F32)

    P.s("pool", lambda g: g.memset(t["ident_bf"][:, :], 0.0))
    P.s("pool", lambda g: g.affine_select(
        out=t["ident_bf"][:, :], in_=t["ident_bf"][:, :],
        compare_op=ALU.not_equal, fill=1.0, base=0,
        pattern=[[-1, 128]], channel_multiplier=1))
    if need_f32_ident:
        P.s("pool", lambda g: g.memset(t["ident_f"][:, :], 0.0))
        P.s("pool", lambda g: g.affine_select(
            out=t["ident_f"][:, :], in_=t["ident_f"][:, :],
            compare_op=ALU.not_equal, fill=1.0, base=0,
            pattern=[[-1, 128]], channel_multiplier=1))
    P.s("pool", lambda g: g.memset(t["ones_col_bf"][:, :], 1.0))
    P.s("pool", lambda g: g.memset(t["ones_row_f"][:, :], 1.0))
    P.s("pool", lambda g: g.memset(t["ones_row_bf"][:, :], 1.0))

    if padflag_src is not None:
        t["padflag"] = P.sb("padflag", [1, KSPAN], BF16)
        with ExitStack() as pes:
            wlo = pes.enter_context(P.nc.sbuf_tensor("wlo", [1, 1], F32))
            pf32 = pes.enter_context(
                P.nc.sbuf_tensor("pf32", [1, KSPAN], F32))
            P.s("sp", lambda s: s.dma_start(out=wlo[:, :], in_=padflag_src),
                dma=True)
            P.s("dve", lambda v: v.tensor_scalar(wlo[:, :], wlo[:, :], 1024.0,
                                                 -2048.0, op0=ALU.mult,
                                                 op1=ALU.add))
            P.s("pool", lambda g: g.iota(pf32[:, :], [[1, KSPAN]], base=0,
                                         channel_multiplier=0,
                                         allow_small_or_imprecise_dtypes=True))
            P.s("dve", lambda v: v.tensor_scalar(pf32[:, :], pf32[:, :],
                                                 wlo[:, :], None, op0=ALU.add))
            P.s("dve", lambda v: v.tensor_scalar(pf32[:, :], pf32[:, :], 0.0,
                                                 None, op0=ALU.is_lt))
            P.s("dve", lambda v: v.tensor_scalar(t["padflag"][:, :],
                                                 pf32[:, :], float(NEG), None,
                                                 op0=ALU.mult))
    return t


def setup_sels(P):
    sels = P.sb("sels", [64, DC * 128], BF16)
    with ExitStack() as ses:
        iob = ses.enter_context(
            P.nc.sbuf_tensor("sels_iob", [64, DC * 128], F32))
        pidx = ses.enter_context(P.nc.sbuf_tensor("sels_pidx", [64, 1], F32))
        P.s("pool", lambda g: g.iota(iob[:, :], [[1, 64], [0, 32]], base=0,
                                     channel_multiplier=0,
                                     allow_small_or_imprecise_dtypes=True))
        P.s("pool", lambda g: g.iota(pidx[:, :], [[0, 1]], base=0,
                                     channel_multiplier=1,
                                     allow_small_or_imprecise_dtypes=True))
        P.s("dve", lambda v: v.tensor_scalar(sels[:, :], iob[:, :],
                                             pidx[:, :], None,
                                             op0=ALU.is_equal))
    return sels


def emit_load_stream_fm(P, C, xin_ap, xfm, dtype):
    nc = P.nc
    with ExitStack() as es:
        xtm = es.enter_context(nc.sbuf_tensor("xtm_tmp", [128, NTC, D], F32))
        P.s("sp", lambda s: s.dma_start(
            out=xtm[:, :, :], in_=xin_ap.rearrange("(n p) d -> p n d", p=128)),
            dma=True)
        if dtype == BF16:
            xtmb = es.enter_context(
                nc.sbuf_tensor("xtmb_tmp", [128, NTC, D], BF16))
            P.s("dve", lambda v: v.tensor_copy(xtmb[:, :, :], xtm[:, :, :]))
            pt = C["PT"]
            for c in range(DC):
                for n in range(NTC):
                    P.s("pe", lambda t, c=c, n=n: t.transpose(
                        pt[:, n * 128:(n + 1) * 128],
                        xtmb[:, n, c * 128:(c + 1) * 128], C["ident_bf"][:, :]))
                P.s("dve", lambda v, c=c: v.tensor_copy(
                    xfm[:, c, :], pt[:, 0:1024]))
        else:
            psS = C["PS_S"]
            for c in range(DC):
                for n in range(NTC):
                    P.s("pe", lambda t, c=c, n=n: t.transpose(
                        psS[:, (n % 4) * 128:(n % 4) * 128 + 128],
                        xtm[:, n, c * 128:(c + 1) * 128], C["ident_f"][:, :]))
                    if n % 4 == 3:
                        P.s("dve", lambda v, c=c, n=n: v.tensor_copy(
                            xfm[:, c, (n - 3) * 128:(n + 1) * 128],
                            psS[:, 0:512]))


def emit_dequant(P, C, sels, codes_bf, scl_bf, dk_idx_pairs):
    PO = C["P_O"]
    for base, dk in dk_idx_pairs:
        for n4 in range(4):
            P.s("pe", lambda t, dk=dk, n4=n4: t.matmul(
                PO[:, 0:512], sels[:, dk * 128:(dk + 1) * 128],
                scl_bf[:, n4 * 512:(n4 + 1) * 512], start=True, stop=True))
            P.s("dve", lambda v, base=base, n4=n4: v.tensor_tensor(
                codes_bf[:, base + n4 * 512:base + (n4 + 1) * 512],
                codes_bf[:, base + n4 * 512:base + (n4 + 1) * 512],
                PO[:, 0:512], ALU.mult))


def emit_ln(P, C, zb_slices, out_writes, tw):
    s = P.s
    zc = C["ln_zc"]
    mu = C["ln_mu"]
    rstd = C["ln_rstd"]
    var = C["ln_var"]
    PS_S, P_O, P_AUX = C["PS_S"], C["P_O"], C["P_AUX"]
    for c in range(DC):
        s("act", lambda a, c=c: a.activation(zc[:, 0:tw], zb_slices[c],
                                             AF.Identity, bias=0.0, scale=1.0))
        s("act", lambda a, c=c: a.activation(zc[:, tw:2 * tw], zb_slices[c],
                                             AF.Square, bias=0.0, scale=1.0))
        s("pe", lambda t, c=c: t.matmul(PS_S[0:1, 0:tw],
                                        C["ones_col_bf"][:, :],
                                        zc[:, 0:tw], start=(c == 0),
                                        stop=(c == DC - 1)))
        s("pe", lambda t, c=c: t.matmul(PS_S[0:1, 512:512 + tw],
                                        C["ones_col_bf"][:, :],
                                        zc[:, tw:2 * tw], start=(c == 0),
                                        stop=(c == DC - 1)))
    s("dve", lambda v: v.tensor_scalar(mu[:, 0:tw], PS_S[0:1, 0:tw],
                                       1.0 / D, None, op0=ALU.mult))
    s("dve", lambda v: v.tensor_scalar(var[:, 0:tw], PS_S[0:1, 512:512 + tw],
                                       1.0 / D, None, op0=ALU.mult))
    s("act", lambda a: a.activation(rstd[:, 0:tw], mu[:, 0:tw], AF.Square,
                                    bias=0.0, scale=1.0))
    s("dve", lambda v: v.tensor_tensor(var[:, 0:tw], var[:, 0:tw],
                                       rstd[:, 0:tw], ALU.subtract))
    s("dve", lambda v: v.tensor_scalar(var[:, 0:tw], var[:, 0:tw], 1e-5, None,
                                       op0=ALU.add))
    s("act", lambda a: a.activation(rstd[:, 0:tw], var[:, 0:tw], AF.Sqrt,
                                    bias=0.0, scale=1.0))
    s("dve", lambda v: v.reciprocal(rstd[:, 0:tw], rstd[:, 0:tw]))
    s("pe", lambda t: t.matmul(P_AUX[:, 0:tw], C["ones_row_f"][:, :],
                               mu[:, 0:tw], start=True, stop=True))
    s("pe", lambda t: t.matmul(P_O[:, 0:tw], C["ones_row_f"][:, :],
                               rstd[:, 0:tw], start=True, stop=True))
    tmp = C["ln_tmp"]
    for c in range(DC):
        s("dve", lambda v, c=c: v.tensor_tensor(tmp[:, 0:tw], zb_slices[c],
                                                P_AUX[:, 0:tw], ALU.subtract))
        out_writes(c, tmp[:, 0:tw], P_O[:, 0:tw])


def emit_store_tm(P, C, fm_slices, out_ap, row0, tw, tag):
    nc = P.nc
    ntile = tw // 128
    with ExitStack() as es:
        otm = es.enter_context(nc.sbuf_tensor("otm_" + tag, [128, D], F32))
        for n in range(ntile):
            for c in range(DC):
                P.s("pe", lambda t, c=c, n=n: t.transpose(
                    C["PS_S"][:, (c % 4) * 128:(c % 4) * 128 + 128],
                    fm_slices[c][:, n * 128:(n + 1) * 128], C["ident_f"][:, :]))
                P.s("dve", lambda v, c=c, n=n: v.tensor_copy(
                    otm[:, c * 128:(c + 1) * 128],
                    C["PS_S"][:, (c % 4) * 128:(c % 4) * 128 + 128]))
            P.s("sp", lambda sy, n=n: sy.dma_start(
                out=out_ap[row0 + n * 128:row0 + (n + 1) * 128, :],
                in_=otm[:, :]), dma=True)



def build_A(layer):
    P = Prog()
    nc = P.nc
    xin = nc.dram_tensor("xin", [TC, D], F32, kind="ExternalInput").ap()
    if layer == 0:
        wlist = [nc.dram_tensor(n, [D, D], F32, kind="ExternalInput").ap()
                 for n in ("Wq", "Wk", "Wv")]
        slist = blist = [None] * 3
    else:
        wlist, slist, blist = [], [], []
        for nm in ("q", "k", "v"):
            wlist.append(nc.dram_tensor("c" + nm, [D, D], F32,
                                        kind="ExternalInput").ap())
            slist.append(nc.dram_tensor("s" + nm, [64, D], F32,
                                        kind="ExternalInput").ap())
            blist.append(nc.dram_tensor("b" + nm, [128, DC], F32,
                                        kind="ExternalInput").ap())
    qo = nc.dram_tensor("qo", [D, TC], BF16, kind="ExternalOutput").ap()
    ko = nc.dram_tensor("ko", [D, TC], BF16, kind="ExternalOutput").ap()
    vo = nc.dram_tensor("vo", [TC, D], BF16, kind="ExternalOutput").ap()
    outs = [qo, ko, vo]

    C = {}
    C["PS_S"] = P.ps("PS_S", [128, 2560], F32)
    C["P_O"] = P.ps("P_O", [128, 512], F32)
    C["P_AUX"] = P.ps("P_AUX", [128, 512], F32)
    C["PT"] = P.ps("PT", [128, 1024], BF16)
    C.update(setup_consts(P, need_f32_ident=False))
    if layer == 1:
        sels = setup_sels(P)

    xfm = P.sb("xfm", [128, DC, TC], BF16)
    emit_load_stream_fm(P, C, xin, xfm, BF16)

    wb = P.sb("wb", [128, DC * D], BF16)
    proj = P.sb("proj", [128, DC, TC], BF16)
    vtmn = P.sb("vtmn", [128, D], BF16)
    if layer == 1:
        sclb = P.sb("sclb", [64, D], BF16)
        biasb = P.sb("biasb", [128, DC], F32)

    for pi in range(3):
        wap = wlist[pi]
        P.s("pool", lambda g, wap=wap: g.dma_start(
            out=wb[:, :].rearrange("p (c f) -> p c f", f=D),
            in_=wap.rearrange("(c p) f -> p c f", p=128)), dma=True)
        if layer == 1:
            sap, bap = slist[pi], blist[pi]
            P.s("pool", lambda g, sap=sap: g.dma_start(
                out=sclb[:, :], in_=sap), dma=True)
            P.s("sp", lambda s, bap=bap: s.dma_start(
                out=biasb[:, :], in_=bap), dma=True)
            emit_dequant(P, C, sels, wb, sclb,
                         [(dk * D, dk) for dk in range(DC)])
        for m in range(DC):
            for st in range(2):
                for dk in range(DC):
                    P.s("pe", lambda t, m=m, st=st, dk=dk: t.matmul(
                        C["PS_S"][:, 0:512],
                        wb[:, dk * D + m * 128:dk * D + (m + 1) * 128],
                        xfm[:, dk, st * 512:(st + 1) * 512],
                        start=(dk == 0), stop=(dk == DC - 1)))
                if layer == 1:
                    P.s("act", lambda a, m=m, st=st: a.activation(
                        proj[:, m, st * 512:(st + 1) * 512],
                        C["PS_S"][:, 0:512], AF.Identity,
                        bias=biasb[:, m:m + 1], scale=1.0))
                else:
                    P.s("act", lambda a, m=m, st=st: a.activation(
                        proj[:, m, st * 512:(st + 1) * 512],
                        C["PS_S"][:, 0:512], AF.Identity, bias=0.0, scale=1.0))
        if pi < 2:
            P.s("sp", lambda s, pi=pi: s.dma_start(
                out=outs[pi].rearrange("(m p) t -> p m t", p=128),
                in_=proj[:, :, :]), dma=True)
        else:
            for n in range(NTC):
                for c in range(DC):
                    P.s("pe", lambda t, n=n, c=c: t.transpose(
                        C["PT"][:, (c % 8) * 128:(c % 8) * 128 + 128],
                        proj[:, c, n * 128:(n + 1) * 128], C["ident_bf"][:, :]))
                    if c % 8 == 7:
                        P.s("dve", lambda v, n=n, c=c: v.tensor_copy(
                            vtmn[:, (c - 7) * 128:(c + 1) * 128], C["PT"][:, :]))
                P.s("sp", lambda s, n=n: s.dma_start(
                    out=vo[n * 128:(n + 1) * 128, :], in_=vtmn[:, :]), dma=True)

    return P.finish()



def build_B(layer, debug_stop=None):
    P = Prog()
    nc = P.nc
    s = P.s
    xin = nc.dram_tensor("xin", [TC, D], F32, kind="ExternalInput").ap()
    qin = nc.dram_tensor("qin", [D, TC], BF16, kind="ExternalInput").ap()
    kspan = KSPAN if layer == 0 else SEQ
    kin = nc.dram_tensor("kin", [D, kspan], BF16, kind="ExternalInput").ap()
    vin = nc.dram_tensor("vin", [kspan, D], BF16, kind="ExternalInput").ap()
    if layer == 0:
        coreq4 = nc.dram_tensor("coreq4", [1, 1], F32,
                                kind="ExternalInput").ap()
        wo_ap = nc.dram_tensor("Wo", [D, D], F32, kind="ExternalInput").ap()
        gate_w = nc.dram_tensor("gate_w", [D, E0], F32,
                                kind="ExternalInput").ap()
        w_gate = nc.dram_tensor("w_gate", [E0, D, F0], F32,
                                kind="ExternalInput").ap()
        w_up = nc.dram_tensor("w_up", [E0, D, F0], F32,
                              kind="ExternalInput").ap()
        w_down = nc.dram_tensor("w_down", [E0, F0, D], F32,
                                kind="ExternalInput").ap()
    else:
        wo_ap = nc.dram_tensor("oc", [D, D], F32, kind="ExternalInput").ap()
        osc = nc.dram_tensor("osc", [64, D], F32, kind="ExternalInput").ap()
        bo = nc.dram_tensor("bo", [128, DC], F32, kind="ExternalInput").ap()
        gwr = nc.dram_tensor("gwr", [D, E1], F32, kind="ExternalInput").ap()
        bcorr = nc.dram_tensor("bcorr", [1, E1], F32,
                               kind="ExternalInput").ap()
        gwT = nc.dram_tensor("gwT", [E1, D, F1], F32,
                             kind="ExternalInput").ap()
        uwT = nc.dram_tensor("uwT", [E1, D, F1], F32,
                             kind="ExternalInput").ap()
        dwT = nc.dram_tensor("dwT", [E1, F1, D], F32,
                             kind="ExternalInput").ap()
        shgT = nc.dram_tensor("shgT", [D, FSH], F32, kind="ExternalInput").ap()
        shuT = nc.dram_tensor("shuT", [D, FSH], F32, kind="ExternalInput").ap()
        shdT = nc.dram_tensor("shdT", [FSH, D], F32, kind="ExternalInput").ap()
    xout = nc.dram_tensor("xout", [TC, D], F32, kind="ExternalOutput").ap()
    dbg = None
    if debug_stop == "ctx":
        dbg = nc.dram_tensor("dbg", [D, TC], BF16, kind="ExternalOutput").ap()
    elif debug_stop in ("z", "h", "y0"):
        dbg = nc.dram_tensor("dbg", [TC, D], F32, kind="ExternalOutput").ap()
    elif debug_stop == "cw":
        nexp0 = E0 if layer == 0 else E1
        ncpt0 = 2 if layer == 0 else 4
        dbg = nc.dram_tensor("dbg", [128, ncpt0, nexp0], F32,
                             kind="ExternalOutput").ap()

    C = {}
    C["PS_S"] = P.ps("PS_S", [128, 2560], F32)
    C["P_O"] = P.ps("P_O", [128, 512], F32)
    C["P_AUX"] = P.ps("P_AUX", [128, 512], F32)
    C["PT"] = P.ps("PT", [128, 1024], BF16)
    C.update(setup_consts(P, need_f32_ident=True,
                          padflag_src=(coreq4 if layer == 0 else None)))
    if layer == 1:
        sels = setup_sels(P)

    C["ln_zc"] = P.sb("ln_zc", [128, 1024], BF16)
    C["ln_mu"] = P.sb("ln_mu", [1, 512], F32)
    C["ln_var"] = P.sb("ln_var", [1, 512], F32)
    C["ln_rstd"] = P.sb("ln_rstd", [1, 512], F32)
    C["ln_tmp"] = P.sb("ln_tmp", [128, 512], F32)

    lacc = P.sb("lacc", [128, 2], F32)
    msml = P.sb("msml", [128, 8], F32)
    mb = P.sb("mb", [128, 1], F32)
    rrow = P.sb("rrow", [1, 256], F32)
    rb_sb = P.sb("rb_sb", [128, 256], F32)
    t128a = P.sb("t128a", [128, 128], F32)
    t128b = P.sb("t128b", [128, 128], F32)

    if layer == 0:
        maskd = P.sb("maskd", [128, 4 * 512], BF16)
        maske = P.sb("maske", [128, 4 * 512], BF16)
        for ph in range(4):
            phi = 128 * ph
            psi = 128 * ph + 1
            s("pool", lambda g, ph=ph: g.memset(
                maskd[:, ph * 512:(ph + 1) * 512], 0.0))
            s("pool", lambda g, ph=ph, phi=phi: g.affine_select(
                out=maskd[:, ph * 512:(ph + 1) * 512],
                in_=maskd[:, ph * 512:(ph + 1) * 512],
                compare_op=ALU.is_ge, fill=NEG, base=phi,
                pattern=[[-1, 512]], channel_multiplier=1))
            s("pool", lambda g, ph=ph: g.memset(
                maske[:, ph * 512:(ph + 1) * 512], 0.0))
            s("pool", lambda g, ph=ph, psi=psi: g.affine_select(
                out=maske[:, ph * 512:(ph + 1) * 512],
                in_=maske[:, ph * 512:(ph + 1) * 512],
                compare_op=ALU.is_ge, fill=NEG, base=-psi,
                pattern=[[1, 512]], channel_multiplier=-1))

    A = P.sb("Abuf", [128, DC, TC], F32)
    emit_load_stream_fm(P, C, xin, A, F32)

    ctx_es = ExitStack()
    ctx_sb = ctx_es.enter_context(nc.sbuf_tensor("ctx_sb", [128, NH, TC],
                                                 BF16))

    with ExitStack() as es:
        q_sb = es.enter_context(nc.sbuf_tensor("q_sb", [128, NH, TC], BF16))
        k_h = es.enter_context(nc.sbuf_tensor("k_h", [128, kspan], BF16))
        nkc_all = kspan // 128
        v_h = es.enter_context(nc.sbuf_tensor("v_h", [128, nkc_all, 128],
                                              BF16))
        rw = 2560 if layer == 0 else 2048
        p_sb = es.enter_context(nc.sbuf_tensor("p_sb", [128, rw], BF16))
        pt_sb = es.enter_context(nc.sbuf_tensor("pt_sb", [128, rw], BF16))

        s("sp", lambda sy: sy.dma_start(
            out=q_sb[:, :, :], in_=qin.rearrange("(h p) t -> p h t", p=128)),
            dma=True)

        for h in range(NH):
            s("sp", lambda sy, h=h: sy.dma_start(
                out=k_h[:, :], in_=kin[h * 128:(h + 1) * 128, :]), dma=True)
            s("sp", lambda sy, h=h: sy.dma_start(
                out=v_h[:, :, :],
                in_=vin[:, h * 128:(h + 1) * 128].rearrange(
                    "(n p) d -> p n d", p=128)), dma=True)
            for i in range(NTC):
                if layer == 0:
                    rounds = [list(range(i // 4, i // 4 + 5))]
                else:
                    rounds = [[0, 1, 2, 3], [4, 5, 6, 7]]
                nr = len(rounds)
                for r, chunks in enumerate(rounds):
                    nch = len(chunks)
                    for j, ch in enumerate(chunks):
                        s("pe", lambda t, h=h, i=i, j=j, ch=ch: t.matmul(
                            C["PS_S"][:, j * 512:(j + 1) * 512],
                            q_sb[:, h, i * 128:(i + 1) * 128],
                            k_h[:, ch * 512:(ch + 1) * 512],
                            start=True, stop=(layer != 0)))
                        if layer == 0:
                            s("pe", lambda t, j=j, ch=ch: t.matmul(
                                C["PS_S"][:, j * 512:(j + 1) * 512],
                                C["ones_row_bf"][:, :],
                                C["padflag"][:, ch * 512:(ch + 1) * 512],
                                start=False, stop=True))
                    if layer == 0:
                        ph = i % 4
                        s("dve", lambda v, ph=ph: v.tensor_tensor(
                            C["PS_S"][:, 0:512], C["PS_S"][:, 0:512],
                            maske[:, ph * 512:(ph + 1) * 512], ALU.add))
                        s("dve", lambda v, ph=ph: v.tensor_tensor(
                            C["PS_S"][:, 4 * 512:5 * 512],
                            C["PS_S"][:, 4 * 512:5 * 512],
                            maskd[:, ph * 512:(ph + 1) * 512], ALU.add))
                    s("dve", lambda v, nch=nch, r=r: v.tensor_reduce(
                        msml[:, r:r + 1], C["PS_S"][:, 0:nch * 512], AX.X,
                        ALU.max))
                    s("dve", lambda v, r=r: v.tensor_scalar(
                        mb[:, :], msml[:, r:r + 1], -TAUI, None, op0=ALU.mult))
                    s("act", lambda a, nch=nch: a.activation(
                        p_sb[:, 0:nch * 512], C["PS_S"][:, 0:nch * 512],
                        AF.Exp, bias=mb[:, :], scale=TAUI,
                        accum_out=lacc[:, 0:1]))
                    s("dve", lambda v, r=r: v.tensor_copy(
                        msml[:, 2 + r:3 + r], lacc[:, 0:1]))
                    nk128 = nch * 4
                    for kc in range(nk128):
                        s("pe", lambda t, kc=kc: t.transpose(
                            C["PT"][:, (kc % 8) * 128:(kc % 8) * 128 + 128],
                            p_sb[:, kc * 128:(kc + 1) * 128],
                            C["ident_bf"][:, :]))
                        if kc % 8 == 7 or kc == nk128 - 1:
                            lo = (kc // 8) * 8
                            s("dve", lambda v, kc=kc, lo=lo: v.tensor_copy(
                                pt_sb[:, lo * 128:(kc + 1) * 128],
                                C["PT"][:, 0:(kc - lo + 1) * 128]))
                    ch0 = chunks[0] * 4
                    for kc in range(nk128):
                        s("pe", lambda t, kc=kc, ch0=ch0, r=r, nk128=nk128:
                          t.matmul(
                            C["P_O"][:, r * 128:(r + 1) * 128],
                            v_h[:, ch0 + kc, :],
                            pt_sb[:, kc * 128:(kc + 1) * 128],
                            start=(kc == 0), stop=(kc == nk128 - 1)))
                if nr == 1:
                    s("dve", lambda v: v.reciprocal(mb[:, :], msml[:, 2:3]))
                    s("pe", lambda t: t.transpose(
                        C["PS_S"][0:1, 0:128], mb[:, :], C["ident_f"][:, :]))
                    s("dve", lambda v: v.tensor_copy(rrow[0:1, 0:128],
                                                     C["PS_S"][0:1, 0:128]))
                    s("pe", lambda t: t.matmul(
                        C["P_AUX"][:, 0:128], C["ones_row_f"][:, :],
                        rrow[0:1, 0:128], start=True, stop=True))
                    s("dve", lambda v: v.tensor_copy(rb_sb[:, 0:128],
                                                     C["P_AUX"][:, 0:128]))
                    s("dve", lambda v, h=h, i=i: v.tensor_tensor(
                        ctx_sb[:, h, i * 128:(i + 1) * 128],
                        C["P_O"][:, 0:128], rb_sb[:, 0:128], ALU.mult))
                else:
                    s("dve", lambda v: v.tensor_tensor(
                        mb[:, :], msml[:, 0:1], msml[:, 1:2], ALU.max))
                    s("dve", lambda v: v.tensor_scalar(
                        mb[:, :], mb[:, :], -TAUI, None, op0=ALU.mult))
                    for r in range(2):
                        s("act", lambda a, r=r: a.activation(
                            msml[:, 4 + r:5 + r], msml[:, r:r + 1], AF.Exp,
                            bias=mb[:, :], scale=TAUI))
                    s("dve", lambda v: v.tensor_tensor(
                        msml[:, 6:7], msml[:, 4:5], msml[:, 2:3], ALU.mult))
                    s("dve", lambda v: v.tensor_tensor(
                        msml[:, 7:8], msml[:, 5:6], msml[:, 3:4], ALU.mult))
                    s("dve", lambda v: v.tensor_tensor(
                        msml[:, 6:7], msml[:, 6:7], msml[:, 7:8], ALU.add))
                    s("dve", lambda v: v.reciprocal(msml[:, 6:7], msml[:, 6:7]))
                    for r in range(2):
                        s("dve", lambda v, r=r: v.tensor_tensor(
                            msml[:, 4 + r:5 + r], msml[:, 4 + r:5 + r],
                            msml[:, 6:7], ALU.mult))
                    for r in range(2):
                        s("pe", lambda t, r=r: t.transpose(
                            C["PS_S"][0:1, r * 128:(r + 1) * 128],
                            msml[:, 4 + r:5 + r], C["ident_f"][:, :]))
                    s("dve", lambda v: v.tensor_copy(rrow[0:1, 0:256],
                                                     C["PS_S"][0:1, 0:256]))
                    for r in range(2):
                        s("pe", lambda t, r=r: t.matmul(
                            C["P_AUX"][:, r * 128:(r + 1) * 128],
                            C["ones_row_f"][:, :],
                            rrow[0:1, r * 128:(r + 1) * 128],
                            start=True, stop=True))
                    s("dve", lambda v: v.tensor_copy(rb_sb[:, 0:256],
                                                     C["P_AUX"][:, 0:256]))
                    s("dve", lambda v: v.tensor_tensor(
                        t128a[:, :], C["P_O"][:, 0:128], rb_sb[:, 0:128],
                        ALU.mult))
                    s("dve", lambda v: v.tensor_tensor(
                        t128b[:, :], C["P_O"][:, 128:256], rb_sb[:, 128:256],
                        ALU.mult))
                    s("dve", lambda v, h=h, i=i: v.tensor_tensor(
                        ctx_sb[:, h, i * 128:(i + 1) * 128],
                        t128a[:, :], t128b[:, :], ALU.add))

    if debug_stop == "ctx":
        s("sp", lambda sy: sy.dma_start(
            out=dbg.rearrange("(h p) t -> p h t", p=128), in_=ctx_sb[:, :, :]),
            dma=True)
        ctx_es.close()
        return P.finish()

    with ExitStack() as es:
        wbh = es.enter_context(nc.sbuf_tensor("wbh", [128, 8 * D], BF16))
        zst = es.enter_context(nc.sbuf_tensor("zst", [128, DC, 512], F32))
        if layer == 1:
            sclb = es.enter_context(nc.sbuf_tensor("sclb", [64, D], BF16))
            biasb = es.enter_context(nc.sbuf_tensor("biasb", [128, DC], F32))
            s("pool", lambda g: g.dma_start(out=sclb[:, :], in_=osc), dma=True)
            s("sp", lambda sy: sy.dma_start(out=biasb[:, :], in_=bo), dma=True)
        for st in range(2):
            for half in range(2):
                s("pool", lambda g, half=half: g.dma_start(
                    out=wbh[:, :].rearrange("p (c f) -> p c f", f=D),
                    in_=wo_ap.rearrange("(c p) f -> p c f", p=128)[
                        :, half * 8:(half + 1) * 8, :]), dma=True)
                if layer == 1:
                    emit_dequant(P, C, sels, wbh, sclb,
                                 [(dk8 * D, half * 8 + dk8)
                                  for dk8 in range(8)])
                for m in range(DC):
                    for dk in range(8):
                        s("pe", lambda t, st=st, half=half, m=m, dk=dk:
                          t.matmul(
                            C["PS_S"][:, 0:512],
                            wbh[:, dk * D + m * 128:dk * D + (m + 1) * 128],
                            ctx_sb[:, half * 8 + dk, st * 512:(st + 1) * 512],
                            start=(dk == 0), stop=(dk == 7)))
                    if half == 0:
                        s("dve", lambda v, st=st, m=m: v.tensor_tensor(
                            zst[:, m, :], C["PS_S"][:, 0:512],
                            A[:, m, st * 512:(st + 1) * 512], ALU.add))
                    else:
                        s("dve", lambda v, m=m: v.tensor_tensor(
                            zst[:, m, :], C["PS_S"][:, 0:512],
                            zst[:, m, :], ALU.add))
            if layer == 1:
                for m in range(DC):
                    s("act", lambda a, m=m: a.activation(
                        zst[:, m, :], zst[:, m, :], AF.Identity,
                        bias=biasb[:, m:m + 1], scale=1.0))
            if debug_stop == "z":
                emit_store_tm(P, C, [zst[:, c, :] for c in range(DC)],
                              dbg, st * 512, 512, f"z{st}")
                continue

            def wr(c, tmp, rstd_ps, st=st):
                s("dve", lambda v, c=c, tmp=tmp, rstd_ps=rstd_ps:
                  v.tensor_tensor(A[:, c, st * 512:(st + 1) * 512], tmp,
                                  rstd_ps, ALU.mult))
            emit_ln(P, C, [zst[:, c, :] for c in range(DC)], wr, 512)

    ctx_es.close()
    if debug_stop == "z":
        return P.finish()
    if debug_stop == "h":
        for st in range(2):
            emit_store_tm(P, C, [A[:, c, st * 512:(st + 1) * 512]
                                 for c in range(DC)], dbg, st * 512, 512,
                          f"h{st}")
        return P.finish()

    if layer == 0:
        emit_ffn(P, C, A, xout, layer=0, router=(gate_w,),
                 weights=(w_gate, w_up, w_down), debug_stop=debug_stop,
                 dbg=dbg)
    else:
        emit_ffn(P, C, A, xout, layer=1, router=(gwr, bcorr),
                 weights=(gwT, uwT, dwT, shgT, shuT, shdT),
                 debug_stop=debug_stop, dbg=dbg)
    return P.finish()



def emit_ffn(P, C, A, xout, layer, router, weights, debug_stop=None,
             dbg=None, to_stream=False, tag=""):
    nc, s = P.nc, P.s
    TW = 256 if layer == 0 else 512
    NST = TC // TW
    NE = E0 if layer == 0 else E1
    FC = FC0 if layer == 0 else FC1
    FF = F0 if layer == 0 else F1
    nexp = NE if layer == 0 else NE + 2
    fcgroups = [list(range(g, min(g + 4, FC))) for g in range(0, FC, 4)]
    NWE = max(DC * FF, FC * D)

    ncpt = TW // 128
    with ExitStack() as es:
        gwb = es.enter_context(nc.sbuf_tensor("gwb" + tag, [128, DC, NE], BF16))
        cw = es.enter_context(nc.sbuf_tensor("cw" + tag, [128, ncpt, NE], F32))
        h_bf = es.enter_context(nc.sbuf_tensor("h_bf_st" + tag, [128, DC, TW], BF16))
        wflat = es.enter_context(nc.sbuf_tensor("wflat" + tag, [128, NWE], BF16))
        a_f = es.enter_context(nc.sbuf_tensor("a_f" + tag, [128, FC, TW], F32))
        a_bf = es.enter_context(nc.sbuf_tensor("a_bf" + tag, [128, FC, TW], BF16))
        y_sb = es.enter_context(nc.sbuf_tensor("y_sb" + tag, [128, DC, TW], F32))
        sm = es.enter_context(nc.sbuf_tensor("sm" + tag, [128, NE], F32))
        m1 = es.enter_context(nc.sbuf_tensor("m1" + tag, [128, 1], F32))
        m2 = es.enter_context(nc.sbuf_tensor("m2" + tag, [128, 1], F32))
        eq = es.enter_context(nc.sbuf_tensor("eq" + tag, [128, NE], F32))
        crow = es.enter_context(nc.sbuf_tensor("crow" + tag, [1, TW], F32))
        if layer == 1:
            sg = es.enter_context(nc.sbuf_tensor("sg" + tag, [128, NE], F32))
            gsb = es.enter_context(nc.sbuf_tensor("gsb" + tag, [128, 4], F32))
            bcb = es.enter_context(nc.sbuf_tensor("bcb" + tag, [1, NE], F32))
            bcbc = es.enter_context(nc.sbuf_tensor("bcbc" + tag, [128, NE], F32))

        gr_ap = router[0]
        s("pool", lambda g: g.dma_start(
            out=gwb[:, :, :], in_=gr_ap.rearrange("(c p) e -> p c e", p=128)),
            dma=True)
        if layer == 1:
            s("sp", lambda sy: sy.dma_start(out=bcb[:, :], in_=router[1]),
              dma=True)
            s("pe", lambda t: t.matmul(C["P_AUX"][:, 0:NE],
                                       C["ones_row_f"][:, :], bcb[:, :],
                                       start=True, stop=True))
            s("dve", lambda v: v.tensor_copy(bcbc[:, :], C["P_AUX"][:, 0:NE]))

        for st in range(NST):
            for c in range(DC):
                s("act", lambda a, c=c, st=st: a.activation(
                    h_bf[:, c, :], A[:, c, st * TW:(st + 1) * TW],
                    AF.Identity, bias=0.0, scale=1.0))
            for nn in range(ncpt):
                for dk in range(DC):
                    s("pe", lambda t, nn=nn, dk=dk: t.matmul(
                        C["P_O"][:, 0:NE], h_bf[:, dk, nn * 128:(nn + 1) * 128],
                        gwb[:, dk, :], start=(dk == 0), stop=(dk == DC - 1)))
                if layer == 0:
                    emit_router_l0(P, C, cw, nn, sm, m1, m2, eq)
                else:
                    emit_router_l1(P, C, cw, nn, sm, sg, gsb, bcbc, m1, m2, eq)
            if debug_stop == "cw":
                s("sp", lambda sy: sy.dma_start(
                    out=dbg, in_=cw[:, :, :]), dma=True)
                return

            s("pool", lambda g: g.memset(y_sb[:, :, :], 0.0))
            for e in range(nexp):
                shared = (layer == 1 and e >= NE)
                half_sh = e - NE
                if layer == 0:
                    gap, uap, dap = weights[0][e], weights[1][e], weights[2][e]
                elif not shared:
                    gap, uap, dap = weights[0][e], weights[1][e], weights[2][e]
                else:
                    shgT, shuT, shdT = weights[3], weights[4], weights[5]
                    gap = shgT[:, half_sh * F1:(half_sh + 1) * F1]
                    uap = shuT[:, half_sh * F1:(half_sh + 1) * F1]
                    dap = shdT[half_sh * F1:(half_sh + 1) * F1, :]
                for which, mat_ap in (("g", gap), ("u", uap)):
                    s("pool", lambda g, mat_ap=mat_ap: g.dma_start(
                        out=wflat[:, 0:DC * FF].rearrange(
                            "p (c f) -> p c f", f=FF),
                        in_=mat_ap.rearrange("(c p) f -> p c f", p=128)),
                        dma=True)
                    for grp in fcgroups:
                        for u, fc in enumerate(grp):
                            for dk in range(DC):
                                s("pe", lambda t, st=st, u=u, fc=fc, dk=dk:
                                  t.matmul(
                                    C["PS_S"][:, u * TW:(u + 1) * TW],
                                    wflat[:, dk * FF + fc * 128:
                                          dk * FF + (fc + 1) * 128],
                                    h_bf[:, dk, :],
                                    start=(dk == 0), stop=(dk == DC - 1)))
                        for u, fc in enumerate(grp):
                            if which == "g":
                                s("act", lambda a, u=u, fc=fc: a.activation(
                                    a_f[:, fc, :],
                                    C["PS_S"][:, u * TW:(u + 1) * TW],
                                    AF.Silu, bias=0.0, scale=1.0))
                            else:
                                s("dve", lambda v, u=u, fc=fc: v.tensor_tensor(
                                    a_f[:, fc, :], a_f[:, fc, :],
                                    C["PS_S"][:, u * TW:(u + 1) * TW],
                                    ALU.mult))
                if not shared:
                    for nn in range(ncpt):
                        s("pe", lambda t, nn=nn, e=e: t.transpose(
                            C["PS_S"][0:1, nn * 128:(nn + 1) * 128],
                            cw[:, nn, e:e + 1], C["ident_f"][:, :]))
                    s("dve", lambda v: v.tensor_copy(crow[0:1, 0:TW],
                                                     C["PS_S"][0:1, 0:TW]))
                    s("pe", lambda t: t.matmul(
                        C["P_O"][:, 0:TW], C["ones_row_f"][:, :],
                        crow[0:1, 0:TW], start=True, stop=True))
                    for fc in range(FC):
                        s("dve", lambda v, fc=fc: v.tensor_tensor(
                            a_bf[:, fc, :], a_f[:, fc, :], C["P_O"][:, 0:TW],
                            ALU.mult))
                else:
                    for fc in range(FC):
                        s("dve", lambda v, fc=fc: v.tensor_copy(
                            a_bf[:, fc, :], a_f[:, fc, :]))
                s("pool", lambda g, dap=dap: g.dma_start(
                    out=wflat[:, 0:FC * D].rearrange("p (c f) -> p c f", f=D),
                    in_=dap.rearrange("(c p) f -> p c f", p=128)), dma=True)
                for m in range(DC):
                    for fc in range(FC):
                        s("pe", lambda t, m=m, fc=fc: t.matmul(
                            C["P_AUX"][:, 0:TW],
                            wflat[:, fc * D + m * 128:fc * D + (m + 1) * 128],
                            a_bf[:, fc, :], start=(fc == 0),
                            stop=(fc == FC - 1)))
                    s("dve", lambda v, m=m: v.tensor_tensor(
                        y_sb[:, m, :], y_sb[:, m, :], C["P_AUX"][:, 0:TW],
                        ALU.add))
            if debug_stop == "y0":
                emit_store_tm(P, C, [y_sb[:, c, :] for c in range(DC)],
                              dbg, 0, TW, "y0")
                return
            for m in range(DC):
                s("dve", lambda v, m=m, st=st: v.tensor_tensor(
                    y_sb[:, m, :], y_sb[:, m, :],
                    A[:, m, st * TW:(st + 1) * TW], ALU.add))

            if to_stream:
                def wr2(c, tmp, rstd_ps, st=st):
                    s("dve", lambda v, c=c, tmp=tmp, rstd_ps=rstd_ps:
                      v.tensor_tensor(A[:, c, st * TW:(st + 1) * TW], tmp,
                                      rstd_ps, ALU.mult))
                emit_ln(P, C, [y_sb[:, c, :] for c in range(DC)], wr2, TW)
            else:
                def wr2(c, tmp, rstd_ps):
                    s("dve", lambda v, c=c, tmp=tmp, rstd_ps=rstd_ps:
                      v.tensor_tensor(y_sb[:, c, :], tmp, rstd_ps, ALU.mult))
                emit_ln(P, C, [y_sb[:, c, :] for c in range(DC)], wr2, TW)
                emit_store_tm(P, C, [y_sb[:, c, :] for c in range(DC)], xout,
                              st * TW, TW, f"o{tag}{st}")


def emit_router_l0(P, C, cw, nn, sm, m1, m2, eq):
    s = P.s
    s("dve", lambda v: v.tensor_reduce(m1[:, :], C["P_O"][:, 0:E0], AX.X,
                                       ALU.max))
    s("dve", lambda v: v.tensor_scalar(m2[:, :], m1[:, :], -1.0, None,
                                       op0=ALU.mult))
    s("act", lambda a: a.activation(sm[:, :], C["P_O"][:, 0:E0], AF.Exp,
                                    bias=m2[:, :], scale=1.0))
    s("dve", lambda v: v.tensor_reduce(m1[:, :], sm[:, :], AX.X, ALU.max))
    s("dve", lambda v: v.tensor_scalar(eq[:, :], sm[:, :], m1[:, :],
                                       float(NEG), op0=ALU.is_ge,
                                       op1=ALU.mult))
    s("dve", lambda v: v.tensor_tensor(eq[:, :], sm[:, :], eq[:, :], ALU.add))
    s("dve", lambda v: v.tensor_reduce(m2[:, :], eq[:, :], AX.X, ALU.max))
    s("dve", lambda v: v.tensor_scalar(eq[:, :], sm[:, :], m2[:, :], None,
                                       op0=ALU.is_ge))
    s("dve", lambda v: v.tensor_tensor(sm[:, :], sm[:, :], eq[:, :], ALU.mult))
    s("dve", lambda v: v.tensor_tensor(m1[:, :], m1[:, :], m2[:, :], ALU.add))
    s("dve", lambda v: v.reciprocal(m1[:, :], m1[:, :]))
    s("dve", lambda v, nn=nn: v.tensor_scalar(cw[:, nn, :], sm[:, :],
                                              m1[:, :], None, op0=ALU.mult))


def emit_router_l1(P, C, cw, nn, s4c, sg, gsb, bcbc, m1, m2, eq):
    s = P.s
    NGRP, EPG, TOPK, RSCALE = 4, 4, 4, 2.5
    s("act", lambda a: a.activation(sg[:, :], C["P_O"][:, 0:E1], AF.Sigmoid,
                                    bias=0.0, scale=1.0))
    s("dve", lambda v: v.tensor_tensor(s4c[:, :], sg[:, :], bcbc[:, :],
                                       ALU.add))
    for g4 in range(NGRP):
        seg = s4c[:, g4 * EPG:(g4 + 1) * EPG]
        s("dve", lambda v, seg=seg: v.tensor_reduce(m1[:, :], seg, AX.X,
                                                    ALU.max))
        s("dve", lambda v, seg=seg: v.tensor_scalar(
            eq[:, 0:EPG], seg, m1[:, :], float(NEG), op0=ALU.is_ge,
            op1=ALU.mult))
        s("dve", lambda v, seg=seg: v.tensor_tensor(eq[:, 0:EPG], seg,
                                                    eq[:, 0:EPG], ALU.add))
        s("dve", lambda v: v.tensor_reduce(m2[:, :], eq[:, 0:EPG], AX.X,
                                           ALU.max))
        s("dve", lambda v, g4=g4: v.tensor_tensor(gsb[:, g4:g4 + 1], m1[:, :],
                                                  m2[:, :], ALU.add))
    s("dve", lambda v: v.tensor_reduce(m1[:, :], gsb[:, :], AX.X, ALU.max))
    s("dve", lambda v: v.tensor_scalar(eq[:, 0:NGRP], gsb[:, :], m1[:, :],
                                       float(NEG), op0=ALU.is_ge,
                                       op1=ALU.mult))
    s("dve", lambda v: v.tensor_tensor(eq[:, 0:NGRP], gsb[:, :], eq[:, 0:NGRP],
                                       ALU.add))
    s("dve", lambda v: v.tensor_reduce(m2[:, :], eq[:, 0:NGRP], AX.X, ALU.max))
    s("dve", lambda v: v.tensor_scalar(gsb[:, :], gsb[:, :], m2[:, :], None,
                                       op0=ALU.is_ge))
    for rep in range(2):
        if rep == 1:
            s("dve", lambda v: v.tensor_tensor(s4c[:, :], sg[:, :], bcbc[:, :],
                                               ALU.add))
        for g4 in range(NGRP):
            s("dve", lambda v, g4=g4: v.tensor_scalar(
                m1[:, :], gsb[:, g4:g4 + 1], float(-NEG), float(NEG),
                op0=ALU.mult, op1=ALU.add))
            s("dve", lambda v, g4=g4: v.tensor_scalar(
                s4c[:, g4 * EPG:(g4 + 1) * EPG],
                s4c[:, g4 * EPG:(g4 + 1) * EPG], m1[:, :], None, op0=ALU.add))
        if rep == 0:
            for it in range(TOPK):
                s("dve", lambda v: v.tensor_reduce(m2[:, :], s4c[:, :], AX.X,
                                                   ALU.max))
                if it < TOPK - 1:
                    s("dve", lambda v: v.tensor_scalar(
                        eq[:, :], s4c[:, :], m2[:, :], float(NEG),
                        op0=ALU.is_ge, op1=ALU.mult))
                    s("dve", lambda v: v.tensor_tensor(s4c[:, :], s4c[:, :],
                                                       eq[:, :], ALU.add))
    s("dve", lambda v: v.tensor_scalar(eq[:, :], s4c[:, :], m2[:, :], None,
                                       op0=ALU.is_ge))
    s("dve", lambda v: v.tensor_tensor(eq[:, :], sg[:, :], eq[:, :], ALU.mult))
    s("dve", lambda v: v.tensor_reduce(m1[:, :], eq[:, :], AX.X, ALU.add))
    s("dve", lambda v: v.tensor_scalar(m1[:, :], m1[:, :], 1e-20, None,
                                       op0=ALU.add))
    s("dve", lambda v: v.reciprocal(m1[:, :], m1[:, :]))
    s("dve", lambda v: v.tensor_scalar(m1[:, :], m1[:, :], RSCALE, None,
                                       op0=ALU.mult))
    s("dve", lambda v, nn=nn: v.tensor_scalar(cw[:, nn, :], eq[:, :],
                                              m1[:, :], None, op0=ALU.mult))



def _to_pc(b):
    return np.ascontiguousarray(np.asarray(b, np.float32).reshape(DC, 128).T)


_prog_cache = {}


def _get_prog(kind, layer):
    key = (kind, layer)
    if key not in _prog_cache:
        _prog_cache[key] = build_A(layer) if kind == "A" else build_B(layer)
    return _prog_cache[key]


def _get_prog_dbg(kind, layer, stop):
    key = (kind, layer, stop)
    if key not in _prog_cache:
        _prog_cache[key] = build_B(layer, debug_stop=stop)
    return _prog_cache[key]


def _run(nc, in_maps):
    res = run_bass_kernel_spmd(nc, in_maps, core_ids=list(range(8)))
    return res.results


def _window_feeds(ko, vo):
    kfeed, vfeed = [], []
    for b in range(2):
        kb = np.concatenate(ko[4 * b:4 * b + 4], axis=1)
        vb = np.concatenate(vo[4 * b:4 * b + 4], axis=0)
        for q4 in range(4):
            wlo = q4 * TC - WIN
            kf = np.zeros((D, KSPAN), dtype=kb.dtype)
            vf = np.zeros((KSPAN, D), dtype=vb.dtype)
            lo = max(0, wlo)
            kf[:, lo - wlo:KSPAN] = kb[:, lo:wlo + KSPAN]
            vf[lo - wlo:KSPAN, :] = vb[lo:wlo + KSPAN, :]
            kfeed.append(np.ascontiguousarray(kf))
            vfeed.append(np.ascontiguousarray(vf))
    return kfeed, vfeed


USE_FUSED = True


def kernel(**inputs):
    if USE_FUSED:
        return kernel_fused(**inputs)
    return kernel_multi(**inputs)


def kernel_multi(**inputs):
    f32 = np.float32
    x = np.asarray(inputs["L0_attn_x"], f32)
    streams = [np.ascontiguousarray(x[c // 4, (c % 4) * TC:(c % 4 + 1) * TC, :])
               for c in range(8)]

    ncA0 = _get_prog("A", 0)
    wq = np.asarray(inputs["L0_attn_Wq"], f32)
    wk = np.asarray(inputs["L0_attn_Wk"], f32)
    wv = np.asarray(inputs["L0_attn_Wv"], f32)
    rA0 = _run(ncA0, [{"xin": streams[c], "Wq": wq, "Wk": wk, "Wv": wv}
                      for c in range(8)])
    qo = [np.asarray(r["qo"]) for r in rA0]
    ko = [np.asarray(r["ko"]) for r in rA0]
    vo = [np.asarray(r["vo"]) for r in rA0]
    kfeed, vfeed = _window_feeds(ko, vo)

    ncB0 = _get_prog("B", 0)
    rB0 = _run(ncB0, [{
        "xin": streams[c], "qin": qo[c], "kin": kfeed[c], "vin": vfeed[c],
        "coreq4": np.full((1, 1), float(c % 4), f32),
        "Wo": np.asarray(inputs["L0_attn_Wo"], f32),
        "gate_w": np.asarray(inputs["L0_ffn_gate_w"], f32),
        "w_gate": np.asarray(inputs["L0_ffn_w_gate"], f32),
        "w_up": np.asarray(inputs["L0_ffn_w_up"], f32),
        "w_down": np.asarray(inputs["L0_ffn_w_down"], f32),
    } for c in range(8)])
    streams = [np.asarray(r["xout"]) for r in rB0]

    ncA1 = _get_prog("A", 1)
    rA1 = _run(ncA1, [{
        "xin": streams[c],
        "cq": np.asarray(inputs["L1_attn_wq_codes"], f32),
        "sq": np.asarray(inputs["L1_attn_wq_scales"], f32),
        "bq": _to_pc(inputs["L1_attn_bq"]),
        "ck": np.asarray(inputs["L1_attn_wk_codes"], f32),
        "sk": np.asarray(inputs["L1_attn_wk_scales"], f32),
        "bk": _to_pc(inputs["L1_attn_bk"]),
        "cv": np.asarray(inputs["L1_attn_wv_codes"], f32),
        "sv": np.asarray(inputs["L1_attn_wv_scales"], f32),
        "bv": _to_pc(inputs["L1_attn_bv"]),
    } for c in range(8)])
    qo = [np.asarray(r["qo"]) for r in rA1]
    ko = [np.asarray(r["ko"]) for r in rA1]
    vo = [np.asarray(r["vo"]) for r in rA1]
    kfull = [np.ascontiguousarray(np.concatenate(ko[4 * b:4 * b + 4], axis=1))
             for b in range(2)]
    vfull = [np.ascontiguousarray(np.concatenate(vo[4 * b:4 * b + 4], axis=0))
             for b in range(2)]

    ncB1 = _get_prog("B", 1)
    gwT = np.ascontiguousarray(
        np.asarray(inputs["L1_ffn_gate_w"], f32).transpose(0, 2, 1))
    uwT = np.ascontiguousarray(
        np.asarray(inputs["L1_ffn_up_w"], f32).transpose(0, 2, 1))
    dwT = np.ascontiguousarray(
        np.asarray(inputs["L1_ffn_down_w"], f32).transpose(0, 2, 1))
    shgT = np.ascontiguousarray(np.asarray(inputs["L1_ffn_sh_gate_w"], f32).T)
    shuT = np.ascontiguousarray(np.asarray(inputs["L1_ffn_sh_up_w"], f32).T)
    shdT = np.ascontiguousarray(np.asarray(inputs["L1_ffn_sh_down_w"], f32).T)
    gwr = np.ascontiguousarray(np.asarray(inputs["L1_ffn_gate_weight"], f32).T)
    bcorr = np.ascontiguousarray(
        np.asarray(inputs["L1_ffn_e_score_correction_bias"], f32)[None, :])
    rB1 = _run(ncB1, [{
        "xin": streams[c], "qin": qo[c], "kin": kfull[c // 4],
        "vin": vfull[c // 4],
        "oc": np.asarray(inputs["L1_attn_wo_codes"], f32),
        "osc": np.asarray(inputs["L1_attn_wo_scales"], f32),
        "bo": _to_pc(inputs["L1_attn_bo"]),
        "gwr": gwr, "bcorr": bcorr, "gwT": gwT, "uwT": uwT, "dwT": dwT,
        "shgT": shgT, "shuT": shuT, "shdT": shdT,
    } for c in range(8)])
    outs = [np.asarray(r["xout"]) for r in rB1]
    out = np.stack([np.concatenate(outs[0:4], 0), np.concatenate(outs[4:8], 0)])
    return out.astype(np.float32)



def build_fused(dbgstop=None):
    P = Prog()
    nc = P.nc
    s = P.s
    xin = nc.dram_tensor("xin", [TC, D], F32, kind="ExternalInput").ap()
    W0 = {}
    for n_ in ("Wq", "Wk", "Wv", "Wo"):
        W0[n_] = nc.dram_tensor(n_, [D, D], F32, kind="ExternalInput").ap()
    W0["gate_w"] = nc.dram_tensor("gate_w", [D, E0], F32,
                                  kind="ExternalInput").ap()
    W0["w_gate"] = nc.dram_tensor("w_gate", [E0, D, F0], F32,
                                  kind="ExternalInput").ap()
    W0["w_up"] = nc.dram_tensor("w_up", [E0, D, F0], F32,
                                kind="ExternalInput").ap()
    W0["w_down"] = nc.dram_tensor("w_down", [E0, F0, D], F32,
                                  kind="ExternalInput").ap()
    W1 = {}
    for n_ in ("q", "k", "v", "o"):
        W1["c" + n_] = nc.dram_tensor("c" + n_, [D, D], F32,
                                      kind="ExternalInput").ap()
        W1["s" + n_] = nc.dram_tensor("s" + n_, [64, D], F32,
                                      kind="ExternalInput").ap()
        W1["b" + n_] = nc.dram_tensor("b" + n_, [128, DC], F32,
                                      kind="ExternalInput").ap()
    W1["gwr"] = nc.dram_tensor("gwr", [D, E1], F32, kind="ExternalInput").ap()
    W1["bcorr"] = nc.dram_tensor("bcorr", [1, E1], F32,
                                 kind="ExternalInput").ap()
    W1["gwT"] = nc.dram_tensor("gwT", [E1, D, F1], F32,
                               kind="ExternalInput").ap()
    W1["uwT"] = nc.dram_tensor("uwT", [E1, D, F1], F32,
                               kind="ExternalInput").ap()
    W1["dwT"] = nc.dram_tensor("dwT", [E1, F1, D], F32,
                               kind="ExternalInput").ap()
    W1["shgT"] = nc.dram_tensor("shgT", [D, FSH], F32,
                                kind="ExternalInput").ap()
    W1["shuT"] = nc.dram_tensor("shuT", [D, FSH], F32,
                                kind="ExternalInput").ap()
    W1["shdT"] = nc.dram_tensor("shdT", [FSH, D], F32,
                                kind="ExternalInput").ap()
    xout = nc.dram_tensor("xout", [TC, D], F32, kind="ExternalOutput").ap()

    a2a = {}
    for n_ in ("aq_i", "aq_o", "ak_i", "ak_o", "av_i", "av_o", "ac_i",
               "ac_o"):
        a2a[n_] = nc.dram_tensor(n_, [2048, TC], BF16)

    C = {}
    C["PS_S"] = P.ps("PS_S", [128, 2560], F32)
    C["P_O"] = P.ps("P_O", [128, 512], F32)
    C["P_AUX"] = P.ps("P_AUX", [128, 512], F32)
    C["PT"] = P.ps("PT", [128, 1024], BF16)
    C.update(setup_consts(P, need_f32_ident=True))
    sels = setup_sels(P)

    C["ln_zc"] = P.sb("ln_zc", [128, 1024], BF16)
    C["ln_mu"] = P.sb("ln_mu", [1, 512], F32)
    C["ln_var"] = P.sb("ln_var", [1, 512], F32)
    C["ln_rstd"] = P.sb("ln_rstd", [1, 512], F32)
    C["ln_tmp"] = P.sb("ln_tmp", [128, 512], F32)

    A = P.sb("Abuf", [128, DC, TC], F32)
    emit_load_stream_fm(P, C, xin, A, F32)

    _emit_fused_layer(P, C, A, sels, 0, W0, a2a, None)
    _emit_fused_layer(P, C, A, sels, 1, W1, a2a, xout)
    return P.finish()


def _emit_fused_layer(P, C, A, sels, layer, W, a2a, xout):
    nc = P.nc
    s = P.s
    tag = "L%d" % layer
    from concourse.aot_env import allow_ici_all_to_all

    with ExitStack() as es:
        xbf = es.enter_context(nc.sbuf_tensor("xbf" + tag,
                                              [128, DC, TC], BF16))
        wb = es.enter_context(nc.sbuf_tensor("wbq" + tag,
                                             [128, DC * D], BF16))
        pbuf = es.enter_context(nc.sbuf_tensor("pbufq" + tag,
                                               [128, 512], BF16))
        if layer == 1:
            sclb = es.enter_context(nc.sbuf_tensor("sclbq" + tag,
                                                   [64, D], BF16))
            biasb = es.enter_context(nc.sbuf_tensor("biasbq" + tag,
                                                    [128, DC], F32))
        s("dve", lambda v: v.tensor_copy(xbf[:, :, :], A[:, :, :]))
        if layer == 0:
            plist = [(W["Wq"], None, None, a2a["aq_i"]),
                     (W["Wk"], None, None, a2a["ak_i"]),
                     (W["Wv"], None, None, a2a["av_i"])]
        else:
            plist = [(W["cq"], W["sq"], W["bq"], a2a["aq_i"]),
                     (W["ck"], W["sk"], W["bk"], a2a["ak_i"]),
                     (W["cv"], W["sv"], W["bv"], a2a["av_i"])]
        for wap, sap, bap, a2a_in in plist:
            s("pool", lambda g, wap=wap: g.dma_start(
                out=wb[:, :].rearrange("p (c f) -> p c f", f=D),
                in_=wap.rearrange("(c p) f -> p c f", p=128)), dma=True)
            if layer == 1:
                s("pool", lambda g, sap=sap: g.dma_start(
                    out=sclb[:, :], in_=sap), dma=True)
                s("sp", lambda sy, bap=bap: sy.dma_start(
                    out=biasb[:, :], in_=bap), dma=True)
                emit_dequant(P, C, sels, wb, sclb,
                             [(dk * D, dk) for dk in range(DC)])
            for m in range(DC):
                for st in range(2):
                    for dk in range(DC):
                        s("pe", lambda t, m=m, st=st, dk=dk: t.matmul(
                            C["PS_S"][:, 0:512],
                            wb[:, dk * D + m * 128:dk * D + (m + 1) * 128],
                            xbf[:, dk, st * 512:(st + 1) * 512],
                            start=(dk == 0), stop=(dk == DC - 1)))
                    if layer == 1:
                        s("act", lambda a, m=m: a.activation(
                            pbuf[:, :], C["PS_S"][:, 0:512], AF.Identity,
                            bias=biasb[:, m:m + 1], scale=1.0))
                    else:
                        s("act", lambda a: a.activation(
                            pbuf[:, :], C["PS_S"][:, 0:512], AF.Identity,
                            bias=0.0, scale=1.0))
                    row = (m % 8) * 256 + (m // 8) * 128
                    s("sp", lambda sy, row=row, st=st, a2a_in=a2a_in:
                      sy.dma_start(
                        out=a2a_in.ap()[row:row + 128,
                                        st * 512:(st + 1) * 512],
                        in_=pbuf[:, :]), dma=True)

    for i_t, o_t in ((a2a["aq_i"], a2a["aq_o"]), (a2a["ak_i"], a2a["ak_o"]),
                     (a2a["av_i"], a2a["av_o"])):
        def cc_fn(g, i_t=i_t, o_t=o_t):
            with allow_ici_all_to_all():
                return g.collective_compute(
                    "AllToAll", ALU.bypass,
                    replica_groups=[[0, 1, 2, 3, 4, 5, 6, 7]],
                    ins=[i_t.ap()], outs=[o_t.ap()])
        s("pool", cc_fn)

    ctx_es0 = ExitStack()
    ctxall = ctx_es0.enter_context(
        nc.sbuf_tensor("ctxall" + tag, [128, 4, SEQ], BF16))
    with ExitStack() as es:
        lacc = es.enter_context(nc.sbuf_tensor("lacc" + tag, [128, 2], F32))
        msml = es.enter_context(nc.sbuf_tensor("msml" + tag, [128, 8], F32))
        mb = es.enter_context(nc.sbuf_tensor("mb" + tag, [128, 1], F32))
        rrow = es.enter_context(nc.sbuf_tensor("rrow" + tag, [1, 256], F32))
        rb_sb = es.enter_context(nc.sbuf_tensor("rb_sb" + tag,
                                                [128, 256], F32))
        t128a = es.enter_context(nc.sbuf_tensor("t128a" + tag,
                                                [128, 128], F32))
        t128b = es.enter_context(nc.sbuf_tensor("t128b" + tag,
                                                [128, 128], F32))
        maskd = maske = None
        if layer == 0:
            maskd = es.enter_context(nc.sbuf_tensor("maskd" + tag,
                                                    [128, 4 * 512], BF16))
            maske = es.enter_context(nc.sbuf_tensor("maske" + tag,
                                                    [128, 4 * 512], BF16))
            for ph in range(4):
                s("pool", lambda g, ph=ph: g.memset(
                    maskd[:, ph * 512:(ph + 1) * 512], 0.0))
                s("pool", lambda g, ph=ph: g.affine_select(
                    out=maskd[:, ph * 512:(ph + 1) * 512],
                    in_=maskd[:, ph * 512:(ph + 1) * 512],
                    compare_op=ALU.is_ge, fill=NEG, base=128 * ph,
                    pattern=[[-1, 512]], channel_multiplier=1))
                s("pool", lambda g, ph=ph: g.memset(
                    maske[:, ph * 512:(ph + 1) * 512], 0.0))
                s("pool", lambda g, ph=ph: g.affine_select(
                    out=maske[:, ph * 512:(ph + 1) * 512],
                    in_=maske[:, ph * 512:(ph + 1) * 512],
                    compare_op=ALU.is_ge, fill=NEG, base=-(128 * ph + 1),
                    pattern=[[1, 512]], channel_multiplier=-1))
        q_h = es.enter_context(nc.sbuf_tensor("q_h" + tag, [128, SEQ], BF16))
        k_h = es.enter_context(nc.sbuf_tensor("k_h" + tag, [128, SEQ], BF16))
        v_f = es.enter_context(nc.sbuf_tensor("v_f" + tag, [128, SEQ], BF16))
        v_tm = es.enter_context(nc.sbuf_tensor("v_tm" + tag,
                                               [128, 32, 128], BF16))
        p_sb = es.enter_context(nc.sbuf_tensor("p_sb" + tag,
                                               [128, 2560], BF16))
        pt_sb = es.enter_context(nc.sbuf_tensor("pt_sb" + tag,
                                                [128, 2560], BF16))
        for b in range(2):
            for hh in range(2):
                _emit_attn_instance(P, C, layer, b, hh, a2a, q_h, k_h, v_f,
                                    v_tm, p_sb, pt_sb, lacc, msml, mb, rrow,
                                    rb_sb, t128a, t128b, maskd, maske, ctxall)
    for j in range(8):
        for hh in range(2):
            s("sp", lambda sy, j=j, hh=hh: sy.dma_start(
                out=a2a["ac_i"].ap()[j * 256 + hh * 128:
                                     j * 256 + hh * 128 + 128, :],
                in_=ctxall[:, (j // 4) * 2 + hh,
                           (j % 4) * TC:(j % 4 + 1) * TC]), dma=True)
    ctx_es0.close()

    def cc2_fn(g):
        with allow_ici_all_to_all():
            return g.collective_compute(
                "AllToAll", ALU.bypass,
                replica_groups=[[0, 1, 2, 3, 4, 5, 6, 7]],
                ins=[a2a["ac_i"].ap()], outs=[a2a["ac_o"].ap()])
    s("pool", cc2_fn)

    ctx_es = ExitStack()
    ctx_own = ctx_es.enter_context(
        nc.sbuf_tensor("ctxown" + tag, [128, DC, TC], BF16))
    for m in range(DC):
        row = 256 * m if m < 8 else 256 * (m - 8) + 128
        s("sp", lambda sy, m=m, row=row: sy.dma_start(
            out=ctx_own[:, m, :], in_=a2a["ac_o"].ap()[row:row + 128, :]),
            dma=True)
    with ExitStack() as es:
        wbh = es.enter_context(nc.sbuf_tensor("wbh" + tag,
                                              [128, 8 * D], BF16))
        zst = es.enter_context(nc.sbuf_tensor("zst" + tag,
                                              [128, DC, 512], F32))
        if layer == 1:
            sclb2 = es.enter_context(nc.sbuf_tensor("sclb2" + tag,
                                                    [64, D], BF16))
            biasb2 = es.enter_context(nc.sbuf_tensor("biasb2" + tag,
                                                     [128, DC], F32))
            s("pool", lambda g: g.dma_start(out=sclb2[:, :], in_=W["so"]),
              dma=True)
            s("sp", lambda sy: sy.dma_start(out=biasb2[:, :], in_=W["bo"]),
              dma=True)
        wo_ap = W["Wo"] if layer == 0 else W["co"]
        for st in range(2):
            for half in range(2):
                s("pool", lambda g, half=half: g.dma_start(
                    out=wbh[:, :].rearrange("p (c f) -> p c f", f=D),
                    in_=wo_ap.rearrange("(c p) f -> p c f", p=128)[
                        :, half * 8:(half + 1) * 8, :]), dma=True)
                if layer == 1:
                    emit_dequant(P, C, sels, wbh, sclb2,
                                 [(dk8 * D, half * 8 + dk8)
                                  for dk8 in range(8)])
                for m in range(DC):
                    for dk in range(8):
                        s("pe", lambda t, st=st, half=half, m=m, dk=dk:
                          t.matmul(
                            C["PS_S"][:, 0:512],
                            wbh[:, dk * D + m * 128:dk * D + (m + 1) * 128],
                            ctx_own[:, half * 8 + dk,
                                    st * 512:(st + 1) * 512],
                            start=(dk == 0), stop=(dk == 7)))
                    if half == 0:
                        s("dve", lambda v, st=st, m=m: v.tensor_tensor(
                            zst[:, m, :], C["PS_S"][:, 0:512],
                            A[:, m, st * 512:(st + 1) * 512], ALU.add))
                    else:
                        s("dve", lambda v, m=m: v.tensor_tensor(
                            zst[:, m, :], C["PS_S"][:, 0:512],
                            zst[:, m, :], ALU.add))
            if layer == 1:
                for m in range(DC):
                    s("act", lambda a, m=m: a.activation(
                        zst[:, m, :], zst[:, m, :], AF.Identity,
                        bias=biasb2[:, m:m + 1], scale=1.0))

            def wr(c, tmp, rstd_ps, st=st):
                s("dve", lambda v, c=c, tmp=tmp, rstd_ps=rstd_ps:
                  v.tensor_tensor(A[:, c, st * 512:(st + 1) * 512], tmp,
                                  rstd_ps, ALU.mult))
            emit_ln(P, C, [zst[:, c, :] for c in range(DC)], wr, 512)
    ctx_es.close()

    if layer == 0:
        emit_ffn(P, C, A, None, layer=0, router=(W["gate_w"],),
                 weights=(W["w_gate"], W["w_up"], W["w_down"]),
                 to_stream=True, tag=tag)
    else:
        emit_ffn(P, C, A, xout, layer=1, router=(W["gwr"], W["bcorr"]),
                 weights=(W["gwT"], W["uwT"], W["dwT"], W["shgT"], W["shuT"],
                          W["shdT"]), tag=tag)


def _emit_attn_instance(P, C, layer, b, hh, a2a, q_h, k_h, v_f, v_tm, p_sb,
                        pt_sb, lacc, msml, mb, rrow, rb_sb, t128a, t128b,
                        maskd, maske, ctxall):
    s = P.s
    inst = b * 2 + hh
    for r in range(4):
        row = 256 * (4 * b + r) + 128 * hh
        s("sp", lambda sy, r=r, row=row: sy.dma_start(
            out=q_h[:, r * TC:(r + 1) * TC],
            in_=a2a["aq_o"].ap()[row:row + 128, :]), dma=True)
        s("sp", lambda sy, r=r, row=row: sy.dma_start(
            out=k_h[:, r * TC:(r + 1) * TC],
            in_=a2a["ak_o"].ap()[row:row + 128, :]), dma=True)
        s("sp", lambda sy, r=r, row=row: sy.dma_start(
            out=v_f[:, r * TC:(r + 1) * TC],
            in_=a2a["av_o"].ap()[row:row + 128, :]), dma=True)
    for n in range(32):
        s("pe", lambda t, n=n: t.transpose(
            C["PT"][:, (n % 8) * 128:(n % 8) * 128 + 128],
            v_f[:, n * 128:(n + 1) * 128], C["ident_bf"][:, :]))
        if n % 8 == 7:
            s("dve", lambda v, n=n: v.tensor_copy(
                v_tm[:, n - 7:n + 1, :].rearrange("p a b -> p (a b)"),
                C["PT"][:, :]))
    for i in range(32):
        qs = 128 * i
        if layer == 0:
            lo = max(0, qs - 2047) // 512
            rounds = [list(range(lo, qs // 512 + 1))]
        else:
            rounds = [[0, 1, 2, 3], [4, 5, 6, 7]]
        nr = len(rounds)
        for r, chunks in enumerate(rounds):
            nch = len(chunks)
            for j, ch in enumerate(chunks):
                s("pe", lambda t, i=i, j=j, ch=ch: t.matmul(
                    C["PS_S"][:, j * 512:(j + 1) * 512],
                    q_h[:, i * 128:(i + 1) * 128],
                    k_h[:, ch * 512:(ch + 1) * 512],
                    start=True, stop=True))
            if layer == 0:
                ph = i % 4
                if qs >= 2048:
                    s("dve", lambda v, ph=ph: v.tensor_tensor(
                        C["PS_S"][:, 0:512], C["PS_S"][:, 0:512],
                        maske[:, ph * 512:(ph + 1) * 512], ALU.add))
                s("dve", lambda v, ph=ph, nch=nch: v.tensor_tensor(
                    C["PS_S"][:, (nch - 1) * 512:nch * 512],
                    C["PS_S"][:, (nch - 1) * 512:nch * 512],
                    maskd[:, ph * 512:(ph + 1) * 512], ALU.add))
            s("dve", lambda v, nch=nch, r=r: v.tensor_reduce(
                msml[:, r:r + 1], C["PS_S"][:, 0:nch * 512], AX.X, ALU.max))
            s("dve", lambda v, r=r: v.tensor_scalar(
                mb[:, :], msml[:, r:r + 1], -TAUI, None, op0=ALU.mult))
            s("act", lambda a, nch=nch: a.activation(
                p_sb[:, 0:nch * 512], C["PS_S"][:, 0:nch * 512], AF.Exp,
                bias=mb[:, :], scale=TAUI, accum_out=lacc[:, 0:1]))
            s("dve", lambda v, r=r: v.tensor_copy(
                msml[:, 2 + r:3 + r], lacc[:, 0:1]))
            nk128 = nch * 4
            for kc in range(nk128):
                s("pe", lambda t, kc=kc: t.transpose(
                    C["PT"][:, (kc % 8) * 128:(kc % 8) * 128 + 128],
                    p_sb[:, kc * 128:(kc + 1) * 128], C["ident_bf"][:, :]))
                if kc % 8 == 7 or kc == nk128 - 1:
                    lo8 = (kc // 8) * 8
                    s("dve", lambda v, kc=kc, lo8=lo8: v.tensor_copy(
                        pt_sb[:, lo8 * 128:(kc + 1) * 128],
                        C["PT"][:, 0:(kc - lo8 + 1) * 128]))
            ch0 = chunks[0] * 4
            for kc in range(nk128):
                s("pe", lambda t, kc=kc, ch0=ch0, r=r, nk128=nk128: t.matmul(
                    C["P_O"][:, r * 128:(r + 1) * 128],
                    v_tm[:, ch0 + kc, :],
                    pt_sb[:, kc * 128:(kc + 1) * 128],
                    start=(kc == 0), stop=(kc == nk128 - 1)))
        if nr == 1:
            s("dve", lambda v: v.reciprocal(mb[:, :], msml[:, 2:3]))
            s("pe", lambda t: t.transpose(
                C["PS_S"][0:1, 0:128], mb[:, :], C["ident_f"][:, :]))
            s("dve", lambda v: v.tensor_copy(rrow[0:1, 0:128],
                                             C["PS_S"][0:1, 0:128]))
            s("pe", lambda t: t.matmul(
                C["P_AUX"][:, 0:128], C["ones_row_f"][:, :],
                rrow[0:1, 0:128], start=True, stop=True))
            s("dve", lambda v: v.tensor_copy(rb_sb[:, 0:128],
                                             C["P_AUX"][:, 0:128]))
            s("dve", lambda v, inst=inst, i=i: v.tensor_tensor(
                ctxall[:, inst, i * 128:(i + 1) * 128],
                C["P_O"][:, 0:128], rb_sb[:, 0:128], ALU.mult))
        else:
            s("dve", lambda v: v.tensor_tensor(
                mb[:, :], msml[:, 0:1], msml[:, 1:2], ALU.max))
            s("dve", lambda v: v.tensor_scalar(
                mb[:, :], mb[:, :], -TAUI, None, op0=ALU.mult))
            for r in range(2):
                s("act", lambda a, r=r: a.activation(
                    msml[:, 4 + r:5 + r], msml[:, r:r + 1], AF.Exp,
                    bias=mb[:, :], scale=TAUI))
            s("dve", lambda v: v.tensor_tensor(
                msml[:, 6:7], msml[:, 4:5], msml[:, 2:3], ALU.mult))
            s("dve", lambda v: v.tensor_tensor(
                msml[:, 7:8], msml[:, 5:6], msml[:, 3:4], ALU.mult))
            s("dve", lambda v: v.tensor_tensor(
                msml[:, 6:7], msml[:, 6:7], msml[:, 7:8], ALU.add))
            s("dve", lambda v: v.reciprocal(msml[:, 6:7], msml[:, 6:7]))
            for r in range(2):
                s("dve", lambda v, r=r: v.tensor_tensor(
                    msml[:, 4 + r:5 + r], msml[:, 4 + r:5 + r],
                    msml[:, 6:7], ALU.mult))
            for r in range(2):
                s("pe", lambda t, r=r: t.transpose(
                    C["PS_S"][0:1, r * 128:(r + 1) * 128],
                    msml[:, 4 + r:5 + r], C["ident_f"][:, :]))
            s("dve", lambda v: v.tensor_copy(rrow[0:1, 0:256],
                                             C["PS_S"][0:1, 0:256]))
            for r in range(2):
                s("pe", lambda t, r=r: t.matmul(
                    C["P_AUX"][:, r * 128:(r + 1) * 128],
                    C["ones_row_f"][:, :], rrow[0:1, r * 128:(r + 1) * 128],
                    start=True, stop=True))
            s("dve", lambda v: v.tensor_copy(rb_sb[:, 0:256],
                                             C["P_AUX"][:, 0:256]))
            s("dve", lambda v: v.tensor_tensor(
                t128a[:, :], C["P_O"][:, 0:128], rb_sb[:, 0:128], ALU.mult))
            s("dve", lambda v: v.tensor_tensor(
                t128b[:, :], C["P_O"][:, 128:256], rb_sb[:, 128:256],
                ALU.mult))
            s("dve", lambda v, inst=inst, i=i: v.tensor_tensor(
                ctxall[:, inst, i * 128:(i + 1) * 128],
                t128a[:, :], t128b[:, :], ALU.add))


def _fused_in_map(inputs, c):
    f32 = np.float32
    x = np.asarray(inputs["L0_attn_x"], f32)
    m = {
        "xin": np.ascontiguousarray(
            x[c // 4, (c % 4) * TC:(c % 4 + 1) * TC, :]),
        "Wq": np.asarray(inputs["L0_attn_Wq"], f32),
        "Wk": np.asarray(inputs["L0_attn_Wk"], f32),
        "Wv": np.asarray(inputs["L0_attn_Wv"], f32),
        "Wo": np.asarray(inputs["L0_attn_Wo"], f32),
        "gate_w": np.asarray(inputs["L0_ffn_gate_w"], f32),
        "w_gate": np.asarray(inputs["L0_ffn_w_gate"], f32),
        "w_up": np.asarray(inputs["L0_ffn_w_up"], f32),
        "w_down": np.asarray(inputs["L0_ffn_w_down"], f32),
        "cq": np.asarray(inputs["L1_attn_wq_codes"], f32),
        "sq": np.asarray(inputs["L1_attn_wq_scales"], f32),
        "bq": _to_pc(inputs["L1_attn_bq"]),
        "ck": np.asarray(inputs["L1_attn_wk_codes"], f32),
        "sk": np.asarray(inputs["L1_attn_wk_scales"], f32),
        "bk": _to_pc(inputs["L1_attn_bk"]),
        "cv": np.asarray(inputs["L1_attn_wv_codes"], f32),
        "sv": np.asarray(inputs["L1_attn_wv_scales"], f32),
        "bv": _to_pc(inputs["L1_attn_bv"]),
        "co": np.asarray(inputs["L1_attn_wo_codes"], f32),
        "so": np.asarray(inputs["L1_attn_wo_scales"], f32),
        "bo": _to_pc(inputs["L1_attn_bo"]),
        "gwr": np.ascontiguousarray(
            np.asarray(inputs["L1_ffn_gate_weight"], f32).T),
        "bcorr": np.ascontiguousarray(
            np.asarray(inputs["L1_ffn_e_score_correction_bias"], f32)[None, :]),
        "gwT": np.ascontiguousarray(
            np.asarray(inputs["L1_ffn_gate_w"], f32).transpose(0, 2, 1)),
        "uwT": np.ascontiguousarray(
            np.asarray(inputs["L1_ffn_up_w"], f32).transpose(0, 2, 1)),
        "dwT": np.ascontiguousarray(
            np.asarray(inputs["L1_ffn_down_w"], f32).transpose(0, 2, 1)),
        "shgT": np.ascontiguousarray(
            np.asarray(inputs["L1_ffn_sh_gate_w"], f32).T),
        "shuT": np.ascontiguousarray(
            np.asarray(inputs["L1_ffn_sh_up_w"], f32).T),
        "shdT": np.ascontiguousarray(
            np.asarray(inputs["L1_ffn_sh_down_w"], f32).T),
    }
    return m


_fused_cache = {}


def kernel_fused(**inputs):
    if "nc" not in _fused_cache:
        _fused_cache["nc"] = build_fused()
    nc = _fused_cache["nc"]
    base = _fused_in_map(inputs, 0)
    in_maps = []
    for c in range(8):
        m = dict(base)
        m["xin"] = _fused_in_map(inputs, c)["xin"] if c else base["xin"]
        in_maps.append(m)
    res = run_bass_kernel_spmd(nc, in_maps, core_ids=list(range(8)))
    outs = [np.asarray(r["xout"]) for r in res.results]
    out = np.stack([np.concatenate(outs[0:4], 0), np.concatenate(outs[4:8], 0)])
    return out.astype(np.float32)


# version 04 (final)
# speedup vs baseline: 2.0148x; 1.6805x over previous
import math
import numpy as np
import ml_dtypes
import concourse.bass as bass
import concourse.mybir as mybir
from concourse.bass_utils import run_bass_kernel_spmd
from contextlib import ExitStack

F32 = mybir.dt.float32
BF16 = mybir.dt.bfloat16
AF = mybir.ActivationFunctionType
ALU = mybir.AluOpType
AX = mybir.AxisListType

D = 2048
DC = 16
NH = 16
TC = 1024
NTC = 8
SEQ = 4096
WIN = 2048
TAUI = 1.0 / math.sqrt(128.0)
NEG = -1e30

E0, F0, FC0 = 8, 1408, 11
E1, F1, FC1 = 16, 1024, 8
FSH = 2048

KSPAN = 3072

ENGNAMES = ("pe", "dve", "act", "pool", "sp")
FULL_SERIAL = True


class Prog:
    def __init__(self):
        self.nc = bass.Bass("TRN2", target_bir_lowering=False, debug=False)
        self.steps = []
        self.ctx = ExitStack()

    def s(self, eng, fn, dma=False):
        key = eng + ("_dma" if dma else "")
        inc = 16 if dma else 1
        self.steps.append((eng, fn, inc, key))

    def sb(self, name, shape, dtype):
        return self.ctx.enter_context(self.nc.sbuf_tensor(name, shape, dtype))

    def ps(self, name, shape, dtype):
        return self.ctx.enter_context(self.nc.psum_tensor(name, shape, dtype))

    def finish(self):
        nc = self.nc
        with ExitStack() as es:
            sems = {}
            for e in ENGNAMES:
                sems[e] = es.enter_context(nc.semaphore("sem_" + e))
                sems[e + "_dma"] = es.enter_context(nc.semaphore("semd_" + e))
            block = es.enter_context(nc.Block())

            steps = self.steps
            cums = []
            run = {}
            for (eng, fn, inc, key) in steps:
                cums.append(dict(run))
                run[key] = run.get(key, 0) + inc
            total = run

            def emit(engine_obj, name):
                for i, (eng, fn, inc, key) in enumerate(steps):
                    if eng != name:
                        continue
                    if i > 0:
                        _, _, _, pkey = steps[i - 1]
                        same_compute = (pkey == key == "pe")
                        if not same_compute:
                            engine_obj.wait_ge(sems[pkey], cums[i].get(pkey, 0))
                    fn(engine_obj).then_inc(sems[key], inc)
                if name == "sp":
                    for k, v in total.items():
                        engine_obj.wait_ge(sems[k], v)

            @block.gpsimd
            def _(g):
                emit(g, "pool")

            @block.vector
            def _(v):
                emit(v, "dve")

            @block.scalar
            def _(a):
                emit(a, "act")

            @block.tensor
            def _(t):
                emit(t, "pe")

            @block.sync
            def _(s):
                emit(s, "sp")

        self.ctx.close()
        return nc



def setup_consts(P, need_f32_ident=True, padflag_src=None):
    t = {}
    t["ident_bf"] = P.sb("ident_bf", [128, 128], BF16)
    t["ones_col_bf"] = P.sb("ones_col_bf", [128, 1], BF16)
    t["ones_row_f"] = P.sb("ones_row_f", [1, 128], F32)
    t["ones_row_bf"] = P.sb("ones_row_bf", [1, 128], BF16)
    if need_f32_ident:
        t["ident_f"] = P.sb("ident_f", [128, 128], F32)

    P.s("pool", lambda g: g.memset(t["ident_bf"][:, :], 0.0))
    P.s("pool", lambda g: g.affine_select(
        out=t["ident_bf"][:, :], in_=t["ident_bf"][:, :],
        compare_op=ALU.not_equal, fill=1.0, base=0,
        pattern=[[-1, 128]], channel_multiplier=1))
    if need_f32_ident:
        P.s("pool", lambda g: g.memset(t["ident_f"][:, :], 0.0))
        P.s("pool", lambda g: g.affine_select(
            out=t["ident_f"][:, :], in_=t["ident_f"][:, :],
            compare_op=ALU.not_equal, fill=1.0, base=0,
            pattern=[[-1, 128]], channel_multiplier=1))
    P.s("pool", lambda g: g.memset(t["ones_col_bf"][:, :], 1.0))
    P.s("pool", lambda g: g.memset(t["ones_row_f"][:, :], 1.0))
    P.s("pool", lambda g: g.memset(t["ones_row_bf"][:, :], 1.0))

    if padflag_src is not None:
        t["padflag"] = P.sb("padflag", [1, KSPAN], BF16)
        with ExitStack() as pes:
            wlo = pes.enter_context(P.nc.sbuf_tensor("wlo", [1, 1], F32))
            pf32 = pes.enter_context(
                P.nc.sbuf_tensor("pf32", [1, KSPAN], F32))
            P.s("sp", lambda s: s.dma_start(out=wlo[:, :], in_=padflag_src),
                dma=True)
            P.s("dve", lambda v: v.tensor_scalar(wlo[:, :], wlo[:, :], 1024.0,
                                                 -2048.0, op0=ALU.mult,
                                                 op1=ALU.add))
            P.s("pool", lambda g: g.iota(pf32[:, :], [[1, KSPAN]], base=0,
                                         channel_multiplier=0,
                                         allow_small_or_imprecise_dtypes=True))
            P.s("dve", lambda v: v.tensor_scalar(pf32[:, :], pf32[:, :],
                                                 wlo[:, :], None, op0=ALU.add))
            P.s("dve", lambda v: v.tensor_scalar(pf32[:, :], pf32[:, :], 0.0,
                                                 None, op0=ALU.is_lt))
            P.s("dve", lambda v: v.tensor_scalar(t["padflag"][:, :],
                                                 pf32[:, :], float(NEG), None,
                                                 op0=ALU.mult))
    return t


def setup_sels(P):
    sels = P.sb("sels", [64, DC * 128], BF16)
    with ExitStack() as ses:
        iob = ses.enter_context(
            P.nc.sbuf_tensor("sels_iob", [64, DC * 128], F32))
        pidx = ses.enter_context(P.nc.sbuf_tensor("sels_pidx", [64, 1], F32))
        P.s("pool", lambda g: g.iota(iob[:, :], [[1, 64], [0, 32]], base=0,
                                     channel_multiplier=0,
                                     allow_small_or_imprecise_dtypes=True))
        P.s("pool", lambda g: g.iota(pidx[:, :], [[0, 1]], base=0,
                                     channel_multiplier=1,
                                     allow_small_or_imprecise_dtypes=True))
        P.s("dve", lambda v: v.tensor_scalar(sels[:, :], iob[:, :],
                                             pidx[:, :], None,
                                             op0=ALU.is_equal))
    return sels


def emit_load_stream_fm(P, C, xin_ap, xfm, dtype):
    nc = P.nc
    with ExitStack() as es:
        xtm = es.enter_context(nc.sbuf_tensor("xtm_tmp", [128, NTC, D], F32))
        P.s("sp", lambda s: s.dma_start(
            out=xtm[:, :, :], in_=xin_ap.rearrange("(n p) d -> p n d", p=128)),
            dma=True)
        if dtype == BF16:
            xtmb = es.enter_context(
                nc.sbuf_tensor("xtmb_tmp", [128, NTC, D], BF16))
            P.s("dve", lambda v: v.tensor_copy(xtmb[:, :, :], xtm[:, :, :]))
            pt = C["PT"]
            for c in range(DC):
                for n in range(NTC):
                    P.s("pe", lambda t, c=c, n=n: t.transpose(
                        pt[:, n * 128:(n + 1) * 128],
                        xtmb[:, n, c * 128:(c + 1) * 128], C["ident_bf"][:, :]))
                P.s("dve", lambda v, c=c: v.tensor_copy(
                    xfm[:, c, :], pt[:, 0:1024]))
        else:
            psS = C["PS_S"]
            for c in range(DC):
                for n in range(NTC):
                    P.s("pe", lambda t, c=c, n=n: t.transpose(
                        psS[:, (n % 4) * 128:(n % 4) * 128 + 128],
                        xtm[:, n, c * 128:(c + 1) * 128], C["ident_f"][:, :]))
                    if n % 4 == 3:
                        P.s("dve", lambda v, c=c, n=n: v.tensor_copy(
                            xfm[:, c, (n - 3) * 128:(n + 1) * 128],
                            psS[:, 0:512]))


def emit_dequant(P, C, sels, codes_bf, scl_bf, dk_idx_pairs):
    PO = C["P_O"]
    for base, dk in dk_idx_pairs:
        for n4 in range(4):
            P.s("pe", lambda t, dk=dk, n4=n4: t.matmul(
                PO[:, 0:512], sels[:, dk * 128:(dk + 1) * 128],
                scl_bf[:, n4 * 512:(n4 + 1) * 512], start=True, stop=True))
            P.s("dve", lambda v, base=base, n4=n4: v.tensor_tensor(
                codes_bf[:, base + n4 * 512:base + (n4 + 1) * 512],
                codes_bf[:, base + n4 * 512:base + (n4 + 1) * 512],
                PO[:, 0:512], ALU.mult))


def emit_ln(P, C, zb_slices, out_writes, tw):
    s = P.s
    zc = C["ln_zc"]
    mu = C["ln_mu"]
    rstd = C["ln_rstd"]
    var = C["ln_var"]
    PS_S, P_O, P_AUX = C["PS_S"], C["P_O"], C["P_AUX"]
    for c in range(DC):
        s("act", lambda a, c=c: a.activation(zc[:, 0:tw], zb_slices[c],
                                             AF.Identity, bias=0.0, scale=1.0))
        s("act", lambda a, c=c: a.activation(zc[:, tw:2 * tw], zb_slices[c],
                                             AF.Square, bias=0.0, scale=1.0))
        s("pe", lambda t, c=c: t.matmul(PS_S[0:1, 0:tw],
                                        C["ones_col_bf"][:, :],
                                        zc[:, 0:tw], start=(c == 0),
                                        stop=(c == DC - 1)))
        s("pe", lambda t, c=c: t.matmul(PS_S[0:1, 512:512 + tw],
                                        C["ones_col_bf"][:, :],
                                        zc[:, tw:2 * tw], start=(c == 0),
                                        stop=(c == DC - 1)))
    s("dve", lambda v: v.tensor_scalar(mu[:, 0:tw], PS_S[0:1, 0:tw],
                                       1.0 / D, None, op0=ALU.mult))
    s("dve", lambda v: v.tensor_scalar(var[:, 0:tw], PS_S[0:1, 512:512 + tw],
                                       1.0 / D, None, op0=ALU.mult))
    s("act", lambda a: a.activation(rstd[:, 0:tw], mu[:, 0:tw], AF.Square,
                                    bias=0.0, scale=1.0))
    s("dve", lambda v: v.tensor_tensor(var[:, 0:tw], var[:, 0:tw],
                                       rstd[:, 0:tw], ALU.subtract))
    s("dve", lambda v: v.tensor_scalar(var[:, 0:tw], var[:, 0:tw], 1e-5, None,
                                       op0=ALU.add))
    s("act", lambda a: a.activation(rstd[:, 0:tw], var[:, 0:tw], AF.Sqrt,
                                    bias=0.0, scale=1.0))
    s("dve", lambda v: v.reciprocal(rstd[:, 0:tw], rstd[:, 0:tw]))
    s("pe", lambda t: t.matmul(P_AUX[:, 0:tw], C["ones_row_f"][:, :],
                               mu[:, 0:tw], start=True, stop=True))
    s("pe", lambda t: t.matmul(P_O[:, 0:tw], C["ones_row_f"][:, :],
                               rstd[:, 0:tw], start=True, stop=True))
    tmp = C["ln_tmp"]
    for c in range(DC):
        s("dve", lambda v, c=c: v.tensor_tensor(tmp[:, 0:tw], zb_slices[c],
                                                P_AUX[:, 0:tw], ALU.subtract))
        out_writes(c, tmp[:, 0:tw], P_O[:, 0:tw])


def emit_store_tm(P, C, fm_slices, out_ap, row0, tw, tag):
    nc = P.nc
    ntile = tw // 128
    with ExitStack() as es:
        otm = es.enter_context(nc.sbuf_tensor("otm_" + tag, [128, D], F32))
        for n in range(ntile):
            for c in range(DC):
                P.s("pe", lambda t, c=c, n=n: t.transpose(
                    C["PS_S"][:, (c % 4) * 128:(c % 4) * 128 + 128],
                    fm_slices[c][:, n * 128:(n + 1) * 128], C["ident_f"][:, :]))
                P.s("dve", lambda v, c=c, n=n: v.tensor_copy(
                    otm[:, c * 128:(c + 1) * 128],
                    C["PS_S"][:, (c % 4) * 128:(c % 4) * 128 + 128]))
            P.s("sp", lambda sy, n=n: sy.dma_start(
                out=out_ap[row0 + n * 128:row0 + (n + 1) * 128, :],
                in_=otm[:, :]), dma=True)



def build_A(layer):
    P = Prog()
    nc = P.nc
    xin = nc.dram_tensor("xin", [TC, D], F32, kind="ExternalInput").ap()
    if layer == 0:
        wlist = [nc.dram_tensor(n, [D, D], F32, kind="ExternalInput").ap()
                 for n in ("Wq", "Wk", "Wv")]
        slist = blist = [None] * 3
    else:
        wlist, slist, blist = [], [], []
        for nm in ("q", "k", "v"):
            wlist.append(nc.dram_tensor("c" + nm, [D, D], F32,
                                        kind="ExternalInput").ap())
            slist.append(nc.dram_tensor("s" + nm, [64, D], F32,
                                        kind="ExternalInput").ap())
            blist.append(nc.dram_tensor("b" + nm, [128, DC], F32,
                                        kind="ExternalInput").ap())
    qo = nc.dram_tensor("qo", [D, TC], BF16, kind="ExternalOutput").ap()
    ko = nc.dram_tensor("ko", [D, TC], BF16, kind="ExternalOutput").ap()
    vo = nc.dram_tensor("vo", [TC, D], BF16, kind="ExternalOutput").ap()
    outs = [qo, ko, vo]

    C = {}
    C["PS_S"] = P.ps("PS_S", [128, 2560], F32)
    C["P_O"] = P.ps("P_O", [128, 512], F32)
    C["P_AUX"] = P.ps("P_AUX", [128, 512], F32)
    C["PT"] = P.ps("PT", [128, 1024], BF16)
    C.update(setup_consts(P, need_f32_ident=False))
    if layer == 1:
        sels = setup_sels(P)

    xfm = P.sb("xfm", [128, DC, TC], BF16)
    emit_load_stream_fm(P, C, xin, xfm, BF16)

    wb = P.sb("wb", [128, DC * D], BF16)
    proj = P.sb("proj", [128, DC, TC], BF16)
    vtmn = P.sb("vtmn", [128, D], BF16)
    if layer == 1:
        sclb = P.sb("sclb", [64, D], BF16)
        biasb = P.sb("biasb", [128, DC], F32)

    for pi in range(3):
        wap = wlist[pi]
        P.s("pool", lambda g, wap=wap: g.dma_start(
            out=wb[:, :].rearrange("p (c f) -> p c f", f=D),
            in_=wap.rearrange("(c p) f -> p c f", p=128)), dma=True)
        if layer == 1:
            sap, bap = slist[pi], blist[pi]
            P.s("pool", lambda g, sap=sap: g.dma_start(
                out=sclb[:, :], in_=sap), dma=True)
            P.s("sp", lambda s, bap=bap: s.dma_start(
                out=biasb[:, :], in_=bap), dma=True)
            emit_dequant(P, C, sels, wb, sclb,
                         [(dk * D, dk) for dk in range(DC)])
        for m in range(DC):
            for st in range(2):
                for dk in range(DC):
                    P.s("pe", lambda t, m=m, st=st, dk=dk: t.matmul(
                        C["PS_S"][:, 0:512],
                        wb[:, dk * D + m * 128:dk * D + (m + 1) * 128],
                        xfm[:, dk, st * 512:(st + 1) * 512],
                        start=(dk == 0), stop=(dk == DC - 1)))
                if layer == 1:
                    P.s("act", lambda a, m=m, st=st: a.activation(
                        proj[:, m, st * 512:(st + 1) * 512],
                        C["PS_S"][:, 0:512], AF.Identity,
                        bias=biasb[:, m:m + 1], scale=1.0))
                else:
                    P.s("act", lambda a, m=m, st=st: a.activation(
                        proj[:, m, st * 512:(st + 1) * 512],
                        C["PS_S"][:, 0:512], AF.Identity, bias=0.0, scale=1.0))
        if pi < 2:
            P.s("sp", lambda s, pi=pi: s.dma_start(
                out=outs[pi].rearrange("(m p) t -> p m t", p=128),
                in_=proj[:, :, :]), dma=True)
        else:
            for n in range(NTC):
                for c in range(DC):
                    P.s("pe", lambda t, n=n, c=c: t.transpose(
                        C["PT"][:, (c % 8) * 128:(c % 8) * 128 + 128],
                        proj[:, c, n * 128:(n + 1) * 128], C["ident_bf"][:, :]))
                    if c % 8 == 7:
                        P.s("dve", lambda v, n=n, c=c: v.tensor_copy(
                            vtmn[:, (c - 7) * 128:(c + 1) * 128], C["PT"][:, :]))
                P.s("sp", lambda s, n=n: s.dma_start(
                    out=vo[n * 128:(n + 1) * 128, :], in_=vtmn[:, :]), dma=True)

    return P.finish()



def build_B(layer, debug_stop=None):
    P = Prog()
    nc = P.nc
    s = P.s
    xin = nc.dram_tensor("xin", [TC, D], F32, kind="ExternalInput").ap()
    qin = nc.dram_tensor("qin", [D, TC], BF16, kind="ExternalInput").ap()
    kspan = KSPAN if layer == 0 else SEQ
    kin = nc.dram_tensor("kin", [D, kspan], BF16, kind="ExternalInput").ap()
    vin = nc.dram_tensor("vin", [kspan, D], BF16, kind="ExternalInput").ap()
    if layer == 0:
        coreq4 = nc.dram_tensor("coreq4", [1, 1], F32,
                                kind="ExternalInput").ap()
        wo_ap = nc.dram_tensor("Wo", [D, D], F32, kind="ExternalInput").ap()
        gate_w = nc.dram_tensor("gate_w", [D, E0], F32,
                                kind="ExternalInput").ap()
        w_gate = nc.dram_tensor("w_gate", [E0, D, F0], F32,
                                kind="ExternalInput").ap()
        w_up = nc.dram_tensor("w_up", [E0, D, F0], F32,
                              kind="ExternalInput").ap()
        w_down = nc.dram_tensor("w_down", [E0, F0, D], F32,
                                kind="ExternalInput").ap()
    else:
        wo_ap = nc.dram_tensor("oc", [D, D], F32, kind="ExternalInput").ap()
        osc = nc.dram_tensor("osc", [64, D], F32, kind="ExternalInput").ap()
        bo = nc.dram_tensor("bo", [128, DC], F32, kind="ExternalInput").ap()
        gwr = nc.dram_tensor("gwr", [D, E1], F32, kind="ExternalInput").ap()
        bcorr = nc.dram_tensor("bcorr", [1, E1], F32,
                               kind="ExternalInput").ap()
        gwT = nc.dram_tensor("gwT", [E1, D, F1], F32,
                             kind="ExternalInput").ap()
        uwT = nc.dram_tensor("uwT", [E1, D, F1], F32,
                             kind="ExternalInput").ap()
        dwT = nc.dram_tensor("dwT", [E1, F1, D], F32,
                             kind="ExternalInput").ap()
        shgT = nc.dram_tensor("shgT", [D, FSH], F32, kind="ExternalInput").ap()
        shuT = nc.dram_tensor("shuT", [D, FSH], F32, kind="ExternalInput").ap()
        shdT = nc.dram_tensor("shdT", [FSH, D], F32, kind="ExternalInput").ap()
    xout = nc.dram_tensor("xout", [TC, D], F32, kind="ExternalOutput").ap()
    dbg = None
    if debug_stop == "ctx":
        dbg = nc.dram_tensor("dbg", [D, TC], BF16, kind="ExternalOutput").ap()
    elif debug_stop in ("z", "h", "y0"):
        dbg = nc.dram_tensor("dbg", [TC, D], F32, kind="ExternalOutput").ap()
    elif debug_stop == "cw":
        nexp0 = E0 if layer == 0 else E1
        ncpt0 = 2 if layer == 0 else 4
        dbg = nc.dram_tensor("dbg", [128, ncpt0, nexp0], F32,
                             kind="ExternalOutput").ap()

    C = {}
    C["PS_S"] = P.ps("PS_S", [128, 2560], F32)
    C["P_O"] = P.ps("P_O", [128, 512], F32)
    C["P_AUX"] = P.ps("P_AUX", [128, 512], F32)
    C["PT"] = P.ps("PT", [128, 1024], BF16)
    C.update(setup_consts(P, need_f32_ident=True,
                          padflag_src=(coreq4 if layer == 0 else None)))
    if layer == 1:
        sels = setup_sels(P)

    C["ln_zc"] = P.sb("ln_zc", [128, 1024], BF16)
    C["ln_mu"] = P.sb("ln_mu", [1, 512], F32)
    C["ln_var"] = P.sb("ln_var", [1, 512], F32)
    C["ln_rstd"] = P.sb("ln_rstd", [1, 512], F32)
    C["ln_tmp"] = P.sb("ln_tmp", [128, 512], F32)

    lacc = P.sb("lacc", [128, 2], F32)
    msml = P.sb("msml", [128, 8], F32)
    mb = P.sb("mb", [128, 1], F32)
    rrow = P.sb("rrow", [1, 256], F32)
    rb_sb = P.sb("rb_sb", [128, 256], F32)
    t128a = P.sb("t128a", [128, 128], F32)
    t128b = P.sb("t128b", [128, 128], F32)

    if layer == 0:
        maskd = P.sb("maskd", [128, 4 * 512], BF16)
        maske = P.sb("maske", [128, 4 * 512], BF16)
        for ph in range(4):
            phi = 128 * ph
            psi = 128 * ph + 1
            s("pool", lambda g, ph=ph: g.memset(
                maskd[:, ph * 512:(ph + 1) * 512], 0.0))
            s("pool", lambda g, ph=ph, phi=phi: g.affine_select(
                out=maskd[:, ph * 512:(ph + 1) * 512],
                in_=maskd[:, ph * 512:(ph + 1) * 512],
                compare_op=ALU.is_ge, fill=NEG, base=phi,
                pattern=[[-1, 512]], channel_multiplier=1))
            s("pool", lambda g, ph=ph: g.memset(
                maske[:, ph * 512:(ph + 1) * 512], 0.0))
            s("pool", lambda g, ph=ph, psi=psi: g.affine_select(
                out=maske[:, ph * 512:(ph + 1) * 512],
                in_=maske[:, ph * 512:(ph + 1) * 512],
                compare_op=ALU.is_ge, fill=NEG, base=-psi,
                pattern=[[1, 512]], channel_multiplier=-1))

    A = P.sb("Abuf", [128, DC, TC], F32)
    emit_load_stream_fm(P, C, xin, A, F32)

    ctx_es = ExitStack()
    ctx_sb = ctx_es.enter_context(nc.sbuf_tensor("ctx_sb", [128, NH, TC],
                                                 BF16))

    with ExitStack() as es:
        q_sb = es.enter_context(nc.sbuf_tensor("q_sb", [128, NH, TC], BF16))
        k_h = es.enter_context(nc.sbuf_tensor("k_h", [128, kspan], BF16))
        nkc_all = kspan // 128
        v_h = es.enter_context(nc.sbuf_tensor("v_h", [128, nkc_all, 128],
                                              BF16))
        rw = 2560 if layer == 0 else 2048
        p_sb = es.enter_context(nc.sbuf_tensor("p_sb", [128, rw], BF16))
        pt_sb = es.enter_context(nc.sbuf_tensor("pt_sb", [128, rw], BF16))

        s("sp", lambda sy: sy.dma_start(
            out=q_sb[:, :, :], in_=qin.rearrange("(h p) t -> p h t", p=128)),
            dma=True)

        for h in range(NH):
            s("sp", lambda sy, h=h: sy.dma_start(
                out=k_h[:, :], in_=kin[h * 128:(h + 1) * 128, :]), dma=True)
            s("sp", lambda sy, h=h: sy.dma_start(
                out=v_h[:, :, :],
                in_=vin[:, h * 128:(h + 1) * 128].rearrange(
                    "(n p) d -> p n d", p=128)), dma=True)
            for i in range(NTC):
                if layer == 0:
                    rounds = [list(range(i // 4, i // 4 + 5))]
                else:
                    rounds = [[0, 1, 2, 3], [4, 5, 6, 7]]
                nr = len(rounds)
                for r, chunks in enumerate(rounds):
                    nch = len(chunks)
                    for j, ch in enumerate(chunks):
                        s("pe", lambda t, h=h, i=i, j=j, ch=ch: t.matmul(
                            C["PS_S"][:, j * 512:(j + 1) * 512],
                            q_sb[:, h, i * 128:(i + 1) * 128],
                            k_h[:, ch * 512:(ch + 1) * 512],
                            start=True, stop=(layer != 0)))
                        if layer == 0:
                            s("pe", lambda t, j=j, ch=ch: t.matmul(
                                C["PS_S"][:, j * 512:(j + 1) * 512],
                                C["ones_row_bf"][:, :],
                                C["padflag"][:, ch * 512:(ch + 1) * 512],
                                start=False, stop=True))
                    if layer == 0:
                        ph = i % 4
                        s("dve", lambda v, ph=ph: v.tensor_tensor(
                            C["PS_S"][:, 0:512], C["PS_S"][:, 0:512],
                            maske[:, ph * 512:(ph + 1) * 512], ALU.add))
                        s("dve", lambda v, ph=ph: v.tensor_tensor(
                            C["PS_S"][:, 4 * 512:5 * 512],
                            C["PS_S"][:, 4 * 512:5 * 512],
                            maskd[:, ph * 512:(ph + 1) * 512], ALU.add))
                    s("dve", lambda v, nch=nch, r=r: v.tensor_reduce(
                        msml[:, r:r + 1], C["PS_S"][:, 0:nch * 512], AX.X,
                        ALU.max))
                    s("dve", lambda v, r=r: v.tensor_scalar(
                        mb[:, :], msml[:, r:r + 1], -TAUI, None, op0=ALU.mult))
                    s("act", lambda a, nch=nch: a.activation(
                        p_sb[:, 0:nch * 512], C["PS_S"][:, 0:nch * 512],
                        AF.Exp, bias=mb[:, :], scale=TAUI,
                        accum_out=lacc[:, 0:1]))
                    s("dve", lambda v, r=r: v.tensor_copy(
                        msml[:, 2 + r:3 + r], lacc[:, 0:1]))
                    nk128 = nch * 4
                    for kc in range(nk128):
                        s("pe", lambda t, kc=kc: t.transpose(
                            C["PT"][:, (kc % 8) * 128:(kc % 8) * 128 + 128],
                            p_sb[:, kc * 128:(kc + 1) * 128],
                            C["ident_bf"][:, :]))
                        if kc % 8 == 7 or kc == nk128 - 1:
                            lo = (kc // 8) * 8
                            s("dve", lambda v, kc=kc, lo=lo: v.tensor_copy(
                                pt_sb[:, lo * 128:(kc + 1) * 128],
                                C["PT"][:, 0:(kc - lo + 1) * 128]))
                    ch0 = chunks[0] * 4
                    for kc in range(nk128):
                        s("pe", lambda t, kc=kc, ch0=ch0, r=r, nk128=nk128:
                          t.matmul(
                            C["P_O"][:, r * 128:(r + 1) * 128],
                            v_h[:, ch0 + kc, :],
                            pt_sb[:, kc * 128:(kc + 1) * 128],
                            start=(kc == 0), stop=(kc == nk128 - 1)))
                if nr == 1:
                    s("dve", lambda v: v.reciprocal(mb[:, :], msml[:, 2:3]))
                    s("pe", lambda t: t.transpose(
                        C["PS_S"][0:1, 0:128], mb[:, :], C["ident_f"][:, :]))
                    s("dve", lambda v: v.tensor_copy(rrow[0:1, 0:128],
                                                     C["PS_S"][0:1, 0:128]))
                    s("pe", lambda t: t.matmul(
                        C["P_AUX"][:, 0:128], C["ones_row_f"][:, :],
                        rrow[0:1, 0:128], start=True, stop=True))
                    s("dve", lambda v: v.tensor_copy(rb_sb[:, 0:128],
                                                     C["P_AUX"][:, 0:128]))
                    s("dve", lambda v, h=h, i=i: v.tensor_tensor(
                        ctx_sb[:, h, i * 128:(i + 1) * 128],
                        C["P_O"][:, 0:128], rb_sb[:, 0:128], ALU.mult))
                else:
                    s("dve", lambda v: v.tensor_tensor(
                        mb[:, :], msml[:, 0:1], msml[:, 1:2], ALU.max))
                    s("dve", lambda v: v.tensor_scalar(
                        mb[:, :], mb[:, :], -TAUI, None, op0=ALU.mult))
                    for r in range(2):
                        s("act", lambda a, r=r: a.activation(
                            msml[:, 4 + r:5 + r], msml[:, r:r + 1], AF.Exp,
                            bias=mb[:, :], scale=TAUI))
                    s("dve", lambda v: v.tensor_tensor(
                        msml[:, 6:7], msml[:, 4:5], msml[:, 2:3], ALU.mult))
                    s("dve", lambda v: v.tensor_tensor(
                        msml[:, 7:8], msml[:, 5:6], msml[:, 3:4], ALU.mult))
                    s("dve", lambda v: v.tensor_tensor(
                        msml[:, 6:7], msml[:, 6:7], msml[:, 7:8], ALU.add))
                    s("dve", lambda v: v.reciprocal(msml[:, 6:7], msml[:, 6:7]))
                    for r in range(2):
                        s("dve", lambda v, r=r: v.tensor_tensor(
                            msml[:, 4 + r:5 + r], msml[:, 4 + r:5 + r],
                            msml[:, 6:7], ALU.mult))
                    for r in range(2):
                        s("pe", lambda t, r=r: t.transpose(
                            C["PS_S"][0:1, r * 128:(r + 1) * 128],
                            msml[:, 4 + r:5 + r], C["ident_f"][:, :]))
                    s("dve", lambda v: v.tensor_copy(rrow[0:1, 0:256],
                                                     C["PS_S"][0:1, 0:256]))
                    for r in range(2):
                        s("pe", lambda t, r=r: t.matmul(
                            C["P_AUX"][:, r * 128:(r + 1) * 128],
                            C["ones_row_f"][:, :],
                            rrow[0:1, r * 128:(r + 1) * 128],
                            start=True, stop=True))
                    s("dve", lambda v: v.tensor_copy(rb_sb[:, 0:256],
                                                     C["P_AUX"][:, 0:256]))
                    s("dve", lambda v: v.tensor_tensor(
                        t128a[:, :], C["P_O"][:, 0:128], rb_sb[:, 0:128],
                        ALU.mult))
                    s("dve", lambda v: v.tensor_tensor(
                        t128b[:, :], C["P_O"][:, 128:256], rb_sb[:, 128:256],
                        ALU.mult))
                    s("dve", lambda v, h=h, i=i: v.tensor_tensor(
                        ctx_sb[:, h, i * 128:(i + 1) * 128],
                        t128a[:, :], t128b[:, :], ALU.add))

    if debug_stop == "ctx":
        s("sp", lambda sy: sy.dma_start(
            out=dbg.rearrange("(h p) t -> p h t", p=128), in_=ctx_sb[:, :, :]),
            dma=True)
        ctx_es.close()
        return P.finish()

    with ExitStack() as es:
        wbh = es.enter_context(nc.sbuf_tensor("wbh", [128, 8 * D], BF16))
        zst = es.enter_context(nc.sbuf_tensor("zst", [128, DC, 512], F32))
        if layer == 1:
            sclb = es.enter_context(nc.sbuf_tensor("sclb", [64, D], BF16))
            biasb = es.enter_context(nc.sbuf_tensor("biasb", [128, DC], F32))
            s("pool", lambda g: g.dma_start(out=sclb[:, :], in_=osc), dma=True)
            s("sp", lambda sy: sy.dma_start(out=biasb[:, :], in_=bo), dma=True)
        for st in range(2):
            for half in range(2):
                s("pool", lambda g, half=half: g.dma_start(
                    out=wbh[:, :].rearrange("p (c f) -> p c f", f=D),
                    in_=wo_ap.rearrange("(c p) f -> p c f", p=128)[
                        :, half * 8:(half + 1) * 8, :]), dma=True)
                if layer == 1:
                    emit_dequant(P, C, sels, wbh, sclb,
                                 [(dk8 * D, half * 8 + dk8)
                                  for dk8 in range(8)])
                for m in range(DC):
                    for dk in range(8):
                        s("pe", lambda t, st=st, half=half, m=m, dk=dk:
                          t.matmul(
                            C["PS_S"][:, 0:512],
                            wbh[:, dk * D + m * 128:dk * D + (m + 1) * 128],
                            ctx_sb[:, half * 8 + dk, st * 512:(st + 1) * 512],
                            start=(dk == 0), stop=(dk == 7)))
                    if half == 0:
                        s("dve", lambda v, st=st, m=m: v.tensor_tensor(
                            zst[:, m, :], C["PS_S"][:, 0:512],
                            A[:, m, st * 512:(st + 1) * 512], ALU.add))
                    else:
                        s("dve", lambda v, m=m: v.tensor_tensor(
                            zst[:, m, :], C["PS_S"][:, 0:512],
                            zst[:, m, :], ALU.add))
            if layer == 1:
                for m in range(DC):
                    s("act", lambda a, m=m: a.activation(
                        zst[:, m, :], zst[:, m, :], AF.Identity,
                        bias=biasb[:, m:m + 1], scale=1.0))
            if debug_stop == "z":
                emit_store_tm(P, C, [zst[:, c, :] for c in range(DC)],
                              dbg, st * 512, 512, f"z{st}")
                continue

            def wr(c, tmp, rstd_ps, st=st):
                s("dve", lambda v, c=c, tmp=tmp, rstd_ps=rstd_ps:
                  v.tensor_tensor(A[:, c, st * 512:(st + 1) * 512], tmp,
                                  rstd_ps, ALU.mult))
            emit_ln(P, C, [zst[:, c, :] for c in range(DC)], wr, 512)

    ctx_es.close()
    if debug_stop == "z":
        return P.finish()
    if debug_stop == "h":
        for st in range(2):
            emit_store_tm(P, C, [A[:, c, st * 512:(st + 1) * 512]
                                 for c in range(DC)], dbg, st * 512, 512,
                          f"h{st}")
        return P.finish()

    if layer == 0:
        emit_ffn(P, C, A, xout, layer=0, router=(gate_w,),
                 weights=(w_gate, w_up, w_down), debug_stop=debug_stop,
                 dbg=dbg)
    else:
        emit_ffn(P, C, A, xout, layer=1, router=(gwr, bcorr),
                 weights=(gwT, uwT, dwT, shgT, shuT, shdT),
                 debug_stop=debug_stop, dbg=dbg)
    return P.finish()



def emit_ffn(P, C, A, xout, layer, router, weights, debug_stop=None,
             dbg=None, to_stream=False, tag=""):
    nc, s = P.nc, P.s
    TW = 256 if layer == 0 else 512
    NST = TC // TW
    NE = E0 if layer == 0 else E1
    FC = FC0 if layer == 0 else FC1
    FF = F0 if layer == 0 else F1
    nexp = NE if layer == 0 else NE + 2
    fcgroups = [list(range(g, min(g + 4, FC))) for g in range(0, FC, 4)]
    NWE = max(DC * FF, FC * D)

    ncpt = TW // 128
    with ExitStack() as es:
        gwb = es.enter_context(nc.sbuf_tensor("gwb" + tag, [128, DC, NE], BF16))
        cw = es.enter_context(nc.sbuf_tensor("cw" + tag, [128, ncpt, NE], F32))
        h_bf = es.enter_context(nc.sbuf_tensor("h_bf_st" + tag, [128, DC, TW], BF16))
        wflat = es.enter_context(nc.sbuf_tensor("wflat" + tag, [128, NWE], BF16))
        a_f = es.enter_context(nc.sbuf_tensor("a_f" + tag, [128, FC, TW], F32))
        a_bf = es.enter_context(nc.sbuf_tensor("a_bf" + tag, [128, FC, TW], BF16))
        y_sb = es.enter_context(nc.sbuf_tensor("y_sb" + tag, [128, DC, TW], F32))
        sm = es.enter_context(nc.sbuf_tensor("sm" + tag, [128, NE], F32))
        m1 = es.enter_context(nc.sbuf_tensor("m1" + tag, [128, 1], F32))
        m2 = es.enter_context(nc.sbuf_tensor("m2" + tag, [128, 1], F32))
        eq = es.enter_context(nc.sbuf_tensor("eq" + tag, [128, NE], F32))
        crow = es.enter_context(nc.sbuf_tensor("crow" + tag, [1, TW], F32))
        if layer == 1:
            sg = es.enter_context(nc.sbuf_tensor("sg" + tag, [128, NE], F32))
            gsb = es.enter_context(nc.sbuf_tensor("gsb" + tag, [128, 4], F32))
            bcb = es.enter_context(nc.sbuf_tensor("bcb" + tag, [1, NE], F32))
            bcbc = es.enter_context(nc.sbuf_tensor("bcbc" + tag, [128, NE], F32))

        gr_ap = router[0]
        s("pool", lambda g: g.dma_start(
            out=gwb[:, :, :], in_=gr_ap.rearrange("(c p) e -> p c e", p=128)),
            dma=True)
        if layer == 1:
            s("sp", lambda sy: sy.dma_start(out=bcb[:, :], in_=router[1]),
              dma=True)
            s("pe", lambda t: t.matmul(C["P_AUX"][:, 0:NE],
                                       C["ones_row_f"][:, :], bcb[:, :],
                                       start=True, stop=True))
            s("dve", lambda v: v.tensor_copy(bcbc[:, :], C["P_AUX"][:, 0:NE]))

        for st in range(NST):
            for c in range(DC):
                s("act", lambda a, c=c, st=st: a.activation(
                    h_bf[:, c, :], A[:, c, st * TW:(st + 1) * TW],
                    AF.Identity, bias=0.0, scale=1.0))
            for nn in range(ncpt):
                for dk in range(DC):
                    s("pe", lambda t, nn=nn, dk=dk: t.matmul(
                        C["P_O"][:, 0:NE], h_bf[:, dk, nn * 128:(nn + 1) * 128],
                        gwb[:, dk, :], start=(dk == 0), stop=(dk == DC - 1)))
                if layer == 0:
                    emit_router_l0(P, C, cw, nn, sm, m1, m2, eq)
                else:
                    emit_router_l1(P, C, cw, nn, sm, sg, gsb, bcbc, m1, m2, eq)
            if debug_stop == "cw":
                s("sp", lambda sy: sy.dma_start(
                    out=dbg, in_=cw[:, :, :]), dma=True)
                return

            s("pool", lambda g: g.memset(y_sb[:, :, :], 0.0))
            for e in range(nexp):
                shared = (layer == 1 and e >= NE)
                half_sh = e - NE
                if layer == 0:
                    gap, uap, dap = weights[0][e], weights[1][e], weights[2][e]
                elif not shared:
                    gap, uap, dap = weights[0][e], weights[1][e], weights[2][e]
                else:
                    shgT, shuT, shdT = weights[3], weights[4], weights[5]
                    gap = shgT[:, half_sh * F1:(half_sh + 1) * F1]
                    uap = shuT[:, half_sh * F1:(half_sh + 1) * F1]
                    dap = shdT[half_sh * F1:(half_sh + 1) * F1, :]
                for which, mat_ap in (("g", gap), ("u", uap)):
                    s("pool", lambda g, mat_ap=mat_ap: g.dma_start(
                        out=wflat[:, 0:DC * FF].rearrange(
                            "p (c f) -> p c f", f=FF),
                        in_=mat_ap.rearrange("(c p) f -> p c f", p=128)),
                        dma=True)
                    for grp in fcgroups:
                        for u, fc in enumerate(grp):
                            for dk in range(DC):
                                s("pe", lambda t, st=st, u=u, fc=fc, dk=dk:
                                  t.matmul(
                                    C["PS_S"][:, u * TW:(u + 1) * TW],
                                    wflat[:, dk * FF + fc * 128:
                                          dk * FF + (fc + 1) * 128],
                                    h_bf[:, dk, :],
                                    start=(dk == 0), stop=(dk == DC - 1)))
                        for u, fc in enumerate(grp):
                            if which == "g":
                                s("act", lambda a, u=u, fc=fc: a.activation(
                                    a_f[:, fc, :],
                                    C["PS_S"][:, u * TW:(u + 1) * TW],
                                    AF.Silu, bias=0.0, scale=1.0))
                            else:
                                s("dve", lambda v, u=u, fc=fc: v.tensor_tensor(
                                    a_f[:, fc, :], a_f[:, fc, :],
                                    C["PS_S"][:, u * TW:(u + 1) * TW],
                                    ALU.mult))
                if not shared:
                    for nn in range(ncpt):
                        s("pe", lambda t, nn=nn, e=e: t.transpose(
                            C["PS_S"][0:1, nn * 128:(nn + 1) * 128],
                            cw[:, nn, e:e + 1], C["ident_f"][:, :]))
                    s("dve", lambda v: v.tensor_copy(crow[0:1, 0:TW],
                                                     C["PS_S"][0:1, 0:TW]))
                    s("pe", lambda t: t.matmul(
                        C["P_O"][:, 0:TW], C["ones_row_f"][:, :],
                        crow[0:1, 0:TW], start=True, stop=True))
                    for fc in range(FC):
                        s("dve", lambda v, fc=fc: v.tensor_tensor(
                            a_bf[:, fc, :], a_f[:, fc, :], C["P_O"][:, 0:TW],
                            ALU.mult))
                else:
                    for fc in range(FC):
                        s("dve", lambda v, fc=fc: v.tensor_copy(
                            a_bf[:, fc, :], a_f[:, fc, :]))
                s("pool", lambda g, dap=dap: g.dma_start(
                    out=wflat[:, 0:FC * D].rearrange("p (c f) -> p c f", f=D),
                    in_=dap.rearrange("(c p) f -> p c f", p=128)), dma=True)
                for m in range(DC):
                    for fc in range(FC):
                        s("pe", lambda t, m=m, fc=fc: t.matmul(
                            C["P_AUX"][:, 0:TW],
                            wflat[:, fc * D + m * 128:fc * D + (m + 1) * 128],
                            a_bf[:, fc, :], start=(fc == 0),
                            stop=(fc == FC - 1)))
                    s("dve", lambda v, m=m: v.tensor_tensor(
                        y_sb[:, m, :], y_sb[:, m, :], C["P_AUX"][:, 0:TW],
                        ALU.add))
            if debug_stop == "y0":
                emit_store_tm(P, C, [y_sb[:, c, :] for c in range(DC)],
                              dbg, 0, TW, "y0")
                return
            for m in range(DC):
                s("dve", lambda v, m=m, st=st: v.tensor_tensor(
                    y_sb[:, m, :], y_sb[:, m, :],
                    A[:, m, st * TW:(st + 1) * TW], ALU.add))

            if to_stream:
                def wr2(c, tmp, rstd_ps, st=st):
                    s("dve", lambda v, c=c, tmp=tmp, rstd_ps=rstd_ps:
                      v.tensor_tensor(A[:, c, st * TW:(st + 1) * TW], tmp,
                                      rstd_ps, ALU.mult))
                emit_ln(P, C, [y_sb[:, c, :] for c in range(DC)], wr2, TW)
            else:
                def wr2(c, tmp, rstd_ps):
                    s("dve", lambda v, c=c, tmp=tmp, rstd_ps=rstd_ps:
                      v.tensor_tensor(y_sb[:, c, :], tmp, rstd_ps, ALU.mult))
                emit_ln(P, C, [y_sb[:, c, :] for c in range(DC)], wr2, TW)
                emit_store_tm(P, C, [y_sb[:, c, :] for c in range(DC)], xout,
                              st * TW, TW, f"o{tag}{st}")


def emit_router_l0(P, C, cw, nn, sm, m1, m2, eq):
    s = P.s
    s("dve", lambda v: v.tensor_reduce(m1[:, :], C["P_O"][:, 0:E0], AX.X,
                                       ALU.max))
    s("dve", lambda v: v.tensor_scalar(m2[:, :], m1[:, :], -1.0, None,
                                       op0=ALU.mult))
    s("act", lambda a: a.activation(sm[:, :], C["P_O"][:, 0:E0], AF.Exp,
                                    bias=m2[:, :], scale=1.0))
    s("dve", lambda v: v.tensor_reduce(m1[:, :], sm[:, :], AX.X, ALU.max))
    s("dve", lambda v: v.tensor_scalar(eq[:, :], sm[:, :], m1[:, :],
                                       float(NEG), op0=ALU.is_ge,
                                       op1=ALU.mult))
    s("dve", lambda v: v.tensor_tensor(eq[:, :], sm[:, :], eq[:, :], ALU.add))
    s("dve", lambda v: v.tensor_reduce(m2[:, :], eq[:, :], AX.X, ALU.max))
    s("dve", lambda v: v.tensor_scalar(eq[:, :], sm[:, :], m2[:, :], None,
                                       op0=ALU.is_ge))
    s("dve", lambda v: v.tensor_tensor(sm[:, :], sm[:, :], eq[:, :], ALU.mult))
    s("dve", lambda v: v.tensor_tensor(m1[:, :], m1[:, :], m2[:, :], ALU.add))
    s("dve", lambda v: v.reciprocal(m1[:, :], m1[:, :]))
    s("dve", lambda v, nn=nn: v.tensor_scalar(cw[:, nn, :], sm[:, :],
                                              m1[:, :], None, op0=ALU.mult))


def emit_router_l1(P, C, cw, nn, s4c, sg, gsb, bcbc, m1, m2, eq):
    s = P.s
    NGRP, EPG, TOPK, RSCALE = 4, 4, 4, 2.5
    s("act", lambda a: a.activation(sg[:, :], C["P_O"][:, 0:E1], AF.Sigmoid,
                                    bias=0.0, scale=1.0))
    s("dve", lambda v: v.tensor_tensor(s4c[:, :], sg[:, :], bcbc[:, :],
                                       ALU.add))
    for g4 in range(NGRP):
        seg = s4c[:, g4 * EPG:(g4 + 1) * EPG]
        s("dve", lambda v, seg=seg: v.tensor_reduce(m1[:, :], seg, AX.X,
                                                    ALU.max))
        s("dve", lambda v, seg=seg: v.tensor_scalar(
            eq[:, 0:EPG], seg, m1[:, :], float(NEG), op0=ALU.is_ge,
            op1=ALU.mult))
        s("dve", lambda v, seg=seg: v.tensor_tensor(eq[:, 0:EPG], seg,
                                                    eq[:, 0:EPG], ALU.add))
        s("dve", lambda v: v.tensor_reduce(m2[:, :], eq[:, 0:EPG], AX.X,
                                           ALU.max))
        s("dve", lambda v, g4=g4: v.tensor_tensor(gsb[:, g4:g4 + 1], m1[:, :],
                                                  m2[:, :], ALU.add))
    s("dve", lambda v: v.tensor_reduce(m1[:, :], gsb[:, :], AX.X, ALU.max))
    s("dve", lambda v: v.tensor_scalar(eq[:, 0:NGRP], gsb[:, :], m1[:, :],
                                       float(NEG), op0=ALU.is_ge,
                                       op1=ALU.mult))
    s("dve", lambda v: v.tensor_tensor(eq[:, 0:NGRP], gsb[:, :], eq[:, 0:NGRP],
                                       ALU.add))
    s("dve", lambda v: v.tensor_reduce(m2[:, :], eq[:, 0:NGRP], AX.X, ALU.max))
    s("dve", lambda v: v.tensor_scalar(gsb[:, :], gsb[:, :], m2[:, :], None,
                                       op0=ALU.is_ge))
    for rep in range(2):
        if rep == 1:
            s("dve", lambda v: v.tensor_tensor(s4c[:, :], sg[:, :], bcbc[:, :],
                                               ALU.add))
        for g4 in range(NGRP):
            s("dve", lambda v, g4=g4: v.tensor_scalar(
                m1[:, :], gsb[:, g4:g4 + 1], float(-NEG), float(NEG),
                op0=ALU.mult, op1=ALU.add))
            s("dve", lambda v, g4=g4: v.tensor_scalar(
                s4c[:, g4 * EPG:(g4 + 1) * EPG],
                s4c[:, g4 * EPG:(g4 + 1) * EPG], m1[:, :], None, op0=ALU.add))
        if rep == 0:
            for it in range(TOPK):
                s("dve", lambda v: v.tensor_reduce(m2[:, :], s4c[:, :], AX.X,
                                                   ALU.max))
                if it < TOPK - 1:
                    s("dve", lambda v: v.tensor_scalar(
                        eq[:, :], s4c[:, :], m2[:, :], float(NEG),
                        op0=ALU.is_ge, op1=ALU.mult))
                    s("dve", lambda v: v.tensor_tensor(s4c[:, :], s4c[:, :],
                                                       eq[:, :], ALU.add))
    s("dve", lambda v: v.tensor_scalar(eq[:, :], s4c[:, :], m2[:, :], None,
                                       op0=ALU.is_ge))
    s("dve", lambda v: v.tensor_tensor(eq[:, :], sg[:, :], eq[:, :], ALU.mult))
    s("dve", lambda v: v.tensor_reduce(m1[:, :], eq[:, :], AX.X, ALU.add))
    s("dve", lambda v: v.tensor_scalar(m1[:, :], m1[:, :], 1e-20, None,
                                       op0=ALU.add))
    s("dve", lambda v: v.reciprocal(m1[:, :], m1[:, :]))
    s("dve", lambda v: v.tensor_scalar(m1[:, :], m1[:, :], RSCALE, None,
                                       op0=ALU.mult))
    s("dve", lambda v, nn=nn: v.tensor_scalar(cw[:, nn, :], eq[:, :],
                                              m1[:, :], None, op0=ALU.mult))



def _to_pc(b):
    return np.ascontiguousarray(np.asarray(b, np.float32).reshape(DC, 128).T)


_prog_cache = {}


def _get_prog(kind, layer):
    key = (kind, layer)
    if key not in _prog_cache:
        _prog_cache[key] = build_A(layer) if kind == "A" else build_B(layer)
    return _prog_cache[key]


def _get_prog_dbg(kind, layer, stop):
    key = (kind, layer, stop)
    if key not in _prog_cache:
        _prog_cache[key] = build_B(layer, debug_stop=stop)
    return _prog_cache[key]


def _run(nc, in_maps):
    res = run_bass_kernel_spmd(nc, in_maps, core_ids=list(range(8)))
    return res.results


def _window_feeds(ko, vo):
    kfeed, vfeed = [], []
    for b in range(2):
        kb = np.concatenate(ko[4 * b:4 * b + 4], axis=1)
        vb = np.concatenate(vo[4 * b:4 * b + 4], axis=0)
        for q4 in range(4):
            wlo = q4 * TC - WIN
            kf = np.zeros((D, KSPAN), dtype=kb.dtype)
            vf = np.zeros((KSPAN, D), dtype=vb.dtype)
            lo = max(0, wlo)
            kf[:, lo - wlo:KSPAN] = kb[:, lo:wlo + KSPAN]
            vf[lo - wlo:KSPAN, :] = vb[lo:wlo + KSPAN, :]
            kfeed.append(np.ascontiguousarray(kf))
            vfeed.append(np.ascontiguousarray(vf))
    return kfeed, vfeed


USE_FUSED = True


def kernel(**inputs):
    if USE_FUSED:
        return kernel_fused(**inputs)
    return kernel_multi(**inputs)


def kernel_multi(**inputs):
    f32 = np.float32
    x = np.asarray(inputs["L0_attn_x"], f32)
    streams = [np.ascontiguousarray(x[c // 4, (c % 4) * TC:(c % 4 + 1) * TC, :])
               for c in range(8)]

    ncA0 = _get_prog("A", 0)
    wq = np.asarray(inputs["L0_attn_Wq"], f32)
    wk = np.asarray(inputs["L0_attn_Wk"], f32)
    wv = np.asarray(inputs["L0_attn_Wv"], f32)
    rA0 = _run(ncA0, [{"xin": streams[c], "Wq": wq, "Wk": wk, "Wv": wv}
                      for c in range(8)])
    qo = [np.asarray(r["qo"]) for r in rA0]
    ko = [np.asarray(r["ko"]) for r in rA0]
    vo = [np.asarray(r["vo"]) for r in rA0]
    kfeed, vfeed = _window_feeds(ko, vo)

    ncB0 = _get_prog("B", 0)
    rB0 = _run(ncB0, [{
        "xin": streams[c], "qin": qo[c], "kin": kfeed[c], "vin": vfeed[c],
        "coreq4": np.full((1, 1), float(c % 4), f32),
        "Wo": np.asarray(inputs["L0_attn_Wo"], f32),
        "gate_w": np.asarray(inputs["L0_ffn_gate_w"], f32),
        "w_gate": np.asarray(inputs["L0_ffn_w_gate"], f32),
        "w_up": np.asarray(inputs["L0_ffn_w_up"], f32),
        "w_down": np.asarray(inputs["L0_ffn_w_down"], f32),
    } for c in range(8)])
    streams = [np.asarray(r["xout"]) for r in rB0]

    ncA1 = _get_prog("A", 1)
    rA1 = _run(ncA1, [{
        "xin": streams[c],
        "cq": np.asarray(inputs["L1_attn_wq_codes"], f32),
        "sq": np.asarray(inputs["L1_attn_wq_scales"], f32),
        "bq": _to_pc(inputs["L1_attn_bq"]),
        "ck": np.asarray(inputs["L1_attn_wk_codes"], f32),
        "sk": np.asarray(inputs["L1_attn_wk_scales"], f32),
        "bk": _to_pc(inputs["L1_attn_bk"]),
        "cv": np.asarray(inputs["L1_attn_wv_codes"], f32),
        "sv": np.asarray(inputs["L1_attn_wv_scales"], f32),
        "bv": _to_pc(inputs["L1_attn_bv"]),
    } for c in range(8)])
    qo = [np.asarray(r["qo"]) for r in rA1]
    ko = [np.asarray(r["ko"]) for r in rA1]
    vo = [np.asarray(r["vo"]) for r in rA1]
    kfull = [np.ascontiguousarray(np.concatenate(ko[4 * b:4 * b + 4], axis=1))
             for b in range(2)]
    vfull = [np.ascontiguousarray(np.concatenate(vo[4 * b:4 * b + 4], axis=0))
             for b in range(2)]

    ncB1 = _get_prog("B", 1)
    gwT = np.ascontiguousarray(
        np.asarray(inputs["L1_ffn_gate_w"], f32).transpose(0, 2, 1))
    uwT = np.ascontiguousarray(
        np.asarray(inputs["L1_ffn_up_w"], f32).transpose(0, 2, 1))
    dwT = np.ascontiguousarray(
        np.asarray(inputs["L1_ffn_down_w"], f32).transpose(0, 2, 1))
    shgT = np.ascontiguousarray(np.asarray(inputs["L1_ffn_sh_gate_w"], f32).T)
    shuT = np.ascontiguousarray(np.asarray(inputs["L1_ffn_sh_up_w"], f32).T)
    shdT = np.ascontiguousarray(np.asarray(inputs["L1_ffn_sh_down_w"], f32).T)
    gwr = np.ascontiguousarray(np.asarray(inputs["L1_ffn_gate_weight"], f32).T)
    bcorr = np.ascontiguousarray(
        np.asarray(inputs["L1_ffn_e_score_correction_bias"], f32)[None, :])
    rB1 = _run(ncB1, [{
        "xin": streams[c], "qin": qo[c], "kin": kfull[c // 4],
        "vin": vfull[c // 4],
        "oc": np.asarray(inputs["L1_attn_wo_codes"], f32),
        "osc": np.asarray(inputs["L1_attn_wo_scales"], f32),
        "bo": _to_pc(inputs["L1_attn_bo"]),
        "gwr": gwr, "bcorr": bcorr, "gwT": gwT, "uwT": uwT, "dwT": dwT,
        "shgT": shgT, "shuT": shuT, "shdT": shdT,
    } for c in range(8)])
    outs = [np.asarray(r["xout"]) for r in rB1]
    out = np.stack([np.concatenate(outs[0:4], 0), np.concatenate(outs[4:8], 0)])
    return out.astype(np.float32)



def build_fused(dbgstop=None):
    P = Prog()
    nc = P.nc
    s = P.s
    xin = nc.dram_tensor("xin", [TC, D], F32, kind="ExternalInput").ap()
    W0 = {}
    for n_ in ("Wq", "Wk", "Wv", "Wo"):
        W0[n_] = nc.dram_tensor(n_, [D, D], F32, kind="ExternalInput").ap()
    W0["gate_w"] = nc.dram_tensor("gate_w", [D, E0], F32,
                                  kind="ExternalInput").ap()
    W0["w_gate"] = nc.dram_tensor("w_gate", [E0, D, F0], F32,
                                  kind="ExternalInput").ap()
    W0["w_up"] = nc.dram_tensor("w_up", [E0, D, F0], F32,
                                kind="ExternalInput").ap()
    W0["w_down"] = nc.dram_tensor("w_down", [E0, F0, D], F32,
                                  kind="ExternalInput").ap()
    W1 = {}
    for n_ in ("q", "k", "v", "o"):
        W1["c" + n_] = nc.dram_tensor("c" + n_, [D, D], F32,
                                      kind="ExternalInput").ap()
        W1["s" + n_] = nc.dram_tensor("s" + n_, [64, D], F32,
                                      kind="ExternalInput").ap()
        W1["b" + n_] = nc.dram_tensor("b" + n_, [128, DC], F32,
                                      kind="ExternalInput").ap()
    W1["gwr"] = nc.dram_tensor("gwr", [D, E1], F32, kind="ExternalInput").ap()
    W1["bcorr"] = nc.dram_tensor("bcorr", [1, E1], F32,
                                 kind="ExternalInput").ap()
    W1["gwT"] = nc.dram_tensor("gwT", [E1, D, F1], F32,
                               kind="ExternalInput").ap()
    W1["uwT"] = nc.dram_tensor("uwT", [E1, D, F1], F32,
                               kind="ExternalInput").ap()
    W1["dwT"] = nc.dram_tensor("dwT", [E1, F1, D], F32,
                               kind="ExternalInput").ap()
    W1["shgT"] = nc.dram_tensor("shgT", [D, FSH], F32,
                                kind="ExternalInput").ap()
    W1["shuT"] = nc.dram_tensor("shuT", [D, FSH], F32,
                                kind="ExternalInput").ap()
    W1["shdT"] = nc.dram_tensor("shdT", [FSH, D], F32,
                                kind="ExternalInput").ap()
    xout = nc.dram_tensor("xout", [TC, D], F32, kind="ExternalOutput").ap()

    a2a = {}
    for n_ in ("aq_i", "aq_o", "ak_i", "ak_o", "av_i", "av_o", "ac_i",
               "ac_o"):
        a2a[n_] = nc.dram_tensor(n_, [2048, TC], BF16)

    C = {}
    C["PS_S"] = P.ps("PS_S", [128, 2560], F32)
    C["P_O"] = P.ps("P_O", [128, 512], F32)
    C["P_AUX"] = P.ps("P_AUX", [128, 512], F32)
    C["PT"] = P.ps("PT", [128, 1024], BF16)
    C.update(setup_consts(P, need_f32_ident=True))
    sels = setup_sels(P)

    C["ln_zc"] = P.sb("ln_zc", [128, 1024], BF16)
    C["ln_mu"] = P.sb("ln_mu", [1, 512], F32)
    C["ln_var"] = P.sb("ln_var", [1, 512], F32)
    C["ln_rstd"] = P.sb("ln_rstd", [1, 512], F32)
    C["ln_tmp"] = P.sb("ln_tmp", [128, 512], F32)

    A = P.sb("Abuf", [128, DC, TC], F32)
    emit_load_stream_fm(P, C, xin, A, F32)

    _emit_fused_layer(P, C, A, sels, 0, W0, a2a, None)
    _emit_fused_layer(P, C, A, sels, 1, W1, a2a, xout)
    return P.finish()


def _emit_fused_layer(P, C, A, sels, layer, W, a2a, xout):
    nc = P.nc
    s = P.s
    tag = "L%d" % layer
    from concourse.aot_env import allow_ici_all_to_all

    with ExitStack() as es:
        xbf = es.enter_context(nc.sbuf_tensor("xbf" + tag,
                                              [128, DC, TC], BF16))
        wb = es.enter_context(nc.sbuf_tensor("wbq" + tag,
                                             [128, DC * D], BF16))
        pbuf = es.enter_context(nc.sbuf_tensor("pbufq" + tag,
                                               [128, 512], BF16))
        if layer == 1:
            sclb = es.enter_context(nc.sbuf_tensor("sclbq" + tag,
                                                   [64, D], BF16))
            biasb = es.enter_context(nc.sbuf_tensor("biasbq" + tag,
                                                    [128, DC], F32))
        s("dve", lambda v: v.tensor_copy(xbf[:, :, :], A[:, :, :]))
        if layer == 0:
            plist = [(W["Wq"], None, None, a2a["aq_i"]),
                     (W["Wk"], None, None, a2a["ak_i"]),
                     (W["Wv"], None, None, a2a["av_i"])]
        else:
            plist = [(W["cq"], W["sq"], W["bq"], a2a["aq_i"]),
                     (W["ck"], W["sk"], W["bk"], a2a["ak_i"]),
                     (W["cv"], W["sv"], W["bv"], a2a["av_i"])]
        for wap, sap, bap, a2a_in in plist:
            s("pool", lambda g, wap=wap: g.dma_start(
                out=wb[:, :].rearrange("p (c f) -> p c f", f=D),
                in_=wap.rearrange("(c p) f -> p c f", p=128)), dma=True)
            if layer == 1:
                s("pool", lambda g, sap=sap: g.dma_start(
                    out=sclb[:, :], in_=sap), dma=True)
                s("sp", lambda sy, bap=bap: sy.dma_start(
                    out=biasb[:, :], in_=bap), dma=True)
                emit_dequant(P, C, sels, wb, sclb,
                             [(dk * D, dk) for dk in range(DC)])
            for m in range(DC):
                for st in range(2):
                    for dk in range(DC):
                        s("pe", lambda t, m=m, st=st, dk=dk: t.matmul(
                            C["PS_S"][:, 0:512],
                            wb[:, dk * D + m * 128:dk * D + (m + 1) * 128],
                            xbf[:, dk, st * 512:(st + 1) * 512],
                            start=(dk == 0), stop=(dk == DC - 1)))
                    if layer == 1:
                        s("act", lambda a, m=m: a.activation(
                            pbuf[:, :], C["PS_S"][:, 0:512], AF.Identity,
                            bias=biasb[:, m:m + 1], scale=1.0))
                    else:
                        s("act", lambda a: a.activation(
                            pbuf[:, :], C["PS_S"][:, 0:512], AF.Identity,
                            bias=0.0, scale=1.0))
                    row = (m % 8) * 256 + (m // 8) * 128
                    s("sp", lambda sy, row=row, st=st, a2a_in=a2a_in:
                      sy.dma_start(
                        out=a2a_in.ap()[row:row + 128,
                                        st * 512:(st + 1) * 512],
                        in_=pbuf[:, :]), dma=True)

    for i_t, o_t in ((a2a["aq_i"], a2a["aq_o"]), (a2a["ak_i"], a2a["ak_o"]),
                     (a2a["av_i"], a2a["av_o"])):
        def cc_fn(g, i_t=i_t, o_t=o_t):
            with allow_ici_all_to_all():
                return g.collective_compute(
                    "AllToAll", ALU.bypass,
                    replica_groups=[[0, 1, 2, 3, 4, 5, 6, 7]],
                    ins=[i_t.ap()], outs=[o_t.ap()])
        s("pool", cc_fn)

    ctx_es0 = ExitStack()
    ctxall = ctx_es0.enter_context(
        nc.sbuf_tensor("ctxall" + tag, [128, 4, SEQ], BF16))
    with ExitStack() as es:
        lacc = es.enter_context(nc.sbuf_tensor("lacc" + tag, [128, 2], F32))
        msml = es.enter_context(nc.sbuf_tensor("msml" + tag, [128, 8], F32))
        mb = es.enter_context(nc.sbuf_tensor("mb" + tag, [128, 1], F32))
        rrow = es.enter_context(nc.sbuf_tensor("rrow" + tag, [1, 256], F32))
        rb_sb = es.enter_context(nc.sbuf_tensor("rb_sb" + tag,
                                                [128, 256], F32))
        t128a = es.enter_context(nc.sbuf_tensor("t128a" + tag,
                                                [128, 128], F32))
        t128b = es.enter_context(nc.sbuf_tensor("t128b" + tag,
                                                [128, 128], F32))
        maskd = maske = None
        if layer == 0:
            maskd = es.enter_context(nc.sbuf_tensor("maskd" + tag,
                                                    [128, 4 * 512], BF16))
            maske = es.enter_context(nc.sbuf_tensor("maske" + tag,
                                                    [128, 4 * 512], BF16))
            for ph in range(4):
                s("pool", lambda g, ph=ph: g.memset(
                    maskd[:, ph * 512:(ph + 1) * 512], 0.0))
                s("pool", lambda g, ph=ph: g.affine_select(
                    out=maskd[:, ph * 512:(ph + 1) * 512],
                    in_=maskd[:, ph * 512:(ph + 1) * 512],
                    compare_op=ALU.is_ge, fill=NEG, base=128 * ph,
                    pattern=[[-1, 512]], channel_multiplier=1))
                s("pool", lambda g, ph=ph: g.memset(
                    maske[:, ph * 512:(ph + 1) * 512], 0.0))
                s("pool", lambda g, ph=ph: g.affine_select(
                    out=maske[:, ph * 512:(ph + 1) * 512],
                    in_=maske[:, ph * 512:(ph + 1) * 512],
                    compare_op=ALU.is_ge, fill=NEG, base=-(128 * ph + 1),
                    pattern=[[1, 512]], channel_multiplier=-1))
        q_h = es.enter_context(nc.sbuf_tensor("q_h" + tag, [128, SEQ], BF16))
        k_h = es.enter_context(nc.sbuf_tensor("k_h" + tag, [128, SEQ], BF16))
        v_f = es.enter_context(nc.sbuf_tensor("v_f" + tag, [128, SEQ], BF16))
        v_tm = es.enter_context(nc.sbuf_tensor("v_tm" + tag,
                                               [128, 32, 128], BF16))
        p_sb = es.enter_context(nc.sbuf_tensor("p_sb" + tag,
                                               [128, 2560], BF16))
        pt_sb = es.enter_context(nc.sbuf_tensor("pt_sb" + tag,
                                                [128, 2560], BF16))
        for b in range(2):
            for hh in range(2):
                _emit_attn_instance(P, C, layer, b, hh, a2a, q_h, k_h, v_f,
                                    v_tm, p_sb, pt_sb, lacc, msml, mb, rrow,
                                    rb_sb, t128a, t128b, maskd, maske, ctxall)
    for j in range(8):
        for hh in range(2):
            s("sp", lambda sy, j=j, hh=hh: sy.dma_start(
                out=a2a["ac_i"].ap()[j * 256 + hh * 128:
                                     j * 256 + hh * 128 + 128, :],
                in_=ctxall[:, (j // 4) * 2 + hh,
                           (j % 4) * TC:(j % 4 + 1) * TC]), dma=True)
    ctx_es0.close()

    def cc2_fn(g):
        with allow_ici_all_to_all():
            return g.collective_compute(
                "AllToAll", ALU.bypass,
                replica_groups=[[0, 1, 2, 3, 4, 5, 6, 7]],
                ins=[a2a["ac_i"].ap()], outs=[a2a["ac_o"].ap()])
    s("pool", cc2_fn)

    ctx_es = ExitStack()
    ctx_own = ctx_es.enter_context(
        nc.sbuf_tensor("ctxown" + tag, [128, DC, TC], BF16))
    for m in range(DC):
        row = 256 * m if m < 8 else 256 * (m - 8) + 128
        s("sp", lambda sy, m=m, row=row: sy.dma_start(
            out=ctx_own[:, m, :], in_=a2a["ac_o"].ap()[row:row + 128, :]),
            dma=True)
    with ExitStack() as es:
        wbh = es.enter_context(nc.sbuf_tensor("wbh" + tag,
                                              [128, 8 * D], BF16))
        zst = es.enter_context(nc.sbuf_tensor("zst" + tag,
                                              [128, DC, 512], F32))
        if layer == 1:
            sclb2 = es.enter_context(nc.sbuf_tensor("sclb2" + tag,
                                                    [64, D], BF16))
            biasb2 = es.enter_context(nc.sbuf_tensor("biasb2" + tag,
                                                     [128, DC], F32))
            s("pool", lambda g: g.dma_start(out=sclb2[:, :], in_=W["so"]),
              dma=True)
            s("sp", lambda sy: sy.dma_start(out=biasb2[:, :], in_=W["bo"]),
              dma=True)
        wo_ap = W["Wo"] if layer == 0 else W["co"]
        for st in range(2):
            for half in range(2):
                s("pool", lambda g, half=half: g.dma_start(
                    out=wbh[:, :].rearrange("p (c f) -> p c f", f=D),
                    in_=wo_ap.rearrange("(c p) f -> p c f", p=128)[
                        :, half * 8:(half + 1) * 8, :]), dma=True)
                if layer == 1:
                    emit_dequant(P, C, sels, wbh, sclb2,
                                 [(dk8 * D, half * 8 + dk8)
                                  for dk8 in range(8)])
                for m in range(DC):
                    for dk in range(8):
                        s("pe", lambda t, st=st, half=half, m=m, dk=dk:
                          t.matmul(
                            C["PS_S"][:, 0:512],
                            wbh[:, dk * D + m * 128:dk * D + (m + 1) * 128],
                            ctx_own[:, half * 8 + dk,
                                    st * 512:(st + 1) * 512],
                            start=(dk == 0), stop=(dk == 7)))
                    if half == 0:
                        s("dve", lambda v, st=st, m=m: v.tensor_tensor(
                            zst[:, m, :], C["PS_S"][:, 0:512],
                            A[:, m, st * 512:(st + 1) * 512], ALU.add))
                    else:
                        s("dve", lambda v, m=m: v.tensor_tensor(
                            zst[:, m, :], C["PS_S"][:, 0:512],
                            zst[:, m, :], ALU.add))
            if layer == 1:
                for m in range(DC):
                    s("act", lambda a, m=m: a.activation(
                        zst[:, m, :], zst[:, m, :], AF.Identity,
                        bias=biasb2[:, m:m + 1], scale=1.0))

            def wr(c, tmp, rstd_ps, st=st):
                s("dve", lambda v, c=c, tmp=tmp, rstd_ps=rstd_ps:
                  v.tensor_tensor(A[:, c, st * 512:(st + 1) * 512], tmp,
                                  rstd_ps, ALU.mult))
            emit_ln(P, C, [zst[:, c, :] for c in range(DC)], wr, 512)
    ctx_es.close()

    if layer == 0:
        emit_ffn(P, C, A, None, layer=0, router=(W["gate_w"],),
                 weights=(W["w_gate"], W["w_up"], W["w_down"]),
                 to_stream=True, tag=tag)
    else:
        emit_ffn(P, C, A, xout, layer=1, router=(W["gwr"], W["bcorr"]),
                 weights=(W["gwT"], W["uwT"], W["dwT"], W["shgT"], W["shuT"],
                          W["shdT"]), tag=tag)


def _emit_attn_instance(P, C, layer, b, hh, a2a, q_h, k_h, v_f, v_tm, p_sb,
                        pt_sb, lacc, msml, mb, rrow, rb_sb, t128a, t128b,
                        maskd, maske, ctxall):
    s = P.s
    inst = b * 2 + hh
    for r in range(4):
        row = 256 * (4 * b + r) + 128 * hh
        s("sp", lambda sy, r=r, row=row: sy.dma_start(
            out=q_h[:, r * TC:(r + 1) * TC],
            in_=a2a["aq_o"].ap()[row:row + 128, :]), dma=True)
        s("sp", lambda sy, r=r, row=row: sy.dma_start(
            out=k_h[:, r * TC:(r + 1) * TC],
            in_=a2a["ak_o"].ap()[row:row + 128, :]), dma=True)
        s("sp", lambda sy, r=r, row=row: sy.dma_start(
            out=v_f[:, r * TC:(r + 1) * TC],
            in_=a2a["av_o"].ap()[row:row + 128, :]), dma=True)
    for n in range(32):
        s("pe", lambda t, n=n: t.transpose(
            C["PT"][:, (n % 8) * 128:(n % 8) * 128 + 128],
            v_f[:, n * 128:(n + 1) * 128], C["ident_bf"][:, :]))
        if n % 8 == 7:
            s("dve", lambda v, n=n: v.tensor_copy(
                v_tm[:, n - 7:n + 1, :].rearrange("p a b -> p (a b)"),
                C["PT"][:, :]))
    for i in range(32):
        qs = 128 * i
        if layer == 0:
            lo = max(0, qs - 2047) // 512
            rounds = [list(range(lo, qs // 512 + 1))]
        else:
            rounds = [[0, 1, 2, 3], [4, 5, 6, 7]]
        nr = len(rounds)
        for r, chunks in enumerate(rounds):
            nch = len(chunks)
            for j, ch in enumerate(chunks):
                s("pe", lambda t, i=i, j=j, ch=ch: t.matmul(
                    C["PS_S"][:, j * 512:(j + 1) * 512],
                    q_h[:, i * 128:(i + 1) * 128],
                    k_h[:, ch * 512:(ch + 1) * 512],
                    start=True, stop=True))
            if layer == 0:
                ph = i % 4
                if qs >= 2048:
                    s("dve", lambda v, ph=ph: v.tensor_tensor(
                        C["PS_S"][:, 0:512], C["PS_S"][:, 0:512],
                        maske[:, ph * 512:(ph + 1) * 512], ALU.add))
                s("dve", lambda v, ph=ph, nch=nch: v.tensor_tensor(
                    C["PS_S"][:, (nch - 1) * 512:nch * 512],
                    C["PS_S"][:, (nch - 1) * 512:nch * 512],
                    maskd[:, ph * 512:(ph + 1) * 512], ALU.add))
            s("dve", lambda v, nch=nch, r=r: v.tensor_reduce(
                msml[:, r:r + 1], C["PS_S"][:, 0:nch * 512], AX.X, ALU.max))
            s("dve", lambda v, r=r: v.tensor_scalar(
                mb[:, :], msml[:, r:r + 1], -TAUI, None, op0=ALU.mult))
            s("act", lambda a, nch=nch: a.activation(
                p_sb[:, 0:nch * 512], C["PS_S"][:, 0:nch * 512], AF.Exp,
                bias=mb[:, :], scale=TAUI, accum_out=lacc[:, 0:1]))
            s("dve", lambda v, r=r: v.tensor_copy(
                msml[:, 2 + r:3 + r], lacc[:, 0:1]))
            nk128 = nch * 4
            for kc in range(nk128):
                s("pe", lambda t, kc=kc: t.transpose(
                    C["PT"][:, (kc % 8) * 128:(kc % 8) * 128 + 128],
                    p_sb[:, kc * 128:(kc + 1) * 128], C["ident_bf"][:, :]))
                if kc % 8 == 7 or kc == nk128 - 1:
                    lo8 = (kc // 8) * 8
                    s("dve", lambda v, kc=kc, lo8=lo8: v.tensor_copy(
                        pt_sb[:, lo8 * 128:(kc + 1) * 128],
                        C["PT"][:, 0:(kc - lo8 + 1) * 128]))
            ch0 = chunks[0] * 4
            for kc in range(nk128):
                s("pe", lambda t, kc=kc, ch0=ch0, r=r, nk128=nk128: t.matmul(
                    C["P_O"][:, r * 128:(r + 1) * 128],
                    v_tm[:, ch0 + kc, :],
                    pt_sb[:, kc * 128:(kc + 1) * 128],
                    start=(kc == 0), stop=(kc == nk128 - 1)))
        if nr == 1:
            s("dve", lambda v: v.reciprocal(mb[:, :], msml[:, 2:3]))
            s("pe", lambda t: t.transpose(
                C["PS_S"][0:1, 0:128], mb[:, :], C["ident_f"][:, :]))
            s("dve", lambda v: v.tensor_copy(rrow[0:1, 0:128],
                                             C["PS_S"][0:1, 0:128]))
            s("pe", lambda t: t.matmul(
                C["P_AUX"][:, 0:128], C["ones_row_f"][:, :],
                rrow[0:1, 0:128], start=True, stop=True))
            s("dve", lambda v: v.tensor_copy(rb_sb[:, 0:128],
                                             C["P_AUX"][:, 0:128]))
            s("dve", lambda v, inst=inst, i=i: v.tensor_tensor(
                ctxall[:, inst, i * 128:(i + 1) * 128],
                C["P_O"][:, 0:128], rb_sb[:, 0:128], ALU.mult))
        else:
            s("dve", lambda v: v.tensor_tensor(
                mb[:, :], msml[:, 0:1], msml[:, 1:2], ALU.max))
            s("dve", lambda v: v.tensor_scalar(
                mb[:, :], mb[:, :], -TAUI, None, op0=ALU.mult))
            for r in range(2):
                s("act", lambda a, r=r: a.activation(
                    msml[:, 4 + r:5 + r], msml[:, r:r + 1], AF.Exp,
                    bias=mb[:, :], scale=TAUI))
            s("dve", lambda v: v.tensor_tensor(
                msml[:, 6:7], msml[:, 4:5], msml[:, 2:3], ALU.mult))
            s("dve", lambda v: v.tensor_tensor(
                msml[:, 7:8], msml[:, 5:6], msml[:, 3:4], ALU.mult))
            s("dve", lambda v: v.tensor_tensor(
                msml[:, 6:7], msml[:, 6:7], msml[:, 7:8], ALU.add))
            s("dve", lambda v: v.reciprocal(msml[:, 6:7], msml[:, 6:7]))
            for r in range(2):
                s("dve", lambda v, r=r: v.tensor_tensor(
                    msml[:, 4 + r:5 + r], msml[:, 4 + r:5 + r],
                    msml[:, 6:7], ALU.mult))
            for r in range(2):
                s("pe", lambda t, r=r: t.transpose(
                    C["PS_S"][0:1, r * 128:(r + 1) * 128],
                    msml[:, 4 + r:5 + r], C["ident_f"][:, :]))
            s("dve", lambda v: v.tensor_copy(rrow[0:1, 0:256],
                                             C["PS_S"][0:1, 0:256]))
            for r in range(2):
                s("pe", lambda t, r=r: t.matmul(
                    C["P_AUX"][:, r * 128:(r + 1) * 128],
                    C["ones_row_f"][:, :], rrow[0:1, r * 128:(r + 1) * 128],
                    start=True, stop=True))
            s("dve", lambda v: v.tensor_copy(rb_sb[:, 0:256],
                                             C["P_AUX"][:, 0:256]))
            s("dve", lambda v: v.tensor_tensor(
                t128a[:, :], C["P_O"][:, 0:128], rb_sb[:, 0:128], ALU.mult))
            s("dve", lambda v: v.tensor_tensor(
                t128b[:, :], C["P_O"][:, 128:256], rb_sb[:, 128:256],
                ALU.mult))
            s("dve", lambda v, inst=inst, i=i: v.tensor_tensor(
                ctxall[:, inst, i * 128:(i + 1) * 128],
                t128a[:, :], t128b[:, :], ALU.add))


def _fused_in_map(inputs, c):
    f32 = np.float32
    x = np.asarray(inputs["L0_attn_x"], f32)
    m = {
        "xin": np.ascontiguousarray(
            x[c // 4, (c % 4) * TC:(c % 4 + 1) * TC, :]),
        "Wq": np.asarray(inputs["L0_attn_Wq"], f32),
        "Wk": np.asarray(inputs["L0_attn_Wk"], f32),
        "Wv": np.asarray(inputs["L0_attn_Wv"], f32),
        "Wo": np.asarray(inputs["L0_attn_Wo"], f32),
        "gate_w": np.asarray(inputs["L0_ffn_gate_w"], f32),
        "w_gate": np.asarray(inputs["L0_ffn_w_gate"], f32),
        "w_up": np.asarray(inputs["L0_ffn_w_up"], f32),
        "w_down": np.asarray(inputs["L0_ffn_w_down"], f32),
        "cq": np.asarray(inputs["L1_attn_wq_codes"], f32),
        "sq": np.asarray(inputs["L1_attn_wq_scales"], f32),
        "bq": _to_pc(inputs["L1_attn_bq"]),
        "ck": np.asarray(inputs["L1_attn_wk_codes"], f32),
        "sk": np.asarray(inputs["L1_attn_wk_scales"], f32),
        "bk": _to_pc(inputs["L1_attn_bk"]),
        "cv": np.asarray(inputs["L1_attn_wv_codes"], f32),
        "sv": np.asarray(inputs["L1_attn_wv_scales"], f32),
        "bv": _to_pc(inputs["L1_attn_bv"]),
        "co": np.asarray(inputs["L1_attn_wo_codes"], f32),
        "so": np.asarray(inputs["L1_attn_wo_scales"], f32),
        "bo": _to_pc(inputs["L1_attn_bo"]),
        "gwr": np.ascontiguousarray(
            np.asarray(inputs["L1_ffn_gate_weight"], f32).T),
        "bcorr": np.ascontiguousarray(
            np.asarray(inputs["L1_ffn_e_score_correction_bias"], f32)[None, :]),
        "gwT": np.ascontiguousarray(
            np.asarray(inputs["L1_ffn_gate_w"], f32).transpose(0, 2, 1)),
        "uwT": np.ascontiguousarray(
            np.asarray(inputs["L1_ffn_up_w"], f32).transpose(0, 2, 1)),
        "dwT": np.ascontiguousarray(
            np.asarray(inputs["L1_ffn_down_w"], f32).transpose(0, 2, 1)),
        "shgT": np.ascontiguousarray(
            np.asarray(inputs["L1_ffn_sh_gate_w"], f32).T),
        "shuT": np.ascontiguousarray(
            np.asarray(inputs["L1_ffn_sh_up_w"], f32).T),
        "shdT": np.ascontiguousarray(
            np.asarray(inputs["L1_ffn_sh_down_w"], f32).T),
    }
    return m


_fused_cache = {}


def kernel_fused(**inputs):
    if "nc" not in _fused_cache:
        _fused_cache["nc"] = build_fused()
    nc = _fused_cache["nc"]
    base = _fused_in_map(inputs, 0)
    in_maps = []
    for c in range(8):
        m = dict(base)
        m["xin"] = _fused_in_map(inputs, c)["xin"] if c else base["xin"]
        in_maps.append(m)
    res = run_bass_kernel_spmd(nc, in_maps, core_ids=list(range(8)))
    outs = [np.asarray(r["xout"]) for r in res.results]
    out = np.stack([np.concatenate(outs[0:4], 0), np.concatenate(outs[4:8], 0)])
    return out.astype(np.float32)
